# version 14
# baseline (speedup 1.0000x reference)
import math
import numpy as np
from contextlib import ExitStack

import concourse.bass as bass
import concourse.mybir as mybir
from concourse.bass_utils import run_bass_kernel_spmd

F32 = mybir.dt.float32
BF16 = mybir.dt.bfloat16
AF = mybir.ActivationFunctionType
OP = mybir.AluOpType
AX = mybir.AxisListType.X

S = 2048
D = 1024
DH = 4096
NT = S // 128
KD = D // 128
KH = DH // 128
M32 = 12582912.0
MASK_SHIFT = 100000.0

ALL_STAGES = ("p1", "p2", "p3", "p4")


class Edge:
    def __init__(self, sem, step=1):
        self.sem, self.step, self.n = sem, step, 0

    def attach(self, instr):
        instr.then_inc(self.sem, self.step)
        return instr


class Sched:

    def __init__(self):
        self.q = {e: [] for e in "tsvgy"}
        self.dirty = {e: set() for e in "tsvgy"}

    def emit(self, eng, fn, reads=(), writes=(), inc=None):
        if eng in "svg" and any(r in self.dirty[eng] for r in (*reads, *writes)):
            self.q[eng].append(lambda E: E.drain())
            self.dirty[eng] = set()
        if inc is not None:
            inc.n += inc.step
            self.q[eng].append(lambda E, f=fn, e=inc: e.attach(f(E)))
        else:
            self.q[eng].append(lambda E, f=fn: f(E))
        for w in writes:
            self.dirty[eng].add(w)

    def dma(self, eng, fn, edge):
        self.emit(eng, fn, inc=edge)
        self.wait(eng, edge)

    def wait(self, eng, edge, val=None):
        v = edge.n if val is None else val
        if v > 0:
            self.q[eng].append(lambda E, e=edge, vv=v: E.wait_ge(e.sem, vv))

    def barrier(self, ed):
        owner = {"vt": "v", "vs": "v", "vg": "v", "ts": "t", "tv": "t",
                 "tg": "t", "st": "s", "sv": "s", "sy": "s", "gt": "g",
                 "gv": "g", "dg": "g", "dy": "y", "do": "y", "dog": "g"}
        for eng in "tsvgy":
            for nm, e in ed.items():
                if owner.get(nm) != eng and e.n > 0:
                    self.wait(eng, e)

    def run(self, block):
        for attr, key in (("tensor", "t"), ("scalar", "s"), ("vector", "v"),
                          ("gpsimd", "g"), ("sync", "y")):
            def make(ops):
                def body(E):
                    for f in ops:
                        f(E)
                return body
            getattr(block, attr)(make(self.q[key]))


class Arena:
    def __init__(self, tensor, cols, name):
        self.t, self.cols, self.name = tensor, cols, name
        self.off, self.marks, self.hw = 0, [], 0

    def alloc(self, n):
        a = self.off
        self.off += n
        self.hw = max(self.hw, self.off)
        assert self.off <= self.cols, f"{self.name} overflow {self.off}>{self.cols}"
        return self.t[:, a:a + n]

    def push(self):
        self.marks.append(self.off)

    def pop(self):
        self.off = self.marks.pop()


class Ctx:
    pass


def build_program(stages=ALL_STAGES, dbg=()):
    nc = bass.Bass("TRN2", target_bir_lowering=False)
    c = Ctx()
    c.nc = nc
    c.stages = stages
    c.dbg = set(dbg)
    c.dt = {}

    def din(name, shape, dtype=F32):
        c.dt[name] = nc.dram_tensor(name, list(shape), dtype, kind="ExternalInput").ap()

    def dscr(name, shape, dtype=F32):
        c.dt[name] = nc.dram_tensor(name, list(shape), dtype, kind="Internal").ap()

    din("x", (S, D))
    din("wqkv_qk", (D, 2048))
    din("wout_r", (D, D))
    din("bout", (1, D))
    din("w1T", (D, DH))
    din("w2T", (DH, D))
    din("convw", (128, KH * 3))
    din("convb", (128, KH))
    din("wqT", (D, D))
    din("wkvgT", (D, 768))
    din("woT", (D, D))
    din("qgain", (1, 16))
    din("lam", (1, 16))
    din("wm", (D, D))
    din("bm", (128, KD))
    din("emb", (128, 16))
    din("w1e", (16, 128, 512))
    din("w2e", (16, 4, 128, 128))
    din("b1e", (128, 64))
    din("b2e", (16, 128))
    din("wo2", (D, D))
    din("bo2", (1, D))
    din("sintab", (128, 512))
    din("costab", (128, 512))

    c.out = nc.dram_tensor("out", [S, D], F32, kind="ExternalOutput").ap()
    c.dbg_out = {}

    def dbg_tap(name, shape, dtype=F32):
        if name in c.dbg and name not in c.dbg_out:
            c.dbg_out[name] = nc.dram_tensor("dbg_" + name, list(shape), dtype,
                                             kind="ExternalOutput").ap()
        return c.dbg_out.get(name)
    c.dbg_tap = dbg_tap

    dscr("xs1", (S, D))
    dscr("xs2", (S, D))
    dscr("xs3", (S, D))
    dscr("w1qd", (KD, 128, DH), BF16)
    dscr("w2qd", (KH, 128, D), BF16)
    dscr("xq2d", (8, 128, KH * 256), BF16)
    dscr("xq1d", (KD, 128, S), BF16)
    dscr("xnd", (KD, 128, S), BF16)

    with ExitStack() as st:
        E = st.enter_context
        arf_t = E(nc.sbuf_tensor("arf", [128, 23 * 1024], F32))
        arb_t = E(nc.sbuf_tensor("arb", [128, 47 * 1024], BF16))
        c.arf = Arena(arf_t, 23 * 1024, "arf")
        c.arb = Arena(arb_t, 47 * 1024, "arb")
        c.banks = [E(nc.psum_tensor(f"bank{i}", [128, 512], F32)) for i in range(6)]
        c.pbb = [E(nc.psum_tensor(f"pbb{i}", [128, 1024], BF16)) for i in range(2)]

        sem_names = ("dg", "dy", "do", "dog", "ts", "tv", "st", "vt", "sv", "vs",
                     "gt", "gv", "vg", "sy", "tg")
        sems = {nm: E(nc.semaphore(nm)) for nm in sem_names}
        c.ed = {nm: Edge(sems[nm], 16 if nm in ("dg", "dy", "do", "dog") else 1)
                for nm in sem_names}

        block = E(nc.Block())
        c.sch = Sched()

        emit_all(c)
        c.sch.run(block)
    return c



def emit_all(c):
    sch, ed, arf, arb = c.sch, c.ed, c.arf, c.arb
    dt = c.dt

    c.ident_f = arf.alloc(128)
    c.ident_b = arb.alloc(128)
    c.ones_b = arb.alloc(128)
    c.ones_f = arf.alloc(128)
    c.magic = arf.alloc(8)
    c.winm1 = arf.alloc(256)
    c.diagm1 = arf.alloc(128)
    c.bout_rep = arf.alloc(1024)
    c.bo2_rep = arf.alloc(1024)
    c.g_ap = c.magic[:, 6:7]
    c.m64_ap = c.magic[:, 2:3]
    c.nm64_ap = c.magic[:, 3:4]

    def setup_g(g):
        g.memset(c.ident_f[:], 0.0)
        g.memset(c.ident_b[:], 0.0)
        g.memset(c.winm1[:], 0.0)
        g.memset(c.diagm1[:], 0.0)
        g.memset(c.ones_b[0:1, :], 1.0)
        g.memset(c.ones_f[0:1, :], 1.0)
        g.memset(c.magic[:, 0:1], 67108864.0)
        g.memset(c.magic[:, 4:5], 0.75)
        g.memset(c.magic[:, 7:8], 1e-5)
        g.drain()
        g.affine_select(out=c.ident_f[:], in_=c.ident_f[:], compare_op=OP.not_equal,
                        fill=1.0, base=0, pattern=[[-1, 128]], channel_multiplier=1)
        g.affine_select(out=c.ident_b[:], in_=c.ident_b[:], compare_op=OP.not_equal,
                        fill=1.0, base=0, pattern=[[-1, 128]], channel_multiplier=1)
        g.affine_select(out=c.winm1[:], in_=c.winm1[:], compare_op=OP.is_ge,
                        fill=-1.0, base=128, pattern=[[-1, 256]], channel_multiplier=1)
        return g.affine_select(out=c.diagm1[:], in_=c.diagm1[:], compare_op=OP.is_ge,
                               fill=-1.0, base=0, pattern=[[1, 128]], channel_multiplier=-1)

    sch.emit("g", setup_g, inc=ed["gv"])
    sch.wait("v", ed["gv"])

    def setup_v(v):
        v.tensor_mul(c.magic[:, 1:2], c.magic[:, 0:1], c.magic[:, 0:1])
        v.drain()
        v.tensor_scalar(c.magic[:, 2:3], c.magic[:, 1:2], 1.5, None, OP.mult)
        v.tensor_scalar(c.magic[:, 3:4], c.magic[:, 1:2], -1.5, None, OP.mult)
        v.drain()
        v.tensor_scalar(c.magic[:, 5:6], c.magic[:, 4:5], c.magic[:, 2:3], None, OP.add)
        v.drain()
        return v.tensor_scalar(c.magic[:, 6:7], c.magic[:, 5:6], c.magic[:, 3:4], None, OP.add)

    sch.emit("v", setup_v, writes=("magic",), inc=ed["vt"])
    c.magic_vt = ed["vt"].n

    c.brow = arf.alloc(1024)
    c.brow2 = arf.alloc(1024)
    sch.dma("y", lambda y: y.dma_start(out=c.brow[0:1, :], in_=dt["bout"]), ed["dy"])
    sch.dma("y", lambda y: y.dma_start(out=c.brow2[0:1, :], in_=dt["bo2"]), ed["dy"])
    sch.wait("t", ed["dy"])
    sch.wait("t", ed["gv"])

    def brep(t):
        t.matmul(c.banks[0][:, 0:512], c.ones_f[0:1, :], c.brow[0:1, 0:512],
                 start=True, stop=True)
        t.matmul(c.banks[1][:, 0:512], c.ones_f[0:1, :], c.brow[0:1, 512:1024],
                 start=True, stop=True)
        t.matmul(c.banks[2][:, 0:512], c.ones_f[0:1, :], c.brow2[0:1, 0:512],
                 start=True, stop=True)
        return t.matmul(c.banks[3][:, 0:512], c.ones_f[0:1, :], c.brow2[0:1, 512:1024],
                        start=True, stop=True)

    sch.emit("t", brep, inc=ed["ts"])
    sch.wait("s", ed["ts"])

    def brep_ev(s):
        s.copy(c.bout_rep[:, 0:512], c.banks[0][:, 0:512])
        s.copy(c.bout_rep[:, 512:1024], c.banks[1][:, 0:512])
        s.copy(c.bo2_rep[:, 0:512], c.banks[2][:, 0:512])
        return s.copy(c.bo2_rep[:, 512:1024], c.banks[3][:, 0:512])

    sch.emit("s", brep_ev, inc=ed["st"])
    c.bias_st = ed["st"].n
    sch.wait("t", ed["st"])

    c.amax_all = arf.alloc(16)

    if "p1" in c.stages:
        p1_l0_attn(c)
        sch.barrier(ed)
    if "p2" in c.stages:
        p2_l0_ffn(c)
        sch.barrier(ed)
    if "p3" in c.stages:
        p3_l1_attn(c)
        sch.barrier(ed)
    if "p4" in c.stages:
        p4_l1_moe(c)

    sch.wait("y", ed["do"])
    sch.wait("y", ed["dy"])
    sch.wait("g", ed["dog"])
    sch.wait("g", ed["dg"])


def emit_round(c, zin, rout, s1, s2, tag, extra_scale=None, inc=None):
    sch = c.sch
    sch.emit("v", lambda v: v.tensor_scalar(s1, zin, M32, -M32, OP.add, OP.add),
             reads=(tag + "z",), writes=(tag + "a",))
    sch.emit("v", lambda v: v.tensor_scalar(s2, zin, c.m64_ap, None, OP.add),
             reads=(tag + "z",), writes=(tag + "b",))
    sch.emit("v", lambda v: v.tensor_scalar(s2, s2, c.nm64_ap, None, OP.add),
             reads=(tag + "b",), writes=(tag + "c",))
    sch.emit("v", lambda v: v.tensor_sub(s2, s2, s1),
             reads=(tag + "c",), writes=(tag + "d",))
    if extra_scale is None:
        sch.emit("v", lambda v: v.scalar_tensor_tensor(rout, s2, c.g_ap, s1,
                                                       OP.mult, OP.add),
                 reads=(tag + "d",), writes=(tag + "o",), inc=inc)
    else:
        sch.emit("v", lambda v: v.scalar_tensor_tensor(s1, s2, c.g_ap, s1,
                                                       OP.mult, OP.add),
                 reads=(tag + "d",), writes=(tag + "e",))
        sch.emit("v", lambda v: v.tensor_scalar(rout, s1, extra_scale, None, OP.mult),
                 reads=(tag + "e",), writes=(tag + "o",), inc=inc)



def p1_l0_attn(c):
    _p1_body(c)


def _p1_body(c):
    sch, ed, arf, arb, banks, dt = c.sch, c.ed, c.arf, c.arb, c.banks, c.dt
    arf.push(); arb.push()

    wqkv = arb.alloc(KD * 2048)
    wout = arb.alloc(KD * 1024)
    xnTr = [arb.alloc(KD * 128), arb.alloc(KD * 128)]
    xq1r = [arb.alloc(KD * 128), arb.alloc(KD * 128)]
    qTr = [arb.alloc(KD * 128), arb.alloc(KD * 128)]
    kTr = [arb.alloc(KD * 128), arb.alloc(KD * 128)]
    ktok = [arb.alloc(1024), arb.alloc(1024)]
    Pw = [arb.alloc(256), arb.alloc(256)]
    PT = [arb.alloc(256), arb.alloc(256)]
    aoT = [arb.alloc(1024), arb.alloc(1024)]

    xin = [arf.alloc(1024), arf.alloc(1024)]
    lnt = [arf.alloc(1024), arf.alloc(1024)]
    xn = [arf.alloc(1024), arf.alloc(1024)]
    q1o = [arf.alloc(1024), arf.alloc(1024)]
    xin2 = [arf.alloc(1024), arf.alloc(1024)]
    x1t = [arf.alloc(1024), arf.alloc(1024)]
    sc = [arf.alloc(24), arf.alloc(24)]
    sqs = [arf.alloc(1024), arf.alloc(1024)]
    insc = [arf.alloc(256), arf.alloc(256)]
    qs1 = arf.alloc(1024)
    qs2 = arf.alloc(1024)
    qs3 = arf.alloc(1024)

    for k in range(KD):
        sch.dma("g", lambda g, k=k: g.dma_start(
            out=wqkv[:, k * 2048:(k + 1) * 2048],
            in_=dt["wqkv_qk"][128 * k:128 * (k + 1), :]), ed["dg"])
        sch.dma("g", lambda g, k=k: g.dma_start(
            out=wout[:, k * 1024:(k + 1) * 1024],
            in_=dt["wout_r"][128 * k:128 * (k + 1), :]), ed["dg"])
    wload_dg = ed["dg"].n
    sch.wait("v", ed["vt"], c.magic_vt)
    sch.wait("t", ed["dg"], wload_dg)

    xntap = c.dbg_tap("xn0", (S, D))
    x1tap = c.dbg_tap("x1", (S, D))
    qtap = c.dbg_tap("qT", (D, S))
    ktap = c.dbg_tap("kT", (D, S))

    ln_done_vt = [0] * NT
    x1_vt = [0] * NT
    store_dy = [0] * NT
    xq1_dma_dy = [0] * NT
    stt_vt = 0
    exp_sv = 0
    ptr_ts = 0
    prev_vs = [0, 0]
    for w in range(NT):
        p = w % 2
        jlo = 0 if w > 0 else 1
        ncols = 256 - 128 * jlo
        scol = 128 * jlo

        if w >= 2:
            sch.wait("y", ed["vt"], ln_done_vt[w - 2])
        sch.dma("y", lambda y, w=w, p=p: y.dma_start(
            out=xin[p][:], in_=dt["x"][128 * w:128 * (w + 1), :]), ed["dy"])
        ld = ed["dy"].n
        sch.wait("s", ed["dy"], ld)
        if w >= 2:
            sch.wait("s", ed["vs"], prev_vs[p])
        sch.emit("s", lambda s, p=p: s.activation(
            sqs[p][:], xin[p][:], AF.Square, accum_out=sc[p][:, 1:2]),
            writes=(f"ssq{p}",), inc=ed["sv"])
        sq_sv = ed["sv"].n
        sch.wait("v", ed["dy"], ld)
        sch.emit("v", lambda v, p=p: v.reduce_sum(sc[p][:, 0:1], xin[p][:], AX),
                 writes=(f"sc{p}",))
        sch.wait("v", ed["sv"], sq_sv)
        sch.emit("v", lambda v, p=p: v.tensor_scalar(
            sc[p][:, 2:3], sc[p][:, 0:1], 1.0 / D, None, OP.mult),
            reads=(f"sc{p}",), writes=(f"sc{p}",))
        sch.emit("v", lambda v, p=p: v.tensor_scalar(
            sc[p][:, 3:4], sc[p][:, 1:2], 1.0 / D, None, OP.mult))
        sch.emit("v", lambda v, p=p: v.tensor_mul(
            sc[p][:, 4:5], sc[p][:, 2:3], sc[p][:, 2:3]),
            reads=(f"sc{p}",), writes=(f"sc{p}",))
        sch.emit("v", lambda v, p=p: v.tensor_sub(
            sc[p][:, 5:6], sc[p][:, 3:4], sc[p][:, 4:5]),
            reads=(f"sc{p}",), writes=(f"sc{p}",), inc=ed["vs"])
        vs_here = ed["vs"].n
        sch.wait("s", ed["vs"], vs_here)
        sch.emit("s", lambda s, p=p: s.activation(
            sc[p][:, 6:7], sc[p][:, 5:6], AF.Sqrt, bias=c.magic[:, 7:8]),
            writes=(f"ssc{p}",), inc=ed["sv"])
        sch.wait("v", ed["sv"])
        sch.emit("v", lambda v, p=p: v.reciprocal(sc[p][:, 7:8], sc[p][:, 6:7]),
                 reads=(f"sc{p}",), writes=(f"sc{p}",))
        sch.emit("v", lambda v, p=p: v.tensor_mul(
            sc[p][:, 8:9], sc[p][:, 2:3], sc[p][:, 7:8]),
            reads=(f"sc{p}",), writes=(f"sc{p}",))
        sch.emit("v", lambda v, p=p: v.tensor_scalar(
            lnt[p][:], xin[p][:], sc[p][:, 7:8], None, OP.mult),
            reads=(f"sc{p}",), writes=(f"lnt{p}",))
        sch.emit("v", lambda v, p=p: v.tensor_scalar(
            xn[p][:], lnt[p][:], sc[p][:, 8:9], None, OP.subtract),
            reads=(f"lnt{p}",), writes=(f"xn{p}",))
        sch.emit("v", lambda v, p=p, w=w: v.reduce_max(
            c.amax_all[:, w:w + 1], xn[p][:], AX, apply_absolute_value=True),
            reads=(f"xn{p}",), writes=("amax",))
        sch.emit("v", lambda v, p=p, w=w: v.tensor_scalar(
            sc[p][:, 9:10], c.amax_all[:, w:w + 1], 1e-5, None, OP.max),
            reads=("amax",), writes=(f"sc{p}",))
        sch.emit("v", lambda v, p=p: v.reciprocal(sc[p][:, 10:11], sc[p][:, 9:10]),
                 reads=(f"sc{p}",), writes=(f"sc{p}",))
        sch.emit("v", lambda v, p=p: v.tensor_scalar(
            sc[p][:, 11:12], sc[p][:, 10:11], 127.0, None, OP.mult),
            reads=(f"sc{p}",), writes=(f"sc{p}",))
        sch.emit("v", lambda v, p=p: v.tensor_scalar(
            qs1[:], xn[p][:], sc[p][:, 11:12], None, OP.mult),
            reads=(f"sc{p}", f"xn{p}", "ro"), writes=("rz",))
        emit_round(c, qs1[:], q1o[p][:], qs2[:], qs3[:], "r", inc=ed["vt"])
        ln_done_vt[w] = ed["vt"].n
        prev_vs[p] = vs_here
        if xntap is not None:
            sch.wait("y", ed["vt"], ln_done_vt[w])
            sch.dma("y", lambda y, w=w, p=p: y.dma_start(
                out=xntap[128 * w:128 * (w + 1), :], in_=xn[p][:]), ed["do"])

        sch.wait("t", ed["vt"], ln_done_vt[w])
        if w >= 1:
            sch.wait("t", ed["vt"], x1_vt[w - 1])
        if w >= 2:
            sch.wait("t", ed["dy"], xq1_dma_dy[w - 2])
        for kk in range(KD):
            bsel = banks[4 + (kk % 2)]

            def tr(tt, p=p, kk=kk, bsel=bsel):
                tt.transpose(bsel[:, 0:128], xn[p][:, 128 * kk:128 * (kk + 1)],
                             c.ident_f[:])
                return tt.transpose(bsel[:, 128:256],
                                    q1o[p][:, 128 * kk:128 * (kk + 1)], c.ident_f[:])
            sch.emit("t", tr, inc=ed["ts"])
            sch.wait("s", ed["ts"])

            def ev(s, p=p, kk=kk, bsel=bsel):
                s.copy(xnTr[p][:, 128 * kk:128 * (kk + 1)], bsel[:, 0:128])
                return s.copy(xq1r[p][:, 128 * kk:128 * (kk + 1)], bsel[:, 128:256])
            sch.emit("s", ev, inc=ed["st"])
            sch.wait("t", ed["st"])
        tr_st = ed["st"].n
        sch.wait("y", ed["st"], tr_st)
        sch.dma("y", lambda y, w=w, p=p: y.dma_start(
            out=dt["xq1d"][:, :, 128 * w:128 * (w + 1)].transpose([1, 0, 2]),
            in_=xq1r[p][:].rearrange("p (k c) -> p k c", k=KD)), ed["dy"])
        xq1_dma_dy[w] = ed["dy"].n

        def qkmm(t, p=p):
            r = None
            for which, base_col in ((0, 0), (2, 1024)):
                for m in range(KD):
                    bk = banks[which + m // 4]
                    cl = 128 * (m % 4)
                    for k in range(KD):
                        r = t.matmul(bk[:, cl:cl + 128],
                                     wqkv[:, k * 2048 + base_col + 128 * m:
                                          k * 2048 + base_col + 128 * (m + 1)],
                                     xnTr[p][:, 128 * k:128 * (k + 1)],
                                     start=(k == 0), stop=(k == KD - 1))
            return r
        sch.emit("t", qkmm, inc=ed["ts"])
        sch.wait("s", ed["ts"])
        if qtap is not None:
            sch.wait("s", ed["dog"])

        def qkev(s, p=p):
            s.copy(qTr[p][:, 0:512], banks[0][:, 0:512])
            s.copy(qTr[p][:, 512:1024], banks[1][:, 0:512])
            s.copy(kTr[p][:, 0:512], banks[2][:, 0:512])
            return s.copy(kTr[p][:, 512:1024], banks[3][:, 0:512])
        sch.emit("s", qkev, inc=ed["st"])
        qk_st = ed["st"].n
        sch.wait("t", ed["st"], qk_st)
        if qtap is not None:
            sch.wait("g", ed["st"], qk_st)
            for kk in range(KD):
                sch.dma("g", lambda g, kk=kk, w=w, p=p: g.dma_start(
                    out=qtap[128 * kk:128 * (kk + 1), 128 * w:128 * (w + 1)],
                    in_=qTr[p][:, 128 * kk:128 * (kk + 1)]), ed["dog"])
                sch.dma("g", lambda g, kk=kk, w=w, p=p: g.dma_start(
                    out=ktap[128 * kk:128 * (kk + 1), 128 * w:128 * (w + 1)],
                    in_=kTr[p][:, 128 * kk:128 * (kk + 1)]), ed["dog"])

        def ktr(t, p=p):
            r = None
            for kk in range(KD):
                r = t.transpose(c.pbb[0][:, 128 * kk:128 * (kk + 1)],
                                kTr[p][:, 128 * kk:128 * (kk + 1)], c.ident_b[:])
            return r
        sch.emit("t", ktr, inc=ed["ts"])
        sch.wait("s", ed["ts"])
        sch.emit("s", lambda s, p=p: s.copy(ktok[p][:], c.pbb[0][:]), inc=ed["st"])
        sch.wait("t", ed["st"])

        sch.wait("t", ed["vt"], stt_vt)

        def sco(t, w=w, p=p, jlo=jlo, ncols=ncols, scol=scol):
            r = None
            if jlo == 0:
                for k in range(KD):
                    r = t.matmul(banks[0][:, 0:128],
                                 qTr[p][:, 128 * k:128 * (k + 1)],
                                 kTr[1 - p][:, 128 * k:128 * (k + 1)],
                                 start=(k == 0), stop=(k == KD - 1))
            for k in range(KD):
                r = t.matmul(banks[0][:, 128:256],
                             qTr[p][:, 128 * k:128 * (k + 1)],
                             kTr[p][:, 128 * k:128 * (k + 1)],
                             start=(k == 0), stop=(k == KD - 1))
            return r
        sch.emit("t", sco, inc=ed["tv"])
        sch.wait("v", ed["tv"])
        sch.wait("v", ed["sv"], exp_sv)
        sch.emit("v", lambda v, p=p, scol=scol, ncols=ncols: v.scalar_tensor_tensor(
            insc[p][:, scol:scol + ncols], c.winm1[:, scol:scol + ncols], MASK_SHIFT,
            banks[0][:, scol:scol + ncols], OP.mult, OP.add),
            writes=(f"insc{p}",), inc=ed["vt"])
        stt_vt = ed["vt"].n
        sch.emit("v", lambda v, p=p, scol=scol, ncols=ncols: v.reduce_max(
            sc[p][:, 12:13], insc[p][:, scol:scol + ncols], AX, negate=True),
            reads=(f"insc{p}",), writes=(f"wn{p}",))
        if w == 0:
            sch.emit("v", lambda v, p=p: v.tensor_scalar(
                sc[p][:, 12:13], sc[p][:, 12:13], 0.0, None, OP.min),
                reads=(f"wn{p}",), writes=(f"wn{p}",))
        sch.emit("v", lambda v, p=p: v.tensor_scalar(
            sc[p][:, 13:14], sc[p][:, 12:13], 1.0 / 32.0, None, OP.mult),
            reads=(f"wn{p}",), writes=(f"wn{p}",), inc=ed["vs"])
        sch.wait("s", ed["vs"])
        sch.wait("s", ed["ts"], ptr_ts)
        sch.emit("s", lambda s, p=p, scol=scol, ncols=ncols: s.activation(
            Pw[p][:, scol:scol + ncols], insc[p][:, scol:scol + ncols], AF.Exp,
            scale=1.0 / 32.0, bias=sc[p][:, 13:14], accum_out=sc[p][:, 14:15]),
            writes=(f"pw{p}",), inc=ed["sv"])
        exp_sv = ed["sv"].n
        sch.wait("v", ed["sv"], exp_sv)
        if w == 0:
            sch.emit("s", lambda s, p=p: s.activation(
                sc[p][:, 15:16], sc[p][:, 13:14], AF.Exp),
                writes=(f"spad{p}",), inc=ed["sv"])
            exp_sv = ed["sv"].n
            sch.wait("v", ed["sv"], exp_sv)
            sch.emit("v", lambda v, p=p: v.scalar_tensor_tensor(
                sc[p][:, 16:17], sc[p][:, 15:16], 128.0, sc[p][:, 14:15],
                OP.mult, OP.add), writes=(f"wd{p}",))
            den = 16
        else:
            den = 14
        sch.emit("v", lambda v, p=p, den=den: v.reciprocal(
            sc[p][:, 17:18], sc[p][:, den:den + 1]),
            reads=(f"wd{p}",), writes=(f"wd{p}",))
        sch.emit("v", lambda v, p=p, scol=scol, ncols=ncols: v.tensor_scalar(
            Pw[p][:, scol:scol + ncols], Pw[p][:, scol:scol + ncols],
            sc[p][:, 17:18], None, OP.mult),
            reads=(f"wd{p}", f"pw{p}"), writes=(f"pw{p}",), inc=ed["vt"])
        pn_vt = ed["vt"].n

        sch.wait("t", ed["vt"], pn_vt)

        def ptr(t, p=p, jlo=jlo):
            r = None
            for jh in range(jlo, 2):
                r = t.transpose(c.pbb[1][:, 128 * jh:128 * (jh + 1)],
                                Pw[p][:, 128 * jh:128 * (jh + 1)], c.ident_b[:])
            return r
        sch.emit("t", ptr, inc=ed["ts"])
        ptr_ts = ed["ts"].n
        sch.wait("s", ed["ts"], ptr_ts)

        def pev(s, p=p, jlo=jlo):
            r = None
            for jh in range(jlo, 2):
                r = s.copy(PT[p][:, 128 * jh:128 * (jh + 1)],
                           c.pbb[1][:, 128 * jh:128 * (jh + 1)])
            return r
        sch.emit("s", pev, inc=ed["st"])
        sch.wait("t", ed["st"])

        def pv(t, w=w, p=p, jlo=jlo):
            r = None
            for dc in range(KD):
                bk = banks[2 + dc // 4]
                cl = 128 * (dc % 4)
                for jh in range(jlo, 2):
                    slot = ktok[(w - 1 + jh) % 2]
                    r = t.matmul(bk[:, cl:cl + 128],
                                 slot[:, 128 * dc:128 * (dc + 1)],
                                 PT[p][:, 128 * jh:128 * (jh + 1)],
                                 start=(jh == jlo), stop=(jh == 1))
            return r
        sch.emit("t", pv, inc=ed["ts"])
        sch.wait("s", ed["ts"])

        def aoev(s, p=p):
            s.copy(aoT[p][:, 0:512], banks[2][:, 0:512])
            return s.copy(aoT[p][:, 512:1024], banks[3][:, 0:512])
        sch.emit("s", aoev, inc=ed["st"])
        sch.wait("t", ed["st"])

        if w >= 2:
            sch.wait("y", ed["vt"], x1_vt[w - 2])
        sch.dma("y", lambda y, w=w, p=p: y.dma_start(
            out=xin2[p][:], in_=dt["x"][128 * w:128 * (w + 1), :]), ed["dy"])
        xre_dy = ed["dy"].n

        def oproj(t, p=p):
            r = None
            for nn in range(2):
                for k in range(KD):
                    r = t.matmul(banks[4 + nn][:, 0:512],
                                 aoT[p][:, 128 * k:128 * (k + 1)],
                                 wout[:, k * 1024 + 512 * nn: k * 1024 + 512 * (nn + 1)],
                                 start=(k == 0), stop=(k == KD - 1))
            return r
        sch.emit("t", oproj, inc=ed["tv"])
        sch.wait("v", ed["tv"])
        sch.wait("v", ed["dy"], xre_dy)
        sch.wait("v", ed["st"], c.bias_st)
        if w >= 2:
            sch.wait("v", ed["dy"], store_dy[w - 2])
        for nn in range(2):
            sch.emit("v", lambda v, nn=nn, p=p: v.tensor_add(
                x1t[p][:, 512 * nn:512 * (nn + 1)], banks[4 + nn][:, 0:512],
                c.bout_rep[:, 512 * nn:512 * (nn + 1)]),
                writes=(f"x1t{p}",))
        sch.emit("v", lambda v, p=p: v.tensor_add(
            x1t[p][:], x1t[p][:], xin2[p][:]),
            reads=(f"x1t{p}",), writes=(f"x1t{p}",), inc=ed["vt"])
        x1_vt[w] = ed["vt"].n
        sch.wait("y", ed["vt"], x1_vt[w])
        sch.dma("y", lambda y, w=w, p=p: y.dma_start(
            out=dt["xs1"][128 * w:128 * (w + 1), :], in_=x1t[p][:]), ed["dy"])
        store_dy[w] = ed["dy"].n
        if x1tap is not None:
            sch.dma("y", lambda y, w=w, p=p: y.dma_start(
                out=x1tap[128 * w:128 * (w + 1), :], in_=x1t[p][:]), ed["do"])
        sch.wait("t", ed["vt"], x1_vt[w])

    c.p1_store_dy = store_dy
    arf.pop()
    arb.pop()




def p2_l0_ffn(c):
    sch, ed, arf, arb, banks, dt = c.sch, c.ed, c.arf, c.arb, c.banks, c.dt
    NSTRIP = 8
    SW = 256
    arf.push(); arb.push()
    scl = arf.alloc(16)
    amax_c = arf.alloc(16)
    c.onescol = arf.alloc(1)
    r127 = arf.alloc(2)
    convw_sb = arf.alloc(KH * 3)
    convb_sb = arf.alloc(KH)
    sch.emit("g", lambda g: (g.memset(c.onescol[:], 1.0),
                             g.memset(r127[:, 0:1], 127.0))[-1], inc=ed["gt"])
    sch.wait("t", ed["gt"])
    sch.wait("v", ed["vt"], c.magic_vt)
    sch.wait("v", ed["gt"])
    sch.emit("v", lambda v: v.reciprocal(r127[:, 1:2], r127[:, 0:1]),
             writes=("r127",))
    sch.dma("g", lambda g: g.dma_start(out=convw_sb[:], in_=dt["convw"]), ed["dg"])
    sch.dma("g", lambda g: g.dma_start(out=convb_sb[:], in_=dt["convb"]), ed["dg"])

    arf.push(); arb.push()
    wbuf = arf.alloc(4096)
    qzW = arf.alloc(4096)
    rsW1 = arf.alloc(4096)
    rsW2 = arf.alloc(4096)
    wqo = [arb.alloc(4096), arb.alloc(4096)]

    def wquant(src_dram, nchunks, width, out_dram, mean_col, tag):
        rd_vt = 0
        for kk in range(nchunks):
            sch.wait("y", ed["vt"], rd_vt)
            sch.dma("y", lambda y, kk=kk: y.dma_start(
                out=wbuf[:, 0:width],
                in_=src_dram[128 * kk:128 * (kk + 1), :]), ed["dy"])
            sch.wait("v", ed["dy"])
            sch.emit("v", lambda v: v.reduce_sum(
                scl[:, 8:9], wbuf[:, 0:width], AX, apply_absolute_value=True),
                writes=(tag + "r",))
            if kk == 0:
                sch.emit("v", lambda v: v.tensor_copy(scl[:, 10:11], scl[:, 8:9]),
                         reads=(tag + "r",), writes=(tag + "a",), inc=ed["vt"])
            else:
                sch.emit("v", lambda v: v.tensor_add(
                    scl[:, 10:11], scl[:, 10:11], scl[:, 8:9]),
                    reads=(tag + "r", tag + "a"), writes=(tag + "a",), inc=ed["vt"])
            rd_vt = ed["vt"].n
        sch.wait("t", ed["vt"])
        sch.emit("t", lambda t: t.matmul(banks[2][0:1, 0:1], c.onescol[:],
                                         scl[:, 10:11], start=True, stop=True),
                 inc=ed["tv"])
        sch.wait("v", ed["tv"])
        sch.emit("v", lambda v: v.tensor_scalar(
            scl[0:1, 0:1], banks[2][0:1, 0:1], 1.0 / 4194304.0, 1e-5,
            OP.mult, OP.max), writes=(tag + "m",))
        sch.emit("v", lambda v: v.reciprocal(scl[0:1, 1:2], scl[0:1, 0:1]),
                 reads=(tag + "m",), writes=(tag + "m",))
        sch.emit("v", lambda v: v.tensor_scalar(
            scl[0:1, 2:3], scl[0:1, 0:1], r127[0:1, 1:2], None, OP.mult),
            reads=(tag + "m", "r127"), writes=(tag + "m",), inc=ed["vt"])
        sch.wait("t", ed["vt"])
        sch.emit("t", lambda t: t.matmul(banks[2][:, 0:2], c.ones_f[0:1, :],
                                         scl[0:1, 1:3], start=True, stop=True),
                 inc=ed["ts"])
        sch.wait("s", ed["ts"])
        sch.emit("s", lambda s: s.copy(scl[:, mean_col:mean_col + 2],
                                       banks[2][:, 0:2]),
                 writes=(tag + "sr",), inc=ed["sv"])
        sch.wait("v", ed["sv"])
        ts_vt = ed["vt"].n
        out_dg = [0, 0]
        for kk in range(nchunks):
            p = kk % 2
            sch.wait("y", ed["vt"], ts_vt)
            sch.dma("y", lambda y, kk=kk: y.dma_start(
                out=wbuf[:, 0:width],
                in_=src_dram[128 * kk:128 * (kk + 1), :]), ed["dy"])
            sch.wait("v", ed["dy"])
            sch.emit("v", lambda v: v.tensor_scalar(
                qzW[:, 0:width], wbuf[:, 0:width],
                scl[:, mean_col:mean_col + 1],
                None, OP.mult), writes=(tag + "z",), inc=ed["vt"])
            ts_vt = ed["vt"].n
            emit_round(c, qzW[:, 0:width], qzW[:, 0:width],
                       rsW1[:, 0:width], rsW2[:, 0:width], tag)
            sch.wait("v", ed["dog"], out_dg[p])
            sch.emit("v", lambda v, p=p: v.tensor_scalar(
                wqo[p][:, 0:width], qzW[:, 0:width], 1.0, -1.0, OP.min, OP.max),
                reads=(tag + "o",), writes=(tag + "c",), inc=ed["vt"])
            sch.wait("g", ed["vt"])
            sch.dma("g", lambda g, kk=kk, p=p: g.dma_start(
                out=out_dram[kk], in_=wqo[p][:, 0:width]), ed["dog"])
            out_dg[p] = ed["dog"].n

    wquant(dt["w1T"], KD, DH, dt["w1qd"], 4, "w1")
    wquant(dt["w2T"], KH, D, dt["w2qd"], 6, "w2")
    arf.pop(); arb.pop()
    sch.barrier(ed)

    arf.push(); arb.push()
    w1q = arb.alloc(KD * DH)
    xqb = [arb.alloc(KD * 258), arb.alloc(KD * 258)]
    xq2b = arb.alloc(KH * SW)
    c_sb = arf.alloc(KH * SW)
    hs = [arf.alloc(258), arf.alloc(258)]
    cv = [arf.alloc(256), arf.alloc(256)]
    cmax = arf.alloc(256)
    cmin = arf.alloc(256)
    cmaxT = arf.alloc(256)
    reps = arf.alloc(512)
    arow = arf.alloc(2050)
    arowr = arf.alloc(2050)
    rows4 = arf.alloc(512)
    rowasm = arf.alloc(512)
    rs1 = arf.alloc(256)
    rs2 = arf.alloc(256)
    qz = arf.alloc(256)
    xq2i = arf.alloc(256)
    a1rep = arf.alloc(258)

    sch.emit("g", lambda g: g.memset(arow[0:1, :], 1.0), inc=ed["gt"])
    sch.wait("t", ed["gt"])
    sch.emit("v", lambda v: v.tensor_scalar(
        amax_c[:], c.amax_all[:], 1e-5, None, OP.max),
        writes=("amc",), inc=ed["vt"])
    sch.wait("t", ed["vt"])
    sch.emit("t", lambda t: t.transpose(banks[3][0:16, 0:128], amax_c[:],
                                        c.ident_f[:]), inc=ed["ts"])
    sch.wait("s", ed["ts"])
    sch.emit("s", lambda s: s.copy(rows4[0:16, 0:128], banks[3][0:16, 0:128]),
             writes=("r4",), inc=ed["sv"])
    sch.wait("y", ed["sv"])
    sch.dma("y", lambda y: y.dma_start(
        out=arowr[0:1, 1:2049].rearrange("o (p c) -> o p c", p=NT),
        in_=rows4[0:16, 0:128]), ed["dy"])
    sch.wait("v", ed["dy"])
    sch.wait("v", ed["gt"])
    sch.emit("v", lambda v: v.tensor_scalar(
        arow[0:1, 1:2049], arowr[0:1, 1:2049], scl[0:1, 5:6], None, OP.mult),
        reads=("arowr",), writes=("arow",), inc=ed["vt"])
    arow_vt = ed["vt"].n

    for kk in range(KD):
        sch.dma("g", lambda g, kk=kk: g.dma_start(
            out=w1q[:, DH * kk:DH * (kk + 1)], in_=dt["w1qd"][kk]), ed["dg"])
    w1q_dg = ed["dg"].n
    sch.wait("t", ed["dg"], w1q_dg)
    sch.wait("t", ed["vt"], arow_vt)

    xqb_free_tv = [0, 0]
    bank_free_vt = [0, 0]
    xq2dma_dy = 0
    q2_vt = 0
    for s in range(NSTRIP):
        sp = s % 2
        t0 = SW * s
        sch.wait("y", ed["tv"], xqb_free_tv[sp])
        if s == 0:
            def z0(g):
                r = None
                for kk in range(KD):
                    r = g.memset(xqb[0][:, 258 * kk:258 * kk + 1], 0.0)
                return r
            sch.emit("g", z0, inc=ed["gt"])
            sch.wait("y", ed["gt"])
            sch.dma("y", lambda y: y.dma_start(
                out=xqb[0][:].rearrange("p (k c) -> p k c", k=KD)[:, :, 1:258],
                in_=dt["xq1d"][:, :, 0:257].transpose([1, 0, 2])), ed["dy"])
        elif s == NSTRIP - 1:
            sch.wait("g", ed["tv"], xqb_free_tv[sp])

            def z7(g, sp=sp):
                r = None
                for kk in range(KD):
                    r = g.memset(xqb[sp][:, 258 * kk + 257:258 * kk + 258], 0.0)
                return r
            sch.emit("g", z7, inc=ed["gt"])
            sch.wait("y", ed["gt"])
            sch.dma("y", lambda y, sp=sp, t0=t0: y.dma_start(
                out=xqb[sp][:].rearrange("p (k c) -> p k c", k=KD)[:, :, 0:257],
                in_=dt["xq1d"][:, :, t0 - 1:S].transpose([1, 0, 2])), ed["dy"])
        else:
            sch.dma("y", lambda y, sp=sp, t0=t0: y.dma_start(
                out=xqb[sp][:].rearrange("p (k c) -> p k c", k=KD),
                in_=dt["xq1d"][:, :, t0 - 1:t0 + 257].transpose([1, 0, 2])), ed["dy"])
        xqld = ed["dy"].n
        sch.wait("t", ed["dy"], xqld)
        if s > 0:
            sch.wait("s", ed["vt"], q2_vt)

        def arep_mm(t, t0=t0):
            return t.matmul(banks[2][:, 0:258], c.ones_f[0:1, :],
                            arow[0:1, t0:t0 + 258], start=True, stop=True)
        sch.emit("t", arep_mm, inc=ed["ts"])
        sch.wait("s", ed["ts"])
        sch.emit("s", lambda s: s.copy(a1rep[:], banks[2][:, 0:258]),
                 writes=("a1rep",), inc=ed["st"])
        sch.wait("v", ed["st"])
        sch.wait("t", ed["st"])

        for hc in range(KH):
            hp = hc % 2
            sch.wait("t", ed["vt"], bank_free_vt[hp])

            def w1mm(t, hc=hc, hp=hp, sp=sp):
                r = None
                for k in range(KD):
                    r = t.matmul(banks[hp][:, 0:258],
                                 w1q[:, DH * k + 128 * hc: DH * k + 128 * (hc + 1)],
                                 xqb[sp][:, 258 * k:258 * (k + 1)],
                                 start=(k == 0), stop=(k == KD - 1))
                return r
            sch.emit("t", w1mm, inc=ed["tv"])
            sch.wait("v", ed["tv"])
            sch.emit("v", lambda v, hp=hp: v.tensor_mul(
                hs[hp][:], banks[hp][:, 0:258], a1rep[:]),
                writes=(f"hs{hp}",), inc=ed["vt"])
            bank_free_vt[hp] = ed["vt"].n
            sch.emit("v", lambda v, hp=hp, hc=hc: v.tensor_scalar(
                cv[0][:], hs[hp][:, 0:256], convw_sb[:, 3 * hc:3 * hc + 1], None,
                OP.mult), reads=(f"hs{hp}",), writes=("cv0",))
            sch.emit("v", lambda v, hp=hp, hc=hc: v.scalar_tensor_tensor(
                cv[1][:], hs[hp][:, 1:257], convw_sb[:, 3 * hc + 1:3 * hc + 2],
                cv[0][:], OP.mult, OP.add), reads=("cv0",), writes=("cv1",))
            sch.emit("v", lambda v, hp=hp, hc=hc: v.scalar_tensor_tensor(
                cv[0][:], hs[hp][:, 2:258], convw_sb[:, 3 * hc + 2:3 * hc + 3],
                cv[1][:], OP.mult, OP.add), reads=("cv1",), writes=("cv0b",),
                inc=ed["vs"])
            sch.wait("s", ed["vs"])
            sch.emit("s", lambda s, hc=hc: s.activation(
                c_sb[:, SW * hc:SW * (hc + 1)], cv[0][:], AF.Silu,
                bias=convb_sb[:, hc:hc + 1]), writes=("csb",), inc=ed["sv"])
            sch.wait("v", ed["sv"])
            if hc == 0:
                sch.emit("v", lambda v, hc=hc: v.tensor_copy(
                    cmax[:], c_sb[:, SW * hc:SW * (hc + 1)]), writes=("cmax",))
                sch.emit("v", lambda v, hc=hc: v.tensor_copy(
                    cmin[:], c_sb[:, SW * hc:SW * (hc + 1)]), writes=("cmin",))
            else:
                sch.emit("v", lambda v, hc=hc: v.tensor_tensor(
                    cmax[:], cmax[:], c_sb[:, SW * hc:SW * (hc + 1)], OP.max),
                    reads=("cmax",), writes=("cmax",))
                sch.emit("v", lambda v, hc=hc: v.tensor_tensor(
                    cmin[:], cmin[:], c_sb[:, SW * hc:SW * (hc + 1)], OP.min),
                    reads=("cmin",), writes=("cmin",))
        w1mm_tv_last = ed["tv"].n
        xqb_free_tv[sp] = w1mm_tv_last
        sch.emit("v", lambda v: v.tensor_copy(scl[0:1, 15:16], cmax[0:1, 0:1]),
                 reads=("cmax",), writes=("cmaxd",), inc=ed["vt"])
        cmax_vt = ed["vt"].n

        sch.wait("t", ed["vt"], cmax_vt)

        def trc(t):
            t.transpose(banks[3][:, 0:128], cmax[:, 0:128], c.ident_f[:])
            return t.transpose(banks[3][:, 128:256], cmax[:, 128:256], c.ident_f[:])
        sch.emit("t", trc, inc=ed["ts"])
        sch.wait("s", ed["ts"])
        sch.emit("s", lambda s: s.copy(cmaxT[:], banks[3][:, 0:256]),
                 writes=("cmT",), inc=ed["sv"])
        sch.wait("v", ed["sv"])
        for j in range(2):
            sch.emit("v", lambda v, j=j: v.reduce_max(
                amax_c[:, j:j + 1], cmaxT[:, 128 * j:128 * (j + 1)], AX),
                reads=("cmT",), writes=("am2",))
        sch.emit("v", lambda v: v.tensor_scalar(
            amax_c[:, 2:4], amax_c[:, 0:2], 1e-5, None, OP.max),
            reads=("am2",), writes=("am2c",))
        sch.emit("v", lambda v: v.reciprocal(amax_c[:, 4:6], amax_c[:, 2:4]),
                 reads=("am2c",), writes=("am2r",))
        sch.emit("v", lambda v: v.tensor_scalar(
            amax_c[:, 6:8], amax_c[:, 4:6], 127.0, None, OP.mult),
            reads=("am2r",), writes=("as2",))
        sch.emit("v", lambda v: v.tensor_scalar(
            amax_c[:, 8:10], amax_c[:, 2:4], scl[:, 7:8], None, OP.mult),
            reads=("am2c",), writes=("al2",), inc=ed["vt"])
        sc2_vt = ed["vt"].n
        sch.wait("t", ed["vt"], sc2_vt)
        sch.emit("t", lambda t: t.transpose(banks[3][0:4, 0:128], amax_c[:, 6:10],
                                            c.ident_f[:]), inc=ed["ts"])
        sch.wait("s", ed["ts"])
        sch.emit("s", lambda s: s.copy(rows4[0:4, 0:128], banks[3][0:4, 0:128]),
                 writes=("r4b",), inc=ed["sv"])
        sch.wait("y", ed["sv"])
        sch.dma("y", lambda y: y.dma_start(
            out=rowasm[0:1, 0:512].rearrange("o (p c) -> o p c", p=4),
            in_=rows4[0:4, 0:128]), ed["dy"])
        rw_dy = ed["dy"].n
        sch.wait("t", ed["dy"], rw_dy)
        sch.emit("t", lambda t: t.matmul(banks[2][:, 0:512], c.ones_f[0:1, :],
                                         rowasm[0:1, 0:512], start=True, stop=True),
                 inc=ed["ts"])
        sch.wait("s", ed["ts"])
        sch.emit("s", lambda s: s.copy(reps[:], banks[2][:, 0:512]),
                 writes=("reps",), inc=ed["sv"])
        sch.wait("v", ed["sv"])

        sch.wait("v", ed["dy"], xq2dma_dy)
        for hc in range(KH):
            sch.emit("v", lambda v, hc=hc: v.tensor_mul(
                qz[:], c_sb[:, SW * hc:SW * (hc + 1)], reps[:, 0:256]),
                reads=("reps", "csb", "ro"), writes=("rz",))
            emit_round(c, qz[:], xq2i[:], rs1[:], rs2[:], "r")
            sch.emit("v", lambda v, hc=hc: v.tensor_mul(
                xq2b[:, SW * hc:SW * (hc + 1)], xq2i[:], reps[:, 256:512]),
                reads=("ro",), writes=("xq2b",))
        sch.emit("v", lambda v: v.tensor_copy(scl[0:1, 15:16], xq2b[0:1, 0:1]),
                 reads=("xq2b",), writes=("xq2b",), inc=ed["vt"])
        q2_vt = ed["vt"].n
        sch.wait("y", ed["vt"], q2_vt)
        sch.dma("y", lambda y, s=s: y.dma_start(out=dt["xq2d"][s], in_=xq2b[:]),
                ed["dy"])
        xq2dma_dy = ed["dy"].n
    arf.pop(); arb.pop()
    sch.barrier(ed)

    arf.push(); arb.push()
    w2q = arb.alloc(KH * D)
    xq2l = arb.alloc(KH * SW)
    x2a = [arf.alloc(1024), arf.alloc(1024)]
    x2b = [arf.alloc(1024), arf.alloc(1024)]
    scl2 = arf.alloc(4)
    xq2l_free_tv = 0
    p2b_store_dy = [0, 0]
    for kk in range(KH):
        sch.dma("g", lambda g, kk=kk: g.dma_start(
            out=w2q[:, D * kk:D * (kk + 1)], in_=dt["w2qd"][kk]), ed["dg"])
    w2q_dg = ed["dg"].n
    sch.wait("t", ed["dg"], w2q_dg)
    x2tap = c.dbg_tap("x2", (S, D))
    x2_vt = [0] * NT
    for s in range(NSTRIP):
        sch.wait("y", ed["tv"], xq2l_free_tv)
        sch.dma("y", lambda y, s=s: y.dma_start(out=xq2l[:], in_=dt["xq2d"][s]),
                ed["dy"])
        ld = ed["dy"].n
        sch.wait("t", ed["dy"], ld)
        for mh in range(2):
            ti = 2 * s + mh
            p = ti % 2
            if ti >= 2:
                sch.wait("y", ed["vt"], x2_vt[ti - 2])
            sch.dma("y", lambda y, ti=ti, p=p: y.dma_start(
                out=x2a[p][:], in_=dt["xs1"][128 * ti:128 * (ti + 1), :]), ed["dy"])
            xld = ed["dy"].n

            def w2mm(t, mh=mh):
                r = None
                for nn in range(2):
                    for k in range(KH):
                        r = t.matmul(banks[2 * mh + nn][:, 0:512],
                                     xq2l[:, SW * k + 128 * mh: SW * k + 128 * (mh + 1)],
                                     w2q[:, D * k + 512 * nn: D * k + 512 * (nn + 1)],
                                     start=(k == 0), stop=(k == KH - 1))
                return r
            sch.emit("t", w2mm, inc=ed["tv"])
            xq2l_free_tv = ed["tv"].n
            sch.wait("v", ed["tv"])
            sch.wait("v", ed["dy"], xld)
            if ti >= 2:
                sch.wait("v", ed["dy"], p2b_store_dy[p])
            for nn in range(2):
                sch.emit("v", lambda v, mh=mh, nn=nn, p=p: v.tensor_add(
                    x2b[p][:, 512 * nn:512 * (nn + 1)],
                    banks[2 * mh + nn][:, 0:512],
                    x2a[p][:, 512 * nn:512 * (nn + 1)]),
                    writes=(f"x2b{p}",))
            sch.emit("v", lambda v, p=p: v.tensor_copy(
                scl2[0:1, 0:1], x2b[p][0:1, 0:1]),
                reads=(f"x2b{p}",), writes=(f"x2b{p}",), inc=ed["vt"])
            x2_vt[ti] = ed["vt"].n
            sch.wait("y", ed["vt"], x2_vt[ti])
            sch.dma("y", lambda y, ti=ti, p=p: y.dma_start(
                out=dt["xs2"][128 * ti:128 * (ti + 1), :], in_=x2b[p][:]), ed["dy"])
            p2b_store_dy[p] = ed["dy"].n
            if x2tap is not None:
                sch.dma("y", lambda y, ti=ti, p=p: y.dma_start(
                    out=x2tap[128 * ti:128 * (ti + 1), :], in_=x2b[p][:]), ed["do"])
            sch.wait("t", ed["vt"], x2_vt[ti])
    arf.pop(); arb.pop()



def p3_l1_attn(c):
    sch, ed, arf, arb, banks, dt = c.sch, c.ed, c.arf, c.arb, c.banks, c.dt
    H, KVH, HD = 16, 4, 64
    SC_HI = float(np.float32(1.0 / np.sqrt(32.0)))
    SC_LO = float(np.float32(1.0 / np.sqrt(32.0) - SC_HI))
    CB = 11.3137085
    EPSF = float(np.finfo(np.float32).eps)

    arf.push(); arb.push()
    qT = arb.alloc(KD * S)
    kT = arb.alloc(2 * S)
    vaug = arb.alloc(NT * 264)
    arb.push()
    xnT2 = arb.alloc(2 * KD * 128)
    vtok = arb.alloc(NT * 256)

    xin = [arf.alloc(1024), arf.alloc(1024)]
    lnt = [arf.alloc(1024), arf.alloc(1024)]
    xn = [arf.alloc(1024), arf.alloc(1024)]
    sc = [arf.alloc(24), arf.alloc(24)]
    sqs = [arf.alloc(1024), arf.alloc(1024)]
    qtile = arf.alloc(1024)
    qsq = arf.alloc(1024)
    qrt = arf.alloc(1024)
    qrot = arb.alloc(1024)
    kvg = arf.alloc(768)
    krot = arb.alloc(256)
    vsig = arf.alloc(256)
    sin_tok = arf.alloc(512)
    cos_tok = arf.alloc(512)
    cosrep = arf.alloc(512)
    sinrep = arf.alloc(512)
    rlat = arf.alloc(64)
    grep = arf.alloc(16)
    lamrep = arf.alloc(16)
    negC = arf.alloc(1)
    epsr = arf.alloc(2)
    qsc = arf.alloc(64)
    rowbuf = arf.alloc(32)
    ind1 = arf.alloc(128)
    ind2 = arf.alloc(128)
    r1sb = arf.alloc(512)
    r2sb = arf.alloc(512)
    rrep_sb = arf.alloc(512)
    an_sb = arf.alloc(512)
    tsb = arf.alloc(512)
    diag01 = arf.alloc(128)
    x3a = xin
    x3b = lnt

    wqTs = arb.alloc(KD * 1024)
    wkvg = arb.alloc(KD * 768)
    for k in range(KD):
        sch.dma("g", lambda g, k=k: g.dma_start(
            out=wqTs[:, 1024 * k:1024 * (k + 1)],
            in_=dt["wqT"][128 * k:128 * (k + 1), :]), ed["dg"])
        sch.dma("g", lambda g, k=k: g.dma_start(
            out=wkvg[:, 768 * k:768 * (k + 1)],
            in_=dt["wkvgT"][128 * k:128 * (k + 1), :]), ed["dg"])
    sch.dma("y", lambda y: y.dma_start(out=rowbuf[0:1, 0:16], in_=dt["qgain"]),
            ed["dy"])
    sch.dma("y", lambda y: y.dma_start(out=rowbuf[0:1, 16:32], in_=dt["lam"]),
            ed["dy"])
    wl_dg = ed["dg"].n
    wl_dy = ed["dy"].n

    def p3setup_g(g):
        g.memset(ind1[:], 0.0)
        g.memset(ind2[:], 0.0)
        g.drain()
        g.iota(ind1[0:1, 0:33], pattern=[[0, 33]], base=1, channel_multiplier=0,
               allow_small_or_imprecise_dtypes=True)
        g.iota(ind2[0:1, 64:97], pattern=[[0, 33]], base=1, channel_multiplier=0,
               allow_small_or_imprecise_dtypes=True)
        return g.tensor_scalar(diag01[:], c.diagm1[:], 1.0, None, OP.add)
    sch.emit("g", p3setup_g, inc=ed["gt"])
    sch.wait("t", ed["gt"])
    sch.wait("v", ed["gt"])

    sch.wait("t", ed["dy"], wl_dy)
    sch.emit("t", lambda t: t.matmul(banks[4][:, 0:32], c.ones_f[0:1, :],
                                     rowbuf[0:1, 0:32], start=True, stop=True),
             inc=ed["ts"])
    sch.wait("s", ed["ts"])
    sch.emit("s", lambda s: s.copy(grep[:], banks[4][:, 0:16]), writes=("grep",))
    sch.emit("s", lambda s: s.copy(lamrep[:], banks[4][:, 16:32]),
             writes=("grep",), inc=ed["sv"])
    sch.wait("v", ed["sv"])
    sch.emit("v", lambda v: v.reduce_max(negC[:, 0:1], grep[:], AX,
                                         apply_absolute_value=True),
             writes=("negC",))
    sch.emit("v", lambda v: v.tensor_scalar(negC[:, 0:1], negC[:, 0:1], -CB, None,
                                            OP.mult),
             reads=("negC",), writes=("negC",))
    EPS64 = 2.220446049250313e-16
    sch.emit("v", lambda v: v.tensor_scalar(
        epsr[:, 0:1], c.g_ap, -EPSF, EPSF, OP.mult, OP.add), writes=("epsr",))
    sch.emit("v", lambda v: v.scalar_tensor_tensor(
        epsr[:, 1:2], c.g_ap, EPS64, epsr[:, 0:1], OP.mult, OP.add),
        reads=("epsr",), writes=("epsr",), inc=ed["vt"])
    setup_vt = ed["vt"].n

    sch.dma("y", lambda y: y.dma_start(out=sin_tok[:], in_=dt["sintab"]),
            ed["dy"])
    sch.dma("y", lambda y: y.dma_start(out=cos_tok[:], in_=dt["costab"]),
            ed["dy"])
    tab_dy = ed["dy"].n
    sch.wait("v", ed["dy"], tab_dy)
    sch.emit("v", lambda v: v.tensor_copy(qsc[:, 3:4], sin_tok[:, 0:1]),
             reads=("stok", "ctok"), writes=("rA",), inc=ed["vt"])
    rot_vt = ed["vt"].n
    rtap = c.dbg_tap("rtab", (128, 1024))
    if rtap is not None:
        sch.wait("y", ed["vt"], rot_vt)
        sch.dma("y", lambda y: y.dma_start(out=rtap[:, 0:512], in_=sin_tok[:]),
                ed["do"])
        sch.dma("y", lambda y: y.dma_start(out=rtap[:, 512:1024], in_=cos_tok[:]),
                ed["do"])

    sch.wait("t", ed["dg"], wl_dg)
    sch.wait("v", ed["vt"], rot_vt)
    sch.wait("v", ed["vt"], setup_vt)
    ln_vt = [0] * NT
    xnd_dy = [0] * NT
    prev_vs = [0, 0]
    for t in range(NT):
        p = t % 2
        if t >= 2:
            sch.wait("y", ed["vt"], ln_vt[t - 2])
        sch.dma("y", lambda y, t=t, p=p: y.dma_start(
            out=xin[p][:], in_=dt["xs2"][128 * t:128 * (t + 1), :]), ed["dy"])
        ld = ed["dy"].n
        sch.wait("s", ed["dy"], ld)
        if t >= 2:
            sch.wait("s", ed["vs"], prev_vs[p])
        sch.emit("s", lambda s, p=p: s.activation(
            sqs[p][:], xin[p][:], AF.Square, accum_out=sc[p][:, 1:2]),
            writes=(f"s3q{p}",), inc=ed["sv"])
        sq_sv = ed["sv"].n
        sch.wait("v", ed["dy"], ld)
        sch.emit("v", lambda v, p=p: v.reduce_sum(sc[p][:, 0:1], xin[p][:], AX),
                 writes=(f"sc{p}",))
        sch.wait("v", ed["sv"], sq_sv)
        sch.emit("v", lambda v, p=p: v.tensor_scalar(
            sc[p][:, 2:3], sc[p][:, 0:1], 1.0 / D, None, OP.mult),
            reads=(f"sc{p}",), writes=(f"sc{p}",))
        sch.emit("v", lambda v, p=p: v.tensor_scalar(
            sc[p][:, 3:4], sc[p][:, 1:2], 1.0 / D, None, OP.mult))
        sch.emit("v", lambda v, p=p: v.tensor_mul(
            sc[p][:, 4:5], sc[p][:, 2:3], sc[p][:, 2:3]),
            reads=(f"sc{p}",), writes=(f"sc{p}",))
        sch.emit("v", lambda v, p=p: v.tensor_sub(
            sc[p][:, 5:6], sc[p][:, 3:4], sc[p][:, 4:5]),
            reads=(f"sc{p}",), writes=(f"sc{p}",), inc=ed["vs"])
        vs_here = ed["vs"].n
        sch.wait("s", ed["vs"], vs_here)
        sch.emit("s", lambda s, p=p: s.activation(
            sc[p][:, 6:7], sc[p][:, 5:6], AF.Sqrt, bias=c.magic[:, 7:8]),
            writes=(f"s3c{p}",), inc=ed["sv"])
        sch.wait("v", ed["sv"])
        sch.emit("v", lambda v, p=p: v.reciprocal(sc[p][:, 7:8], sc[p][:, 6:7]),
                 reads=(f"sc{p}",), writes=(f"sc{p}",))
        sch.emit("v", lambda v, p=p: v.tensor_mul(
            sc[p][:, 8:9], sc[p][:, 2:3], sc[p][:, 7:8]),
            reads=(f"sc{p}",), writes=(f"sc{p}",))
        sch.emit("v", lambda v, p=p: v.tensor_scalar(
            lnt[p][:], xin[p][:], sc[p][:, 7:8], None, OP.mult),
            reads=(f"sc{p}",), writes=(f"lnt{p}",))
        sch.emit("v", lambda v, p=p: v.tensor_scalar(
            xn[p][:], lnt[p][:], sc[p][:, 8:9], None, OP.subtract),
            reads=(f"lnt{p}",), writes=(f"xn{p}",), inc=ed["vt"])
        ln_vt[t] = ed["vt"].n
        prev_vs[p] = vs_here
        sch.wait("t", ed["vt"], ln_vt[t])
        if t >= 1:
            pass
        if t >= 2:
            sch.wait("t", ed["dy"], xnd_dy[t - 2])
        for kk in range(KD):
            bsel = banks[4 + (kk % 2)]
            sch.emit("t", lambda tt_, p=p, kk=kk, bsel=bsel: tt_.transpose(
                bsel[:, 0:128], xn[p][:, 128 * kk:128 * (kk + 1)], c.ident_f[:]),
                inc=ed["ts"])
            sch.wait("s", ed["ts"])
            sch.emit("s", lambda s, p=p, kk=kk, bsel=bsel: s.copy(
                xnT2[:, 1024 * p + 128 * kk: 1024 * p + 128 * (kk + 1)],
                bsel[:, 0:128]), inc=ed["st"])
            sch.wait("t", ed["st"])
        tr_st = ed["st"].n
        sch.wait("y", ed["st"], tr_st)
        sch.dma("y", lambda y, t=t, p=p: y.dma_start(
            out=dt["xnd"][:, :, 128 * t:128 * (t + 1)].transpose([1, 0, 2]),
            in_=xnT2[:, 1024 * p:1024 * (p + 1)].rearrange("p (k c) -> p k c", k=KD)),
            ed["dy"])
        xnd_dy[t] = ed["dy"].n

        for hh in range(16):
            sch.emit("v", lambda v, t=t, hh=hh: v.tensor_copy(
                cosrep[:, 32 * hh:32 * (hh + 1)], cos_tok[:, 32 * t:32 * (t + 1)]),
                reads=("f2o",), writes=("crep",))
            sch.emit("v", lambda v, t=t, hh=hh: v.tensor_copy(
                sinrep[:, 32 * hh:32 * (hh + 1)], sin_tok[:, 32 * t:32 * (t + 1)]),
                reads=("f2o",), writes=("crep",))

        def qmm(tt_, p=p):
            r = None
            for nn in range(2):
                for k in range(KD):
                    r = tt_.matmul(banks[nn][:, 0:512],
                                   xnT2[:, 1024 * p + 128 * k: 1024 * p + 128 * (k + 1)],
                                   wqTs[:, 1024 * k + 512 * nn: 1024 * k + 512 * (nn + 1)],
                                   start=(k == 0), stop=(k == KD - 1))
            return r
        sch.emit("t", qmm, inc=ed["ts"])
        sch.wait("s", ed["ts"])

        def qev(s):
            s.copy(qtile[:, 0:512], banks[0][:, 0:512])
            return s.copy(qtile[:, 512:1024], banks[1][:, 0:512])
        sch.emit("s", qev, writes=("qtile",), inc=ed["sv"])
        sch.wait("v", ed["sv"])
        sch.wait("s", ed["sv"])
        sch.emit("s", lambda s: s.activation(qsq[:], qtile[:], AF.Square),
                 reads=("qtile",), writes=("qsq",), inc=ed["sv"])
        sch.wait("v", ed["sv"])
        for hh in range(16):
            sch.emit("v", lambda v, hh=hh: v.reduce_sum(
                qsc[:, hh:hh + 1], qsq[:, 64 * hh:64 * (hh + 1)], AX),
                reads=("qsq", "qrmz"), writes=("qrm",))
        sch.emit("v", lambda v: v.tensor_scalar(
            qsc[:, 16:32], qsc[:, 0:16], 1.0 / 64.0, None, OP.mult),
            reads=("qrm",), writes=("qrmz",), inc=ed["vs"])
        sch.wait("s", ed["vs"])
        sch.emit("s", lambda s: s.activation(qsc[:, 32:48], qsc[:, 16:32], AF.Sqrt,
                                             bias=epsr[:, 1:2]),
                 writes=("qrs",), inc=ed["sv"])
        sch.wait("v", ed["sv"])
        sch.emit("v", lambda v: v.reciprocal(qsc[:, 48:64], qsc[:, 32:48]),
                 reads=("qrs", "qrmz"), writes=("qrr",))
        sch.emit("v", lambda v: v.tensor_mul(qsc[:, 0:16], qsc[:, 48:64], grep[:]),
                 reads=("qrr",), writes=("qrg",))
        sch.emit("v", lambda v: v.tensor_scalar(
            qsc[:, 16:32], qsc[:, 0:16], SC_HI, None, OP.mult),
            reads=("qrg",), writes=("qrg2",))
        sch.emit("v", lambda v: v.scalar_tensor_tensor(
            qsc[:, 32:48], qsc[:, 0:16], SC_LO, qsc[:, 16:32], OP.mult, OP.add),
            reads=("qrg", "qrg2"), writes=("qrg3",))
        q3 = qtile[:].rearrange("p (h d) -> p h d", h=16)
        r3 = qrt[:].rearrange("p (h d) -> p h d", h=16)
        c3 = cosrep[:].rearrange("p (h d) -> p h d", h=16)
        s3 = sinrep[:].rearrange("p (h d) -> p h d", h=16)
        sch.emit("v", lambda v: v.tensor_mul(r3[:, :, 0:32], q3[:, :, 0:32],
                                             c3[:, :, 0:32]),
                 reads=("qtile", "crep"), writes=("qrt",))
        sch.emit("v", lambda v: v.scalar_tensor_tensor(
            r3[:, :, 0:32], q3[:, :, 32:64], 1.0, r3[:, :, 0:32], OP.bypass, OP.mult)
            if False else
            v.tensor_mul(qsq[:].rearrange("p (h d) -> p h d", h=16)[:, :, 0:32],
                         q3[:, :, 32:64], s3[:, :, 0:32]),
            reads=("qtile", "crep", "qsq"), writes=("qs2",))
        sch.emit("v", lambda v: v.tensor_add(
            r3[:, :, 0:32], r3[:, :, 0:32],
            qsq[:].rearrange("p (h d) -> p h d", h=16)[:, :, 0:32]),
            reads=("qrt", "qs2"), writes=("qrt",))
        sch.emit("v", lambda v: v.tensor_mul(
            qsq[:].rearrange("p (h d) -> p h d", h=16)[:, :, 32:64],
            q3[:, :, 32:64], c3[:, :, 0:32]),
            reads=("qtile", "crep"), writes=("qs3",))
        sch.emit("v", lambda v: v.tensor_mul(
            r3[:, :, 32:64], q3[:, :, 0:32], s3[:, :, 0:32]),
            reads=("qtile", "crep"), writes=("qrt2",))
        sch.emit("v", lambda v: v.tensor_sub(
            r3[:, :, 32:64],
            qsq[:].rearrange("p (h d) -> p h d", h=16)[:, :, 32:64],
            r3[:, :, 32:64]),
            reads=("qs3", "qrt2"), writes=("qrt",))
        OUTBLK = [0, 2, 4, 6, 1, 3, 5, 7, 8, 10, 12, 14, 9, 11, 13, 15]
        for hh in range(16):
            ob = OUTBLK[hh]
            sch.emit("v", lambda v, hh=hh, ob=ob: v.tensor_scalar(
                qrot[:, 64 * ob:64 * (ob + 1)], qrt[:, 64 * hh:64 * (hh + 1)],
                qsc[:, 32 + hh:33 + hh], None, OP.mult),
                reads=("qrt", "qrg3"), writes=("qrot",))
        sch.emit("v", lambda v: v.tensor_copy(qsc[:, 32:33], qrot[:, 0:1]),
                 reads=("qrot",), writes=("qrot",), inc=ed["vt"])
        q_vt = ed["vt"].n
        sch.wait("t", ed["vt"], q_vt)

        def qtr(tt_, t=t):
            r = None
            for kk in range(KD):
                r = tt_.transpose(c.pbb[0][:, 128 * kk:128 * (kk + 1)],
                                  qrot[:, 128 * kk:128 * (kk + 1)], c.ident_b[:])
            return r
        sch.emit("t", qtr, inc=ed["ts"])
        sch.wait("s", ed["ts"])

        def qtev(s, t=t):
            r = None
            for kk in range(KD):
                r = s.copy(qT[:, kk * S + 128 * t: kk * S + 128 * (t + 1)],
                           c.pbb[0][:, 128 * kk:128 * (kk + 1)])
            return r
        sch.emit("s", qtev, inc=ed["st"])
        sch.wait("t", ed["st"])

        def kmm(tt_, p=p):
            r = None
            for nn, (c0, cn) in enumerate(((0, 512), (512, 256))):
                for k in range(KD):
                    r = tt_.matmul(banks[nn][:, 0:cn],
                                   xnT2[:, 1024 * p + 128 * k: 1024 * p + 128 * (k + 1)],
                                   wkvg[:, 768 * k + c0: 768 * k + c0 + cn],
                                   start=(k == 0), stop=(k == KD - 1))
            return r
        sch.emit("t", kmm, inc=ed["ts"])
        sch.wait("s", ed["ts"])

        def kev(s):
            s.copy(kvg[:, 0:512], banks[0][:, 0:512])
            return s.copy(kvg[:, 512:768], banks[1][:, 0:256])
        sch.emit("s", kev, writes=("kvg",), inc=ed["sv"])
        sch.wait("v", ed["sv"])
        sch.wait("s", ed["sv"])
        sch.emit("s", lambda s: s.activation(qsq[:, 0:256], kvg[:, 0:256], AF.Square),
                 reads=("kvg",), writes=("ksq",), inc=ed["sv"])
        sch.wait("v", ed["sv"])
        for hh in range(4):
            sch.emit("v", lambda v, hh=hh: v.reduce_sum(
                qsc[:, hh:hh + 1], qsq[:, 64 * hh:64 * (hh + 1)], AX),
                reads=("ksq", "krmz"), writes=("krm",))
        sch.emit("v", lambda v: v.tensor_scalar(
            qsc[:, 4:8], qsc[:, 0:4], 1.0 / 64.0, None, OP.mult),
            reads=("krm",), writes=("krmz",), inc=ed["vs"])
        sch.wait("s", ed["vs"])
        sch.emit("s", lambda s: s.activation(qsc[:, 8:12], qsc[:, 4:8], AF.Sqrt,
                                             bias=epsr[:, 1:2]),
                 writes=("krs",), inc=ed["sv"])
        sch.wait("v", ed["sv"])
        sch.emit("v", lambda v: v.reciprocal(qsc[:, 12:16], qsc[:, 8:12]),
                 reads=("krs", "krmz"), writes=("krr",))
        k3 = kvg[:, 0:256].rearrange("p (h d) -> p h d", h=4)
        kr3 = qrt[:, 0:256].rearrange("p (h d) -> p h d", h=4)
        kq3 = qsq[:, 256:512].rearrange("p (h d) -> p h d", h=4)
        kc3 = cosrep[:, 0:128].rearrange("p (h d) -> p h d", h=4)
        ks3 = sinrep[:, 0:128].rearrange("p (h d) -> p h d", h=4)
        sch.emit("v", lambda v: v.tensor_mul(kr3[:, :, 0:32], k3[:, :, 0:32],
                                             kc3[:, :, 0:32]),
                 reads=("kvg", "crep"), writes=("krt",))
        sch.emit("v", lambda v: v.tensor_mul(kq3[:, :, 0:32], k3[:, :, 32:64],
                                             ks3[:, :, 0:32]),
                 reads=("kvg", "crep"), writes=("kq2",))
        sch.emit("v", lambda v: v.tensor_add(kr3[:, :, 0:32], kr3[:, :, 0:32],
                                             kq3[:, :, 0:32]),
                 reads=("krt", "kq2"), writes=("krt",))
        sch.emit("v", lambda v: v.tensor_mul(kq3[:, :, 32:64], k3[:, :, 32:64],
                                             kc3[:, :, 0:32]),
                 reads=("kvg", "crep"), writes=("kq3",))
        sch.emit("v", lambda v: v.tensor_mul(kr3[:, :, 32:64], k3[:, :, 0:32],
                                             ks3[:, :, 0:32]),
                 reads=("kvg", "crep"), writes=("krt2",))
        sch.emit("v", lambda v: v.tensor_sub(kr3[:, :, 32:64], kq3[:, :, 32:64],
                                             kr3[:, :, 32:64]),
                 reads=("kq3", "krt2"), writes=("krt",))
        for hh in range(4):
            sch.emit("v", lambda v, hh=hh: v.tensor_scalar(
                krot[:, 64 * hh:64 * (hh + 1)], qrt[:, 64 * hh:64 * (hh + 1)],
                qsc[:, 12 + hh:13 + hh], None, OP.mult),
                reads=("krt", "krr"), writes=("krot",))
        sch.emit("v", lambda v: v.tensor_copy(qsc[:, 32:33], krot[:, 0:1]),
                 reads=("krot",), writes=("krot",), inc=ed["vs"])
        k_vs = ed["vs"].n
        sch.wait("s", ed["vs"], k_vs)
        sch.emit("s", lambda s: s.activation(vsig[:], kvg[:, 512:768], AF.Sigmoid),
                 reads=("kvg",), writes=("vsig",), inc=ed["sv"])
        sch.wait("v", ed["sv"])
        sch.emit("v", lambda v, t=t: v.tensor_mul(
            vtok[:, 256 * t:256 * (t + 1)], kvg[:, 256:512], vsig[:]),
            reads=("vsig", "kvg"), writes=("vtok",), inc=ed["vt"])
        kv_vt = ed["vt"].n
        sch.wait("t", ed["vt"], kv_vt)

        def ktr(tt_, t=t):
            tt_.transpose(c.pbb[0][:, 0:128], krot[:, 0:128], c.ident_b[:])
            return tt_.transpose(c.pbb[0][:, 128:256], krot[:, 128:256], c.ident_b[:])
        sch.emit("t", ktr, inc=ed["ts"])
        sch.wait("s", ed["ts"])

        def ktev(s, t=t):
            s.copy(kT[:, 128 * t:128 * (t + 1)], c.pbb[0][:, 0:128])
            return s.copy(kT[:, S + 128 * t:S + 128 * (t + 1)],
                          c.pbb[0][:, 128:256])
        sch.emit("s", ktev, inc=ed["st"])
        sch.wait("t", ed["st"])

    def vaug_build(g):
        r = None
        for jt in range(NT):
            for kvh in range(4):
                for hf in range(2):
                    off = 264 * jt + 66 * kvh + 33 * hf
                    r = g.memset(vaug[:, off + 32:off + 33], 1.0)
        return r
    sch.emit("g", vaug_build, inc=ed["gv"])
    sch.wait("v", ed["gv"])
    vg_vt = []
    for jt in range(NT):
        for kvh in range(4):
            for hf in range(2):
                off = 264 * jt + 66 * kvh + 33 * hf
                sch.emit("v", lambda v, jt=jt, kvh=kvh, hf=hf, off=off: v.tensor_copy(
                    vaug[:, off:off + 32],
                    vtok[:, 256 * jt + 64 * kvh + 32 * hf: 256 * jt + 64 * kvh + 32 * hf + 32]),
                    reads=("vtok",), writes=("vaug",))
    sch.emit("v", lambda v: v.tensor_copy(qsc[:, 32:33], vaug[:, 0:1]),
             reads=("vaug",), writes=("vaug",), inc=ed["vt"])
    vaug_vt = ed["vt"].n
    sch.wait("t", ed["vt"], vaug_vt)
    arb.pop()
    sch.barrier(ed)
    arb.push()
    yT = [arb.alloc(KD * 512), arb.alloc(KD * 512)]
    Pb = [arb.alloc(512), arb.alloc(512)]
    woTs = arb.alloc(KD * 1024)
    for k in range(KD):
        sch.dma("g", lambda g, k=k: g.dma_start(
            out=woTs[:, 1024 * k:1024 * (k + 1)],
            in_=dt["woT"][128 * k:128 * (k + 1), :]), ed["dg"])
    wo_dg = ed["dg"].n
    sch.wait("t", ed["dg"], wo_dg)

    x3tap = c.dbg_tap("x3", (S, D))
    sco_free_sv = {0: 0, 1: 0, 4: 0, 5: 0}
    pv_free_vt = 0
    x3_vt = [0] * NT
    st3_dy = [0] * NT
    yring_free_tv = [0, 0]
    for ic in range(4):
        icp = ic % 2
        njc = 4 * ic + 4
        sch.wait("v", ed["tv"], yring_free_tv[icp])
        for h in range(H):
            kvh = h // 4
            kchunk = kvh // 2
            kbase = (64 * kvh) % 128
            for jc in range(njc):
                m = jc - 4 * ic
                icol0 = 128 * m if m > 0 else 0
                ncols = 512 - icol0
                for hf in range(2):
                    sb = (jc % 2) + (0 if hf == 0 else 4)
                    qchunk = (h % 4) + 4 * (h // 8)
                    qbase = 64 * ((h // 4) % 2) + 32 * hf
                    kb2 = kbase + 32 * hf
                    sch.wait("t", ed["sv"], sco_free_sv[sb])

                    def smm(tt_, ic=ic, jc=jc, sb=sb, qbase=qbase,
                            qchunk=qchunk, kb2=kb2, kchunk=kchunk, icol0=icol0,
                            ncols=ncols):
                        return tt_.matmul(
                            banks[sb][:, icol0:icol0 + ncols],
                            kT[kb2:kb2 + 32,
                               kchunk * S + 128 * jc: kchunk * S + 128 * (jc + 1)],
                            qT[qbase:qbase + 32,
                               qchunk * S + 512 * ic + icol0:
                               qchunk * S + 512 * ic + icol0 + ncols],
                            start=True, stop=True,
                            tile_position=(kb2, 0))
                    sch.emit("t", smm, inc=ed["ts"])
                    sch.wait("s", ed["ts"])
                    sch.wait("s", ed["vt"], pv_free_vt)

                    def sev(s, hf=hf, sb=sb, icol0=icol0, ncols=ncols):
                        return s.activation(
                            Pb[hf][:, icol0:icol0 + ncols],
                            banks[sb][:, icol0:icol0 + ncols], AF.Exp,
                            bias=negC[:, 0:1])
                    sch.emit("s", sev, writes=(f"pb{hf}",), inc=ed["sv"])
                    sco_free_sv[sb] = ed["sv"].n
                sch.wait("v", ed["sv"], sco_free_sv[(jc % 2) + 4])
                if m >= 0:
                    for hf in range(2):
                        if icol0 > 0:
                            sch.emit("v", lambda v, hf=hf, icol0=icol0: v.tensor_scalar(
                                Pb[hf][:, 0:icol0], Pb[hf][:, 0:icol0], 0.0, None,
                                OP.mult), reads=(f"pb{hf}",), writes=(f"pb{hf}",))
                        sch.emit("v", lambda v, hf=hf, icol0=icol0: v.tensor_mul(
                            Pb[hf][:, icol0:icol0 + 128],
                            Pb[hf][:, icol0:icol0 + 128], diag01[:]),
                            reads=(f"pb{hf}",), writes=(f"pb{hf}",))
                sch.emit("v", lambda v: v.tensor_copy(qsc[:, 33:34], Pb[0][:, 0:1].to_broadcast([128, 1]) if False else Pb[0][:, 0:1]),
                         reads=("pb0", "pb1"), writes=("pbm",), inc=ed["vt"])
                pv_free_vt = ed["vt"].n
                sch.wait("t", ed["vt"], pv_free_vt)

                def pvm2(tt_, jc=jc, kvh=kvh, njc=njc):
                    r = None
                    for hf in range(2):
                        r = tt_.matmul(banks[2 + hf][0:33, 0:512],
                                       vaug[:, 264 * jc + 66 * kvh + 33 * hf:
                                            264 * jc + 66 * kvh + 33 * hf + 33],
                                       Pb[hf][:, 0:512],
                                       start=(jc == 0), stop=(jc == njc - 1))
                    return r
                sch.emit("t", pvm2, inc=ed["ts"])
            pv_done_ts = ed["ts"].n
            sch.wait("v", ed["ts"], pv_done_ts)
            sch.emit("v", lambda v: v.reciprocal(r1sb[0:1, :], banks[2][32:33, 0:512]),
                     writes=("r12",))
            sch.emit("v", lambda v: v.reciprocal(r2sb[0:1, :], banks[3][32:33, 0:512]),
                     writes=("r12",), inc=ed["vt"])
            rc_vt = ed["vt"].n
            sch.wait("t", ed["vt"], rc_vt)

            def rrepmm(tt_):
                tt_.matmul(banks[4][:, 0:512], ind1[0:1, :], r1sb[0:1, :],
                           start=True, stop=False)
                return tt_.matmul(banks[4][:, 0:512], ind2[0:1, :], r2sb[0:1, :],
                                  start=False, stop=True)
            sch.emit("t", rrepmm, inc=ed["ts"])
            sch.wait("s", ed["ts"])
            sch.emit("s", lambda s: s.copy(rrep_sb[:], banks[4][:, 0:512]),
                     writes=("rrep",), inc=ed["sv"])
            sch.wait("v", ed["sv"])
            ybase = (64 * h) % 128
            ychunk = (64 * h) // 128
            sch.emit("v", lambda v: v.tensor_mul(
                an_sb[0:32, :], banks[2][0:32, 0:512], rrep_sb[0:32, :]),
                reads=("rrep",), writes=("an",))
            sch.emit("v", lambda v: v.tensor_mul(
                an_sb[64:96, :], banks[3][0:32, 0:512], rrep_sb[64:96, :]),
                reads=("rrep",), writes=("an",))
            sch.emit("v", lambda v, h=h: v.tensor_scalar(
                tsb[0:32, :], an_sb[64:96, :], lamrep[64:96, h:h + 1], None, OP.mult),
                reads=("an",), writes=("tsb",))
            sch.emit("v", lambda v, icp=icp, ybase=ybase, ychunk=ychunk: v.tensor_sub(
                yT[icp][ybase:ybase + 32, 512 * ychunk:512 * (ychunk + 1)],
                an_sb[0:32, :], tsb[0:32, :]),
                reads=("an", "tsb"), writes=("yT",))
            sch.emit("v", lambda v, icp=icp, ybase=ybase, ychunk=ychunk: v.tensor_add(
                yT[icp][ybase + 32:ybase + 64, 512 * ychunk:512 * (ychunk + 1)],
                an_sb[0:32, :], tsb[0:32, :]),
                reads=("an", "tsb"), writes=("yT",), inc=ed["vt"])
            yv = ed["vt"].n
            sch.wait("t", ed["vt"], yv)
        y_ic_vt = ed["vt"].n

        sch.wait("t", ed["vt"], y_ic_vt)
        for tl in range(4):
            t = 4 * ic + tl
            p = t % 2
            if t >= 2:
                sch.wait("y", ed["vt"], x3_vt[t - 2])
            else:
                sch.wait("y", ed["vt"], y_ic_vt)
            sch.dma("y", lambda y, t=t, p=p: y.dma_start(
                out=x3a[p][:], in_=dt["xs2"][128 * t:128 * (t + 1), :]), ed["dy"])
            xld = ed["dy"].n

            def omm(tt_, icp=icp, tl=tl):
                r = None
                for nn in range(2):
                    for k in range(KD):
                        r = tt_.matmul(banks[4 + nn][:, 0:512],
                                       yT[icp][:, 512 * k + 128 * tl:
                                               512 * k + 128 * (tl + 1)],
                                       woTs[:, 1024 * k + 512 * nn:
                                            1024 * k + 512 * (nn + 1)],
                                       start=(k == 0), stop=(k == KD - 1))
                return r
            sch.emit("t", omm, inc=ed["tv"])
            sch.wait("v", ed["tv"])
            sch.wait("v", ed["dy"], xld)
            if t >= 2:
                sch.wait("v", ed["dy"], st3_dy[t - 2])
            for nn in range(2):
                sch.emit("v", lambda v, nn=nn, p=p: v.tensor_add(
                    x3b[p][:, 512 * nn:512 * (nn + 1)], banks[4 + nn][:, 0:512],
                    x3a[p][:, 512 * nn:512 * (nn + 1)]),
                    writes=(f"x3b{p}",))
            sch.emit("v", lambda v, p=p: v.tensor_copy(
                qsc[0:1, 34:35], x3b[p][0:1, 0:1]),
                reads=(f"x3b{p}",), writes=(f"x3b{p}",), inc=ed["vt"])
            x3_vt[t] = ed["vt"].n
            sch.wait("y", ed["vt"], x3_vt[t])
            sch.dma("y", lambda y, t=t, p=p: y.dma_start(
                out=dt["xs3"][128 * t:128 * (t + 1), :], in_=x3b[p][:]), ed["dy"])
            st3_dy[t] = ed["dy"].n
            if x3tap is not None:
                sch.dma("y", lambda y, t=t, p=p: y.dma_start(
                    out=x3tap[128 * t:128 * (t + 1), :], in_=x3b[p][:]), ed["do"])
            sch.wait("t", ed["vt"], x3_vt[t])
        yring_free_tv[icp] = ed["tv"].n
    arb.pop()
    arf.pop(); arb.pop()



def p4_l1_moe(c):
    sch, ed, arf, arb, banks, dt = c.sch, c.ed, c.arf, c.arb, c.banks, c.dt
    HM = 8
    NE = 16

    arf.push(); arb.push()
    hT = arb.alloc(KD * S)
    bm_sb = arf.alloc(KD)
    scl = arf.alloc(16)
    wsel = arf.alloc(64)
    comb = [arf.alloc(512), arf.alloc(512)]
    moet = arf.alloc(4096)
    x4a = [arf.alloc(1024), arf.alloc(1024)]
    x4b = [arf.alloc(1024), arf.alloc(1024)]

    arb.push()
    xnT2m = arb.alloc(KD * S)
    wm_sb = arb.alloc(KD * 1024)
    for k in range(KD):
        sch.dma("g", lambda g, k=k: g.dma_start(
            out=xnT2m[:, S * k:S * (k + 1)], in_=dt["xnd"][k]), ed["dg"])
        sch.dma("g", lambda g, k=k: g.dma_start(
            out=wm_sb[:, 1024 * k:1024 * (k + 1)],
            in_=dt["wm"][128 * k:128 * (k + 1), :]), ed["dg"])
    sch.dma("y", lambda y: y.dma_start(out=bm_sb[:], in_=dt["bm"]), ed["dy"])
    wm_dg = ed["dg"].n
    wm_dy = ed["dy"].n
    sch.wait("t", ed["dg"], wm_dg)
    for m in range(KD):
        for n in range(4):
            b = (m * 4 + n) % 2
            sch.wait("t", ed["st"])
            sch.wait("v", ed["dy"], wm_dy)

            def wmmm(tt_, m=m, n=n, b=b):
                r = None
                for k in range(KD):
                    r = tt_.matmul(banks[b][:, 0:512],
                                   wm_sb[:, 1024 * k + 128 * m: 1024 * k + 128 * (m + 1)],
                                   xnT2m[:, S * k + 512 * n: S * k + 512 * (n + 1)],
                                   start=(k == 0), stop=(k == KD - 1))
                return r
            sch.emit("t", wmmm, inc=ed["tv"])
            sch.wait("v", ed["tv"])
            sch.emit("v", lambda v, m=m, n=n, b=b: v.tensor_scalar(
                hT[:, S * m + 512 * n: S * m + 512 * (n + 1)], banks[b][:, 0:512],
                bm_sb[:, m:m + 1], None, OP.add), inc=ed["st"])
    hT_st = ed["st"].n
    arb.pop()
    sch.barrier(ed)

    arb.push()
    w1e = arb.alloc(NE * 512)
    w2e = arb.alloc(NE * 512)
    wo2s = arb.alloc(KD * 1024)
    emb_sb = arb.alloc(16)
    wselT = arb.alloc(512)
    b2_sb = arb.alloc(128)
    hm_sb = arb.alloc(2048)
    moeT = arb.alloc(1024)
    b1_sb = arf.alloc(64)
    for e in range(NE):
        sch.dma("g", lambda g, e=e: g.dma_start(
            out=w1e[:, 512 * e:512 * (e + 1)], in_=dt["w1e"][e]), ed["dg"])
        sch.dma("g", lambda g, e=e: g.dma_start(
            out=w2e[:, 512 * e:512 * (e + 1)].rearrange("p (i o) -> p i o", i=4),
            in_=dt["w2e"][e].transpose([1, 0, 2])), ed["dg"])
    for k in range(KD):
        sch.dma("g", lambda g, k=k: g.dma_start(
            out=wo2s[:, 1024 * k:1024 * (k + 1)],
            in_=dt["wo2"][128 * k:128 * (k + 1), :]), ed["dg"])
    sch.dma("g", lambda g: g.dma_start(out=emb_sb[:], in_=dt["emb"]), ed["dg"])
    sch.dma("g", lambda g: g.dma_start(
        out=b2_sb[0:16, 0:128], in_=dt["b2e"]), ed["dg"])
    sch.dma("y", lambda y: y.dma_start(out=b1_sb[:], in_=dt["b1e"]), ed["dy"])
    b1_dy = ed["dy"].n
    we_dg = ed["dg"].n
    sch.wait("t", ed["dg"], we_dg)
    sch.wait("s", ed["dy"], b1_dy)
    sch.wait("t", ed["st"], hT_st)
    sch.wait("v", ed["vt"], c.magic_vt)

    out4tap = c.dbg_tap("x4", (S, D))
    hb_free_sv = [0, 0]
    eo_free_vt = [0, 0]
    wsel_done_vs = 0
    x4_vt = [0] * NT
    st4_dy = [0] * NT
    moet_free_ts = [0] * 2
    for tc in range(4):
        tcp = tc % 2
        sch.wait("v", ed["ts"], moet_free_ts[tcp])
        for h in range(HM):
            for tl in range(4):
                sch.wait("t", ed["vs"], wsel_done_vs)

                def lgm(tt_, h=h, tc=tc, tl=tl):
                    return tt_.matmul(banks[4][:, 0:16],
                                      hT[:, S * h + 512 * tc + 128 * tl:
                                         S * h + 512 * tc + 128 * (tl + 1)],
                                      emb_sb[:], start=True, stop=True)
                sch.emit("t", lgm, inc=ed["tv"])
                sch.wait("v", ed["tv"])
                W = wsel[:, 16 * tl:16 * (tl + 1)]
                sch.emit("v", lambda v, W=W: v.reduce_max(
                    scl[:, 0:1], banks[4][:, 0:16], AX, negate=True),
                    writes=("mx",))
                sch.emit("v", lambda v: v.tensor_copy(scl[:, 8:9], scl[:, 0:1]),
                         reads=("mx",), writes=("mx",), inc=ed["vs"])
                sch.wait("s", ed["vs"])
                sch.emit("s", lambda s, W=W: s.activation(
                    W, banks[4][:, 0:16], AF.Exp, bias=scl[:, 8:9],
                    accum_out=scl[:, 1:2]), writes=("wex",), inc=ed["sv"])
                sch.wait("v", ed["sv"])
                sch.emit("v", lambda v: v.reciprocal(scl[:, 2:3], scl[:, 1:2]),
                         writes=("mx",))
                sch.emit("v", lambda v, W=W: v.tensor_scalar(
                    W, W, scl[:, 2:3], None, OP.mult),
                    reads=("wex", "mx"), writes=("wex",))
                sch.emit("v", lambda v, W=W: v.reduce_max(scl[:, 3:4], W, AX),
                         reads=("wex",), writes=("mx",))
                sch.emit("v", lambda v, W=W, tl=tl: v.tensor_scalar(
                    comb[0][:, 128 * tl:128 * tl + 16], W, scl[:, 3:4], None,
                    OP.is_ge), reads=("wex", "mx"), writes=("ismax",))
                sch.emit("v", lambda v, W=W, tl=tl: v.scalar_tensor_tensor(
                    comb[1][:, 128 * tl:128 * tl + 16],
                    comb[0][:, 128 * tl:128 * tl + 16], -1e9, W, OP.mult, OP.add),
                    reads=("ismax", "wex"), writes=("wmask",))
                sch.emit("v", lambda v, tl=tl: v.reduce_max(
                    scl[:, 5:6], comb[1][:, 128 * tl:128 * tl + 16], AX),
                    reads=("wmask",), writes=("m2",))
                sch.emit("v", lambda v, W=W, tl=tl: v.tensor_scalar(
                    comb[0][:, 128 * tl:128 * tl + 16], W, scl[:, 5:6], None,
                    OP.is_ge), reads=("m2", "wex"), writes=("sel01",))
                sch.emit("v", lambda v, W=W, tl=tl: v.tensor_mul(
                    W, W, comb[0][:, 128 * tl:128 * tl + 16]),
                    reads=("sel01", "wex"), writes=("wsel",), inc=ed["vs"])
                wsel_done_vs = ed["vs"].n
            sch.wait("t", ed["vs"], wsel_done_vs)

            def wtr(tt_):
                r = None
                for tl in range(4):
                    r = tt_.transpose(banks[5][0:16, 128 * tl:128 * (tl + 1)],
                                      wsel[:, 16 * tl:16 * (tl + 1)], c.ident_f[:])
                return r
            sch.emit("t", wtr, inc=ed["ts"])
            sch.wait("s", ed["ts"])
            sch.emit("s", lambda s: s.copy(wselT[0:16, :], banks[5][0:16, 0:512]),
                     writes=("wselT",), inc=ed["st"])
            sch.wait("t", ed["st"])

            for tl in range(4):
                sch.wait("t", ed["vt"], eo_free_vt[0])

                def b2m(tt_, tl=tl):
                    return tt_.matmul(banks[2][:, 128 * tl:128 * (tl + 1)],
                                      wselT[0:16, 128 * tl:128 * (tl + 1)],
                                      b2_sb[0:16, 0:128], start=True, stop=True)
                sch.emit("t", b2m, inc=ed["ts"])
            sch.wait("s", ed["ts"])

            def b2ev(s):
                return s.copy(comb[1][:, 0:512], banks[2][:, 0:512])
            sch.emit("s", b2ev, writes=("comb1",), inc=ed["sv"])
            sch.wait("v", ed["sv"])
            sch.wait("t", ed["sv"])

            for e in range(NE):
                ep = e % 2
                src_comb = comb[1 - (e % 2)] if e > 0 else comb[1]
                dst_comb = comb[e % 2]
                for icx in range(4):
                    hb = icx % 2
                    sch.wait("t", ed["sv"], hb_free_sv[hb])

                    def hmm(tt_, h=h, tc=tc, e=e, icx=icx, hb=hb):
                        return tt_.matmul(
                            banks[hb][:, 0:512],
                            w1e[:, 512 * e + 128 * icx: 512 * e + 128 * (icx + 1)],
                            hT[:, S * h + 512 * tc: S * h + 512 * (tc + 1)],
                            start=True, stop=True)
                    sch.emit("t", hmm, inc=ed["ts"])
                    sch.wait("s", ed["ts"])
                    sch.emit("s", lambda s, e=e, icx=icx, hb=hb: s.activation(
                        hm_sb[:, 512 * icx:512 * (icx + 1)], banks[hb][:, 0:512],
                        AF.Gelu, bias=b1_sb[:, 4 * e + icx:4 * e + icx + 1]),
                        writes=("hm",), inc=ed["sv"])
                    hb_free_sv[hb] = ed["sv"].n
                hm_sv = ed["sv"].n
                sch.wait("t", ed["sv"], hm_sv)
                for tl in range(4):
                    eb = 2 + (tl % 2)
                    sch.wait("t", ed["vt"], eo_free_vt[tl % 2])

                    def eom(tt_, e=e, tl=tl, eb=eb):
                        r = None
                        for icx in range(4):
                            r = tt_.matmul(
                                banks[eb][:, 0:128],
                                hm_sb[:, 512 * icx + 128 * tl: 512 * icx + 128 * (tl + 1)],
                                w2e[:, 512 * e + 128 * icx: 512 * e + 128 * (icx + 1)],
                                start=(icx == 0), stop=(icx == 3))
                        return r
                    sch.emit("t", eom, inc=ed["tv"])
                    sch.wait("v", ed["tv"])
                    if e == NE - 1:
                        dst = moet[:, 1024 * tl + 128 * h: 1024 * tl + 128 * (h + 1)]
                        dtag = "moet"
                    else:
                        dst = dst_comb[:, 128 * tl:128 * (tl + 1)]
                        dtag = f"comb{e % 2}"
                    sch.emit("v", lambda v, e=e, tl=tl, eb=eb, dst=dst,
                             src_comb=src_comb: v.scalar_tensor_tensor(
                        dst, banks[eb][:, 0:128], wsel[:, 16 * tl + e:16 * tl + e + 1],
                        src_comb[:, 128 * tl:128 * (tl + 1)], OP.mult, OP.add),
                        reads=("wsel", f"comb{1 - (e % 2)}" if e > 0 else "comb1"),
                        writes=(dtag,), inc=ed["vt"])
                    eo_free_vt[tl % 2] = ed["vt"].n
        moet_done_vt = ed["vt"].n

        sch.wait("t", ed["vt"], moet_done_vt)
        for tl in range(4):
            t = 4 * tc + tl
            p = t % 2

            def mtr(tt_, tl=tl):
                r = None
                for kk in range(KD):
                    r = tt_.transpose(banks[4 + kk // 4][:, 128 * (kk % 4):128 * (kk % 4 + 1)],
                                      moet[:, 1024 * tl + 128 * kk:
                                           1024 * tl + 128 * (kk + 1)], c.ident_f[:])
                return r
            sch.emit("t", mtr, inc=ed["ts"])
            sch.wait("s", ed["ts"])

            def mtev(s):
                s.copy(moeT[:, 0:512], banks[4][:, 0:512])
                return s.copy(moeT[:, 512:1024], banks[5][:, 0:512])
            sch.emit("s", mtev, writes=("moeT",), inc=ed["st"])
            sch.wait("t", ed["st"])
            if t >= 2:
                sch.wait("y", ed["vt"], x4_vt[t - 2])
            sch.dma("y", lambda y, t=t, p=p: y.dma_start(
                out=x4a[p][:], in_=dt["xs3"][128 * t:128 * (t + 1), :]), ed["dy"])
            xld = ed["dy"].n

            def o2mm(tt_):
                r = None
                for nn in range(2):
                    for k in range(KD):
                        r = tt_.matmul(banks[4 + nn][:, 0:512],
                                       moeT[:, 128 * k:128 * (k + 1)],
                                       wo2s[:, 1024 * k + 512 * nn:
                                            1024 * k + 512 * (nn + 1)],
                                       start=(k == 0), stop=(k == KD - 1))
                return r
            sch.emit("t", o2mm, inc=ed["tv"])
            sch.wait("v", ed["tv"])
            sch.wait("v", ed["dy"], xld)
            sch.wait("v", ed["st"], c.bias_st)
            if t >= 2:
                sch.wait("v", ed["dy"], st4_dy[t - 2])
            for nn in range(2):
                sch.emit("v", lambda v, nn=nn, p=p: v.tensor_add(
                    x4b[p][:, 512 * nn:512 * (nn + 1)], banks[4 + nn][:, 0:512],
                    c.bo2_rep[:, 512 * nn:512 * (nn + 1)]),
                    writes=(f"x4b{p}",))
            sch.emit("v", lambda v, p=p: v.tensor_add(
                x4b[p][:], x4b[p][:], x4a[p][:]),
                reads=(f"x4b{p}",), writes=(f"x4b{p}",), inc=ed["vt"])
            x4_vt[t] = ed["vt"].n
            sch.wait("y", ed["vt"], x4_vt[t])
            sch.dma("y", lambda y, t=t, p=p: y.dma_start(
                out=c.out[128 * t:128 * (t + 1), :], in_=x4b[p][:]), ed["do"])
            st4_dy[t] = ed["do"].n
            sch.wait("t", ed["vt"], x4_vt[t])
        moet_free_ts[tcp] = ed["ts"].n
    arf.pop(); arb.pop()




def host_inputs(ins):
    w = {}
    w["wqkv_qk"] = np.ascontiguousarray(ins["L0_attn_w_qkv"][:, :2048])
    w["wout_r"] = np.ascontiguousarray(ins["L0_attn_w_out"])
    w["bout"] = np.ascontiguousarray(ins["L0_attn_b_out"].reshape(1, D))
    w["w1T"] = np.ascontiguousarray(ins["L0_ffn_w1"].T)
    w["w2T"] = np.ascontiguousarray(ins["L0_ffn_w2"].T)
    w["convw"] = np.ascontiguousarray(
        ins["L0_ffn_conv_w"].reshape(KH, 128, 3).transpose(1, 0, 2).reshape(128, KH * 3))
    w["convb"] = np.ascontiguousarray(ins["L0_ffn_conv_b"].reshape(KH, 128).T)
    w["wqT"] = np.ascontiguousarray(ins["L1_attn_Wq"].T)
    w["wkvgT"] = np.ascontiguousarray(np.concatenate(
        [ins["L1_attn_Wk"].T, ins["L1_attn_Wv"].T, ins["L1_attn_Wg"].T], axis=1))
    w["woT"] = np.ascontiguousarray(ins["L1_attn_Wo"].T)
    w["qgain"] = np.ascontiguousarray(ins["L1_attn_q_gain"].reshape(1, 16))
    w["lam"] = np.ascontiguousarray(ins["L1_attn_lambda_param"].reshape(1, 16))
    w["wm"] = np.ascontiguousarray(ins["L1_ffn_Wm"])
    w["bm"] = np.ascontiguousarray(ins["L1_ffn_bm"].reshape(KD, 128).T)
    w["emb"] = np.ascontiguousarray(ins["L1_ffn_expert_emb"])
    w["w1e"] = np.ascontiguousarray(ins["L1_ffn_W1"])
    w["w2e"] = np.ascontiguousarray(ins["L1_ffn_W2"].reshape(16, 4, 128, 128))
    w["b1e"] = np.ascontiguousarray(
        ins["L1_ffn_b1"].reshape(16, 4, 128).transpose(2, 0, 1).reshape(128, 64))
    w["b2e"] = np.ascontiguousarray(ins["L1_ffn_b2"])
    w["wo2"] = np.ascontiguousarray(ins["L1_ffn_Wo"])
    w["bo2"] = np.ascontiguousarray(ins["L1_ffn_bo"].reshape(1, D))
    import jax
    import jax.numpy as jnp
    with jax.default_device(jax.devices("cpu")[0]):
        angles = jnp.arange(0, 64, 2, dtype=jnp.float32) / 64 * math.pi
        pos = jnp.arange(S, dtype=jnp.float32)
        radius = 1.0 / (1.0 + pos[:, None] * 0.01)
        ang = pos[:, None] * angles[None, :]
        costab = np.asarray(radius * jnp.cos(ang))
        sintab = np.asarray(radius * jnp.sin(ang))
    w["sintab"] = np.ascontiguousarray(
        sintab.reshape(16, 128, 32).transpose(1, 0, 2).reshape(128, 512))
    w["costab"] = np.ascontiguousarray(
        costab.reshape(16, 128, 32).transpose(1, 0, 2).reshape(128, 512))
    return w


def kernel(**ins):
    x = np.ascontiguousarray(ins["L0_attn_x"])
    shared = host_inputs(ins)
    c = build_program()
    in_maps = [dict(shared, x=np.ascontiguousarray(x[cc])) for cc in range(8)]
    res = run_bass_kernel_spmd(c.nc, in_maps, core_ids=list(range(8)))
    return np.stack([r["out"] for r in res.results], axis=0)


# version 15
# speedup vs baseline: 1.1773x; 1.1773x over previous
import math
import numpy as np
from contextlib import ExitStack

import concourse.bass as bass
import concourse.mybir as mybir
from concourse.bass_utils import run_bass_kernel_spmd

F32 = mybir.dt.float32
BF16 = mybir.dt.bfloat16
AF = mybir.ActivationFunctionType
OP = mybir.AluOpType
AX = mybir.AxisListType.X

S = 2048
D = 1024
DH = 4096
NT = S // 128
KD = D // 128
KH = DH // 128
M32 = 12582912.0
MASK_SHIFT = 100000.0

ALL_STAGES = ("p1", "p2", "p3", "p4")


class Edge:
    def __init__(self, sem, step=1):
        self.sem, self.step, self.n = sem, step, 0

    def attach(self, instr):
        instr.then_inc(self.sem, self.step)
        return instr


class Sched:

    def __init__(self):
        self.q = {e: [] for e in "tsvgy"}
        self.dirty = {e: set() for e in "tsvgy"}

    def emit(self, eng, fn, reads=(), writes=(), inc=None):
        if eng in "svg" and any(r in self.dirty[eng] for r in (*reads, *writes)):
            self.q[eng].append(lambda E: E.drain())
            self.dirty[eng] = set()
        if inc is not None:
            inc.n += inc.step
            self.q[eng].append(lambda E, f=fn, e=inc: e.attach(f(E)))
        else:
            self.q[eng].append(lambda E, f=fn: f(E))
        for w in writes:
            self.dirty[eng].add(w)

    def dma(self, eng, fn, edge):
        self.emit(eng, fn, inc=edge)
        self.wait(eng, edge)

    def wait(self, eng, edge, val=None):
        v = edge.n if val is None else val
        if v > 0:
            self.q[eng].append(lambda E, e=edge, vv=v: E.wait_ge(e.sem, vv))

    def barrier(self, ed):
        owner = {"vt": "v", "vs": "v", "vg": "v", "ts": "t", "tv": "t",
                 "tg": "t", "st": "s", "sv": "s", "sy": "s", "gt": "g",
                 "gv": "g", "dg": "g", "dy": "y", "do": "y", "dog": "g"}
        for eng in "tsvgy":
            for nm, e in ed.items():
                if owner.get(nm) != eng and e.n > 0:
                    self.wait(eng, e)

    def run(self, block):
        for attr, key in (("tensor", "t"), ("scalar", "s"), ("vector", "v"),
                          ("gpsimd", "g"), ("sync", "y")):
            def make(ops):
                def body(E):
                    for f in ops:
                        f(E)
                return body
            getattr(block, attr)(make(self.q[key]))


class Arena:
    def __init__(self, tensor, cols, name):
        self.t, self.cols, self.name = tensor, cols, name
        self.off, self.marks, self.hw = 0, [], 0

    def alloc(self, n):
        a = self.off
        self.off += n
        self.hw = max(self.hw, self.off)
        assert self.off <= self.cols, f"{self.name} overflow {self.off}>{self.cols}"
        return self.t[:, a:a + n]

    def push(self):
        self.marks.append(self.off)

    def pop(self):
        self.off = self.marks.pop()


class Ctx:
    pass


def build_program(stages=ALL_STAGES, dbg=()):
    nc = bass.Bass("TRN2", target_bir_lowering=False)
    c = Ctx()
    c.nc = nc
    c.stages = stages
    c.dbg = set(dbg)
    c.dt = {}

    def din(name, shape, dtype=F32):
        c.dt[name] = nc.dram_tensor(name, list(shape), dtype, kind="ExternalInput").ap()

    def dscr(name, shape, dtype=F32):
        c.dt[name] = nc.dram_tensor(name, list(shape), dtype, kind="Internal").ap()

    din("x", (S, D))
    din("wqkv_qk", (D, 2048))
    din("wout_r", (D, D))
    din("bout", (1, D))
    din("w1T", (D, DH))
    din("w2T", (DH, D))
    din("convw", (128, KH * 3))
    din("convb", (128, KH))
    din("wqT", (D, D))
    din("wkvgT", (D, 768))
    din("woT", (D, D))
    din("qgain", (1, 16))
    din("lam", (1, 16))
    din("wm", (D, D))
    din("bm", (128, KD))
    din("emb", (128, 16))
    din("w1e", (16, 128, 512))
    din("w2e", (16, 4, 128, 128))
    din("b1e", (128, 64))
    din("b2e", (16, 128))
    din("wo2", (D, D))
    din("bo2", (1, D))
    din("sintab", (128, 512))
    din("costab", (128, 512))

    c.out = nc.dram_tensor("out", [S, D], F32, kind="ExternalOutput").ap()
    c.dbg_out = {}

    def dbg_tap(name, shape, dtype=F32):
        if name in c.dbg and name not in c.dbg_out:
            c.dbg_out[name] = nc.dram_tensor("dbg_" + name, list(shape), dtype,
                                             kind="ExternalOutput").ap()
        return c.dbg_out.get(name)
    c.dbg_tap = dbg_tap

    dscr("xs1", (S, D))
    dscr("xs2", (S, D))
    dscr("xs3", (S, D))
    dscr("w1qd", (KD, 128, DH), BF16)
    dscr("w2qd", (KH, 128, D), BF16)
    dscr("xq2d", (8, 128, KH * 256), BF16)
    dscr("xq1d", (KD, 128, S), BF16)
    dscr("xnd", (KD, 128, S), BF16)

    with ExitStack() as st:
        E = st.enter_context
        arf_t = E(nc.sbuf_tensor("arf", [128, 23 * 1024], F32))
        arb_t = E(nc.sbuf_tensor("arb", [128, 47 * 1024], BF16))
        c.arf = Arena(arf_t, 23 * 1024, "arf")
        c.arb = Arena(arb_t, 47 * 1024, "arb")
        c.banks = [E(nc.psum_tensor(f"bank{i}", [128, 512], F32)) for i in range(6)]
        c.pbb = [E(nc.psum_tensor(f"pbb{i}", [128, 1024], BF16)) for i in range(2)]

        sem_names = ("dg", "dy", "do", "dog", "ts", "tv", "st", "vt", "sv", "vs",
                     "gt", "gv", "vg", "sy", "tg")
        sems = {nm: E(nc.semaphore(nm)) for nm in sem_names}
        c.ed = {nm: Edge(sems[nm], 16 if nm in ("dg", "dy", "do", "dog") else 1)
                for nm in sem_names}

        block = E(nc.Block())
        c.sch = Sched()

        emit_all(c)
        c.sch.run(block)
    return c



def emit_all(c):
    sch, ed, arf, arb = c.sch, c.ed, c.arf, c.arb
    dt = c.dt

    c.ident_f = arf.alloc(128)
    c.ident_b = arb.alloc(128)
    c.ones_b = arb.alloc(128)
    c.ones_f = arf.alloc(128)
    c.magic = arf.alloc(8)
    c.winm1 = arf.alloc(256)
    c.diagm1 = arf.alloc(128)
    c.bout_rep = arf.alloc(1024)
    c.bo2_rep = arf.alloc(1024)
    c.g_ap = c.magic[:, 6:7]
    c.m64_ap = c.magic[:, 2:3]
    c.nm64_ap = c.magic[:, 3:4]

    def setup_g(g):
        g.memset(c.ident_f[:], 0.0)
        g.memset(c.ident_b[:], 0.0)
        g.memset(c.winm1[:], 0.0)
        g.memset(c.diagm1[:], 0.0)
        g.memset(c.ones_b[0:1, :], 1.0)
        g.memset(c.ones_f[0:1, :], 1.0)
        g.memset(c.magic[:, 0:1], 67108864.0)
        g.memset(c.magic[:, 4:5], 0.75)
        g.memset(c.magic[:, 7:8], 1e-5)
        g.drain()
        g.affine_select(out=c.ident_f[:], in_=c.ident_f[:], compare_op=OP.not_equal,
                        fill=1.0, base=0, pattern=[[-1, 128]], channel_multiplier=1)
        g.affine_select(out=c.ident_b[:], in_=c.ident_b[:], compare_op=OP.not_equal,
                        fill=1.0, base=0, pattern=[[-1, 128]], channel_multiplier=1)
        g.affine_select(out=c.winm1[:], in_=c.winm1[:], compare_op=OP.is_ge,
                        fill=-1.0, base=128, pattern=[[-1, 256]], channel_multiplier=1)
        return g.affine_select(out=c.diagm1[:], in_=c.diagm1[:], compare_op=OP.is_ge,
                               fill=-1.0, base=0, pattern=[[1, 128]], channel_multiplier=-1)

    sch.emit("g", setup_g, inc=ed["gv"])
    sch.wait("v", ed["gv"])

    def setup_v(v):
        v.tensor_mul(c.magic[:, 1:2], c.magic[:, 0:1], c.magic[:, 0:1])
        v.drain()
        v.tensor_scalar(c.magic[:, 2:3], c.magic[:, 1:2], 1.5, None, OP.mult)
        v.tensor_scalar(c.magic[:, 3:4], c.magic[:, 1:2], -1.5, None, OP.mult)
        v.drain()
        v.tensor_scalar(c.magic[:, 5:6], c.magic[:, 4:5], c.magic[:, 2:3], None, OP.add)
        v.drain()
        return v.tensor_scalar(c.magic[:, 6:7], c.magic[:, 5:6], c.magic[:, 3:4], None, OP.add)

    sch.emit("v", setup_v, writes=("magic",), inc=ed["vt"])
    c.magic_vt = ed["vt"].n

    c.brow = arf.alloc(1024)
    c.brow2 = arf.alloc(1024)
    sch.dma("y", lambda y: y.dma_start(out=c.brow[0:1, :], in_=dt["bout"]), ed["dy"])
    sch.dma("y", lambda y: y.dma_start(out=c.brow2[0:1, :], in_=dt["bo2"]), ed["dy"])
    sch.wait("t", ed["dy"])
    sch.wait("t", ed["gv"])

    def brep(t):
        t.matmul(c.banks[0][:, 0:512], c.ones_f[0:1, :], c.brow[0:1, 0:512],
                 start=True, stop=True)
        t.matmul(c.banks[1][:, 0:512], c.ones_f[0:1, :], c.brow[0:1, 512:1024],
                 start=True, stop=True)
        t.matmul(c.banks[2][:, 0:512], c.ones_f[0:1, :], c.brow2[0:1, 0:512],
                 start=True, stop=True)
        return t.matmul(c.banks[3][:, 0:512], c.ones_f[0:1, :], c.brow2[0:1, 512:1024],
                        start=True, stop=True)

    sch.emit("t", brep, inc=ed["ts"])
    sch.wait("s", ed["ts"])

    def brep_ev(s):
        s.copy(c.bout_rep[:, 0:512], c.banks[0][:, 0:512])
        s.copy(c.bout_rep[:, 512:1024], c.banks[1][:, 0:512])
        s.copy(c.bo2_rep[:, 0:512], c.banks[2][:, 0:512])
        return s.copy(c.bo2_rep[:, 512:1024], c.banks[3][:, 0:512])

    sch.emit("s", brep_ev, inc=ed["st"])
    c.bias_st = ed["st"].n
    sch.wait("t", ed["st"])

    c.amax_all = arf.alloc(16)

    if "p1" in c.stages:
        p1_l0_attn(c)
        sch.barrier(ed)
    if "p2" in c.stages:
        p2_l0_ffn(c)
        sch.barrier(ed)
    if "p3" in c.stages:
        p3_l1_attn(c)
        sch.barrier(ed)
    if "p4" in c.stages:
        p4_l1_moe(c)

    sch.wait("y", ed["do"])
    sch.wait("y", ed["dy"])
    sch.wait("g", ed["dog"])
    sch.wait("g", ed["dg"])


def emit_round(c, zin, rout, s1, s2, tag, extra_scale=None, inc=None):
    sch = c.sch
    sch.emit("v", lambda v: v.tensor_scalar(s1, zin, M32, -M32, OP.add, OP.add),
             reads=(tag + "z",), writes=(tag + "a",))
    sch.emit("v", lambda v: v.tensor_scalar(s2, zin, c.m64_ap, None, OP.add),
             reads=(tag + "z",), writes=(tag + "b",))
    sch.emit("v", lambda v: v.tensor_scalar(s2, s2, c.nm64_ap, None, OP.add),
             reads=(tag + "b",), writes=(tag + "c",))
    sch.emit("v", lambda v: v.tensor_sub(s2, s2, s1),
             reads=(tag + "c",), writes=(tag + "d",))
    if extra_scale is None:
        sch.emit("v", lambda v: v.scalar_tensor_tensor(rout, s2, c.g_ap, s1,
                                                       OP.mult, OP.add),
                 reads=(tag + "d",), writes=(tag + "o",), inc=inc)
    else:
        sch.emit("v", lambda v: v.scalar_tensor_tensor(s1, s2, c.g_ap, s1,
                                                       OP.mult, OP.add),
                 reads=(tag + "d",), writes=(tag + "e",))
        sch.emit("v", lambda v: v.tensor_scalar(rout, s1, extra_scale, None, OP.mult),
                 reads=(tag + "e",), writes=(tag + "o",), inc=inc)



def p1_l0_attn(c):
    _p1_body(c)


def _p1_body(c):
    sch, ed, arf, arb, banks, dt = c.sch, c.ed, c.arf, c.arb, c.banks, c.dt
    arf.push(); arb.push()

    wqkv = arb.alloc(KD * 2048)
    wout = arb.alloc(KD * 1024)
    xnTr = [arb.alloc(KD * 128), arb.alloc(KD * 128)]
    xq1r = [arb.alloc(KD * 128), arb.alloc(KD * 128)]
    qTr = [arb.alloc(KD * 128), arb.alloc(KD * 128)]
    kTr = [arb.alloc(KD * 128), arb.alloc(KD * 128)]
    ktok = [arb.alloc(1024), arb.alloc(1024)]
    Pw = [arb.alloc(256), arb.alloc(256)]
    PT = [arb.alloc(256), arb.alloc(256)]
    aoT = [arb.alloc(1024), arb.alloc(1024)]

    xin = [arf.alloc(1024), arf.alloc(1024)]
    lnt = [arf.alloc(1024), arf.alloc(1024)]
    xn = [arf.alloc(1024), arf.alloc(1024)]
    q1o = [arf.alloc(1024), arf.alloc(1024)]
    xin2 = [arf.alloc(1024), arf.alloc(1024)]
    x1t = [arf.alloc(1024), arf.alloc(1024)]
    sc = [arf.alloc(24), arf.alloc(24)]
    sqs = [arf.alloc(1024), arf.alloc(1024)]
    insc = [arf.alloc(256), arf.alloc(256)]
    qs1 = arf.alloc(1024)
    qs2 = arf.alloc(1024)
    qs3 = arf.alloc(1024)

    for k in range(KD):
        sch.dma("g", lambda g, k=k: g.dma_start(
            out=wqkv[:, k * 2048:(k + 1) * 2048],
            in_=dt["wqkv_qk"][128 * k:128 * (k + 1), :]), ed["dg"])
        sch.dma("g", lambda g, k=k: g.dma_start(
            out=wout[:, k * 1024:(k + 1) * 1024],
            in_=dt["wout_r"][128 * k:128 * (k + 1), :]), ed["dg"])
    wload_dg = ed["dg"].n
    sch.wait("v", ed["vt"], c.magic_vt)
    sch.wait("t", ed["dg"], wload_dg)

    xntap = c.dbg_tap("xn0", (S, D))
    x1tap = c.dbg_tap("x1", (S, D))
    qtap = c.dbg_tap("qT", (D, S))
    ktap = c.dbg_tap("kT", (D, S))

    ln_done_vt = [0] * NT
    x1_vt = [0] * NT
    store_dy = [0] * NT
    xq1_dma_dy = [0] * NT
    stt_vt = 0
    exp_sv = 0
    ptr_ts = 0
    prev_vs = [0, 0]
    for w in range(NT):
        p = w % 2
        jlo = 0 if w > 0 else 1
        ncols = 256 - 128 * jlo
        scol = 128 * jlo

        if w >= 2:
            sch.wait("y", ed["vt"], ln_done_vt[w - 2])
        sch.dma("y", lambda y, w=w, p=p: y.dma_start(
            out=xin[p][:], in_=dt["x"][128 * w:128 * (w + 1), :]), ed["dy"])
        ld = ed["dy"].n
        sch.wait("s", ed["dy"], ld)
        if w >= 2:
            sch.wait("s", ed["vs"], prev_vs[p])
        sch.emit("s", lambda s, p=p: s.activation(
            sqs[p][:], xin[p][:], AF.Square, accum_out=sc[p][:, 1:2]),
            writes=(f"ssq{p}",), inc=ed["sv"])
        sq_sv = ed["sv"].n
        sch.wait("v", ed["dy"], ld)
        sch.emit("v", lambda v, p=p: v.reduce_sum(sc[p][:, 0:1], xin[p][:], AX),
                 writes=(f"sc{p}",))
        sch.wait("v", ed["sv"], sq_sv)
        sch.emit("v", lambda v, p=p: v.tensor_scalar(
            sc[p][:, 2:3], sc[p][:, 0:1], 1.0 / D, None, OP.mult),
            reads=(f"sc{p}",), writes=(f"sc{p}",))
        sch.emit("v", lambda v, p=p: v.tensor_scalar(
            sc[p][:, 3:4], sc[p][:, 1:2], 1.0 / D, None, OP.mult))
        sch.emit("v", lambda v, p=p: v.tensor_mul(
            sc[p][:, 4:5], sc[p][:, 2:3], sc[p][:, 2:3]),
            reads=(f"sc{p}",), writes=(f"sc{p}",))
        sch.emit("v", lambda v, p=p: v.tensor_sub(
            sc[p][:, 5:6], sc[p][:, 3:4], sc[p][:, 4:5]),
            reads=(f"sc{p}",), writes=(f"sc{p}",), inc=ed["vs"])
        vs_here = ed["vs"].n
        sch.wait("s", ed["vs"], vs_here)
        sch.emit("s", lambda s, p=p: s.activation(
            sc[p][:, 6:7], sc[p][:, 5:6], AF.Sqrt, bias=c.magic[:, 7:8]),
            writes=(f"ssc{p}",), inc=ed["sv"])
        sch.wait("v", ed["sv"])
        sch.emit("v", lambda v, p=p: v.reciprocal(sc[p][:, 7:8], sc[p][:, 6:7]),
                 reads=(f"sc{p}",), writes=(f"sc{p}",))
        sch.emit("v", lambda v, p=p: v.tensor_mul(
            sc[p][:, 8:9], sc[p][:, 2:3], sc[p][:, 7:8]),
            reads=(f"sc{p}",), writes=(f"sc{p}",))
        sch.emit("v", lambda v, p=p: v.tensor_scalar(
            lnt[p][:], xin[p][:], sc[p][:, 7:8], None, OP.mult),
            reads=(f"sc{p}",), writes=(f"lnt{p}",))
        sch.emit("v", lambda v, p=p: v.tensor_scalar(
            xn[p][:], lnt[p][:], sc[p][:, 8:9], None, OP.subtract),
            reads=(f"lnt{p}",), writes=(f"xn{p}",))
        sch.emit("v", lambda v, p=p, w=w: v.reduce_max(
            c.amax_all[:, w:w + 1], xn[p][:], AX, apply_absolute_value=True),
            reads=(f"xn{p}",), writes=("amax",))
        sch.emit("v", lambda v, p=p, w=w: v.tensor_scalar(
            sc[p][:, 9:10], c.amax_all[:, w:w + 1], 1e-5, None, OP.max),
            reads=("amax",), writes=(f"sc{p}",))
        sch.emit("v", lambda v, p=p: v.reciprocal(sc[p][:, 10:11], sc[p][:, 9:10]),
                 reads=(f"sc{p}",), writes=(f"sc{p}",))
        sch.emit("v", lambda v, p=p: v.tensor_scalar(
            sc[p][:, 11:12], sc[p][:, 10:11], 127.0, None, OP.mult),
            reads=(f"sc{p}",), writes=(f"sc{p}",))
        sch.emit("v", lambda v, p=p: v.tensor_scalar(
            qs1[:], xn[p][:], sc[p][:, 11:12], None, OP.mult),
            reads=(f"sc{p}", f"xn{p}", "ro"), writes=("rz",))
        emit_round(c, qs1[:], q1o[p][:], qs2[:], qs3[:], "r", inc=ed["vt"])
        ln_done_vt[w] = ed["vt"].n
        prev_vs[p] = vs_here
        if xntap is not None:
            sch.wait("y", ed["vt"], ln_done_vt[w])
            sch.dma("y", lambda y, w=w, p=p: y.dma_start(
                out=xntap[128 * w:128 * (w + 1), :], in_=xn[p][:]), ed["do"])

        sch.wait("t", ed["vt"], ln_done_vt[w])
        if w >= 1:
            sch.wait("t", ed["vt"], x1_vt[w - 1])
        if w >= 2:
            sch.wait("t", ed["dy"], xq1_dma_dy[w - 2])
        for kk in range(KD):
            bsel = banks[4 + (kk % 2)]

            def tr(tt, p=p, kk=kk, bsel=bsel):
                tt.transpose(bsel[:, 0:128], xn[p][:, 128 * kk:128 * (kk + 1)],
                             c.ident_f[:])
                return tt.transpose(bsel[:, 128:256],
                                    q1o[p][:, 128 * kk:128 * (kk + 1)], c.ident_f[:])
            sch.emit("t", tr, inc=ed["ts"])
            sch.wait("s", ed["ts"])

            def ev(s, p=p, kk=kk, bsel=bsel):
                s.copy(xnTr[p][:, 128 * kk:128 * (kk + 1)], bsel[:, 0:128])
                return s.copy(xq1r[p][:, 128 * kk:128 * (kk + 1)], bsel[:, 128:256])
            sch.emit("s", ev, inc=ed["st"])
            sch.wait("t", ed["st"])
        tr_st = ed["st"].n
        sch.wait("y", ed["st"], tr_st)
        sch.dma("y", lambda y, w=w, p=p: y.dma_start(
            out=dt["xq1d"][:, :, 128 * w:128 * (w + 1)].transpose([1, 0, 2]),
            in_=xq1r[p][:].rearrange("p (k c) -> p k c", k=KD)), ed["dy"])
        xq1_dma_dy[w] = ed["dy"].n

        def qkmm(t, p=p):
            r = None
            for which, base_col in ((0, 0), (2, 1024)):
                for m in range(KD):
                    bk = banks[which + m // 4]
                    cl = 128 * (m % 4)
                    for k in range(KD):
                        r = t.matmul(bk[:, cl:cl + 128],
                                     wqkv[:, k * 2048 + base_col + 128 * m:
                                          k * 2048 + base_col + 128 * (m + 1)],
                                     xnTr[p][:, 128 * k:128 * (k + 1)],
                                     start=(k == 0), stop=(k == KD - 1))
            return r
        sch.emit("t", qkmm, inc=ed["ts"])
        sch.wait("s", ed["ts"])
        if qtap is not None:
            sch.wait("s", ed["dog"])

        def qkev(s, p=p):
            s.copy(qTr[p][:, 0:512], banks[0][:, 0:512])
            s.copy(qTr[p][:, 512:1024], banks[1][:, 0:512])
            s.copy(kTr[p][:, 0:512], banks[2][:, 0:512])
            return s.copy(kTr[p][:, 512:1024], banks[3][:, 0:512])
        sch.emit("s", qkev, inc=ed["st"])
        qk_st = ed["st"].n
        sch.wait("t", ed["st"], qk_st)
        if qtap is not None:
            sch.wait("g", ed["st"], qk_st)
            for kk in range(KD):
                sch.dma("g", lambda g, kk=kk, w=w, p=p: g.dma_start(
                    out=qtap[128 * kk:128 * (kk + 1), 128 * w:128 * (w + 1)],
                    in_=qTr[p][:, 128 * kk:128 * (kk + 1)]), ed["dog"])
                sch.dma("g", lambda g, kk=kk, w=w, p=p: g.dma_start(
                    out=ktap[128 * kk:128 * (kk + 1), 128 * w:128 * (w + 1)],
                    in_=kTr[p][:, 128 * kk:128 * (kk + 1)]), ed["dog"])

        def ktr(t, p=p):
            r = None
            for kk in range(KD):
                r = t.transpose(c.pbb[0][:, 128 * kk:128 * (kk + 1)],
                                kTr[p][:, 128 * kk:128 * (kk + 1)], c.ident_b[:])
            return r
        sch.emit("t", ktr, inc=ed["ts"])
        sch.wait("s", ed["ts"])
        sch.emit("s", lambda s, p=p: s.copy(ktok[p][:], c.pbb[0][:]), inc=ed["st"])
        sch.wait("t", ed["st"])

        sch.wait("t", ed["vt"], stt_vt)

        def sco(t, w=w, p=p, jlo=jlo, ncols=ncols, scol=scol):
            r = None
            if jlo == 0:
                for k in range(KD):
                    r = t.matmul(banks[0][:, 0:128],
                                 qTr[p][:, 128 * k:128 * (k + 1)],
                                 kTr[1 - p][:, 128 * k:128 * (k + 1)],
                                 start=(k == 0), stop=(k == KD - 1))
            for k in range(KD):
                r = t.matmul(banks[0][:, 128:256],
                             qTr[p][:, 128 * k:128 * (k + 1)],
                             kTr[p][:, 128 * k:128 * (k + 1)],
                             start=(k == 0), stop=(k == KD - 1))
            return r
        sch.emit("t", sco, inc=ed["tv"])
        sch.wait("v", ed["tv"])
        sch.wait("v", ed["sv"], exp_sv)
        sch.emit("v", lambda v, p=p, scol=scol, ncols=ncols: v.scalar_tensor_tensor(
            insc[p][:, scol:scol + ncols], c.winm1[:, scol:scol + ncols], MASK_SHIFT,
            banks[0][:, scol:scol + ncols], OP.mult, OP.add),
            writes=(f"insc{p}",), inc=ed["vt"])
        stt_vt = ed["vt"].n
        sch.emit("v", lambda v, p=p, scol=scol, ncols=ncols: v.reduce_max(
            sc[p][:, 12:13], insc[p][:, scol:scol + ncols], AX, negate=True),
            reads=(f"insc{p}",), writes=(f"wn{p}",))
        if w == 0:
            sch.emit("v", lambda v, p=p: v.tensor_scalar(
                sc[p][:, 12:13], sc[p][:, 12:13], 0.0, None, OP.min),
                reads=(f"wn{p}",), writes=(f"wn{p}",))
        sch.emit("v", lambda v, p=p: v.tensor_scalar(
            sc[p][:, 13:14], sc[p][:, 12:13], 1.0 / 32.0, None, OP.mult),
            reads=(f"wn{p}",), writes=(f"wn{p}",), inc=ed["vs"])
        sch.wait("s", ed["vs"])
        sch.wait("s", ed["ts"], ptr_ts)
        sch.emit("s", lambda s, p=p, scol=scol, ncols=ncols: s.activation(
            Pw[p][:, scol:scol + ncols], insc[p][:, scol:scol + ncols], AF.Exp,
            scale=1.0 / 32.0, bias=sc[p][:, 13:14], accum_out=sc[p][:, 14:15]),
            writes=(f"pw{p}",), inc=ed["sv"])
        exp_sv = ed["sv"].n
        sch.wait("v", ed["sv"], exp_sv)
        if w == 0:
            sch.emit("s", lambda s, p=p: s.activation(
                sc[p][:, 15:16], sc[p][:, 13:14], AF.Exp),
                writes=(f"spad{p}",), inc=ed["sv"])
            exp_sv = ed["sv"].n
            sch.wait("v", ed["sv"], exp_sv)
            sch.emit("v", lambda v, p=p: v.scalar_tensor_tensor(
                sc[p][:, 16:17], sc[p][:, 15:16], 128.0, sc[p][:, 14:15],
                OP.mult, OP.add), writes=(f"wd{p}",))
            den = 16
        else:
            den = 14
        sch.emit("v", lambda v, p=p, den=den: v.reciprocal(
            sc[p][:, 17:18], sc[p][:, den:den + 1]),
            reads=(f"wd{p}",), writes=(f"wd{p}",))
        sch.emit("v", lambda v, p=p, scol=scol, ncols=ncols: v.tensor_scalar(
            Pw[p][:, scol:scol + ncols], Pw[p][:, scol:scol + ncols],
            sc[p][:, 17:18], None, OP.mult),
            reads=(f"wd{p}", f"pw{p}"), writes=(f"pw{p}",), inc=ed["vt"])
        pn_vt = ed["vt"].n

        sch.wait("t", ed["vt"], pn_vt)

        def ptr(t, p=p, jlo=jlo):
            r = None
            for jh in range(jlo, 2):
                r = t.transpose(c.pbb[1][:, 128 * jh:128 * (jh + 1)],
                                Pw[p][:, 128 * jh:128 * (jh + 1)], c.ident_b[:])
            return r
        sch.emit("t", ptr, inc=ed["ts"])
        ptr_ts = ed["ts"].n
        sch.wait("s", ed["ts"], ptr_ts)

        def pev(s, p=p, jlo=jlo):
            r = None
            for jh in range(jlo, 2):
                r = s.copy(PT[p][:, 128 * jh:128 * (jh + 1)],
                           c.pbb[1][:, 128 * jh:128 * (jh + 1)])
            return r
        sch.emit("s", pev, inc=ed["st"])
        sch.wait("t", ed["st"])

        def pv(t, w=w, p=p, jlo=jlo):
            r = None
            for dc in range(KD):
                bk = banks[2 + dc // 4]
                cl = 128 * (dc % 4)
                for jh in range(jlo, 2):
                    slot = ktok[(w - 1 + jh) % 2]
                    r = t.matmul(bk[:, cl:cl + 128],
                                 slot[:, 128 * dc:128 * (dc + 1)],
                                 PT[p][:, 128 * jh:128 * (jh + 1)],
                                 start=(jh == jlo), stop=(jh == 1))
            return r
        sch.emit("t", pv, inc=ed["ts"])
        sch.wait("s", ed["ts"])

        def aoev(s, p=p):
            s.copy(aoT[p][:, 0:512], banks[2][:, 0:512])
            return s.copy(aoT[p][:, 512:1024], banks[3][:, 0:512])
        sch.emit("s", aoev, inc=ed["st"])
        sch.wait("t", ed["st"])

        if w >= 2:
            sch.wait("y", ed["vt"], x1_vt[w - 2])
        sch.dma("y", lambda y, w=w, p=p: y.dma_start(
            out=xin2[p][:], in_=dt["x"][128 * w:128 * (w + 1), :]), ed["dy"])
        xre_dy = ed["dy"].n

        def oproj(t, p=p):
            r = None
            for nn in range(2):
                for k in range(KD):
                    r = t.matmul(banks[4 + nn][:, 0:512],
                                 aoT[p][:, 128 * k:128 * (k + 1)],
                                 wout[:, k * 1024 + 512 * nn: k * 1024 + 512 * (nn + 1)],
                                 start=(k == 0), stop=(k == KD - 1))
            return r
        sch.emit("t", oproj, inc=ed["tv"])
        sch.wait("v", ed["tv"])
        sch.wait("v", ed["dy"], xre_dy)
        sch.wait("v", ed["st"], c.bias_st)
        if w >= 2:
            sch.wait("v", ed["dy"], store_dy[w - 2])
        for nn in range(2):
            sch.emit("v", lambda v, nn=nn, p=p: v.tensor_add(
                x1t[p][:, 512 * nn:512 * (nn + 1)], banks[4 + nn][:, 0:512],
                c.bout_rep[:, 512 * nn:512 * (nn + 1)]),
                writes=(f"x1t{p}",))
        sch.emit("v", lambda v, p=p: v.tensor_add(
            x1t[p][:], x1t[p][:], xin2[p][:]),
            reads=(f"x1t{p}",), writes=(f"x1t{p}",), inc=ed["vt"])
        x1_vt[w] = ed["vt"].n
        sch.wait("y", ed["vt"], x1_vt[w])
        sch.dma("y", lambda y, w=w, p=p: y.dma_start(
            out=dt["xs1"][128 * w:128 * (w + 1), :], in_=x1t[p][:]), ed["dy"])
        store_dy[w] = ed["dy"].n
        if x1tap is not None:
            sch.dma("y", lambda y, w=w, p=p: y.dma_start(
                out=x1tap[128 * w:128 * (w + 1), :], in_=x1t[p][:]), ed["do"])
        sch.wait("t", ed["vt"], x1_vt[w])

    c.p1_store_dy = store_dy
    arf.pop()
    arb.pop()




def p2_l0_ffn(c):
    sch, ed, arf, arb, banks, dt = c.sch, c.ed, c.arf, c.arb, c.banks, c.dt
    NSTRIP = 8
    SW = 256
    arf.push(); arb.push()
    scl = arf.alloc(16)
    amax_c = arf.alloc(16)
    c.onescol = arf.alloc(1)
    r127 = arf.alloc(2)
    convw_sb = arf.alloc(KH * 3)
    convb_sb = arf.alloc(KH)
    sch.emit("g", lambda g: (g.memset(c.onescol[:], 1.0),
                             g.memset(r127[:, 0:1], 127.0))[-1], inc=ed["gt"])
    sch.wait("t", ed["gt"])
    sch.wait("v", ed["vt"], c.magic_vt)
    sch.wait("v", ed["gt"])
    sch.emit("v", lambda v: v.reciprocal(r127[:, 1:2], r127[:, 0:1]),
             writes=("r127",))
    sch.dma("g", lambda g: g.dma_start(out=convw_sb[:], in_=dt["convw"]), ed["dg"])
    sch.dma("g", lambda g: g.dma_start(out=convb_sb[:], in_=dt["convb"]), ed["dg"])

    arf.push(); arb.push()
    wbuf = arf.alloc(4096)
    qzW = arf.alloc(4096)
    rsW1 = arf.alloc(4096)
    rsW2 = arf.alloc(4096)
    wqo = [arb.alloc(4096), arb.alloc(4096)]

    def wquant(src_dram, nchunks, width, out_dram, mean_col, tag):
        rd_vt = 0
        for kk in range(nchunks):
            sch.wait("y", ed["vt"], rd_vt)
            sch.dma("y", lambda y, kk=kk: y.dma_start(
                out=wbuf[:, 0:width],
                in_=src_dram[128 * kk:128 * (kk + 1), :]), ed["dy"])
            sch.wait("v", ed["dy"])
            sch.emit("v", lambda v: v.reduce_sum(
                scl[:, 8:9], wbuf[:, 0:width], AX, apply_absolute_value=True),
                writes=(tag + "r",))
            if kk == 0:
                sch.emit("v", lambda v: v.tensor_copy(scl[:, 10:11], scl[:, 8:9]),
                         reads=(tag + "r",), writes=(tag + "a",), inc=ed["vt"])
            else:
                sch.emit("v", lambda v: v.tensor_add(
                    scl[:, 10:11], scl[:, 10:11], scl[:, 8:9]),
                    reads=(tag + "r", tag + "a"), writes=(tag + "a",), inc=ed["vt"])
            rd_vt = ed["vt"].n
        sch.wait("t", ed["vt"])
        sch.emit("t", lambda t: t.matmul(banks[2][0:1, 0:1], c.onescol[:],
                                         scl[:, 10:11], start=True, stop=True),
                 inc=ed["tv"])
        sch.wait("v", ed["tv"])
        sch.emit("v", lambda v: v.tensor_scalar(
            scl[0:1, 0:1], banks[2][0:1, 0:1], 1.0 / 4194304.0, 1e-5,
            OP.mult, OP.max), writes=(tag + "m",))
        sch.emit("v", lambda v: v.reciprocal(scl[0:1, 1:2], scl[0:1, 0:1]),
                 reads=(tag + "m",), writes=(tag + "m",))
        sch.emit("v", lambda v: v.tensor_scalar(
            scl[0:1, 2:3], scl[0:1, 0:1], r127[0:1, 1:2], None, OP.mult),
            reads=(tag + "m", "r127"), writes=(tag + "m",), inc=ed["vt"])
        sch.wait("t", ed["vt"])
        sch.emit("t", lambda t: t.matmul(banks[2][:, 0:2], c.ones_f[0:1, :],
                                         scl[0:1, 1:3], start=True, stop=True),
                 inc=ed["ts"])
        sch.wait("s", ed["ts"])
        sch.emit("s", lambda s: s.copy(scl[:, mean_col:mean_col + 2],
                                       banks[2][:, 0:2]),
                 writes=(tag + "sr",), inc=ed["sv"])
        sch.wait("v", ed["sv"])
        ts_vt = ed["vt"].n
        out_dg = [0, 0]
        for kk in range(nchunks):
            p = kk % 2
            sch.wait("y", ed["vt"], ts_vt)
            sch.dma("y", lambda y, kk=kk: y.dma_start(
                out=wbuf[:, 0:width],
                in_=src_dram[128 * kk:128 * (kk + 1), :]), ed["dy"])
            sch.wait("v", ed["dy"])
            sch.emit("v", lambda v: v.tensor_scalar(
                qzW[:, 0:width], wbuf[:, 0:width],
                scl[:, mean_col:mean_col + 1],
                None, OP.mult),
                reads=(tag + "c",),
                writes=(tag + "z",), inc=ed["vt"])
            ts_vt = ed["vt"].n
            emit_round(c, qzW[:, 0:width], qzW[:, 0:width],
                       rsW1[:, 0:width], rsW2[:, 0:width], tag)
            sch.wait("v", ed["dog"], out_dg[p])
            sch.emit("v", lambda v, p=p: v.tensor_scalar(
                wqo[p][:, 0:width], qzW[:, 0:width], 1.0, -1.0, OP.min, OP.max),
                reads=(tag + "o",), writes=(tag + "c",), inc=ed["vt"])
            sch.wait("g", ed["vt"])
            sch.dma("g", lambda g, kk=kk, p=p: g.dma_start(
                out=out_dram[kk], in_=wqo[p][:, 0:width]), ed["dog"])
            out_dg[p] = ed["dog"].n

    wquant(dt["w1T"], KD, DH, dt["w1qd"], 4, "w1")
    wquant(dt["w2T"], KH, D, dt["w2qd"], 6, "w2")
    arf.pop(); arb.pop()
    sch.barrier(ed)

    arf.push(); arb.push()
    w1q = arb.alloc(KD * DH)
    xqb = [arb.alloc(KD * 258), arb.alloc(KD * 258)]
    xq2b = arb.alloc(KH * SW)
    c_sb = arf.alloc(KH * SW)
    hs = [arf.alloc(258), arf.alloc(258)]
    cv = [arf.alloc(256), arf.alloc(256)]
    cmax = arf.alloc(256)
    cmin = arf.alloc(256)
    cmaxT = arf.alloc(256)
    reps = arf.alloc(512)
    arow = arf.alloc(2050)
    arowr = arf.alloc(2050)
    rows4 = arf.alloc(512)
    rowasm = arf.alloc(512)
    rs1 = arf.alloc(256)
    rs2 = arf.alloc(256)
    qz = arf.alloc(256)
    xq2i = arf.alloc(256)
    a1rep = arf.alloc(258)

    sch.emit("g", lambda g: g.memset(arow[0:1, :], 1.0), inc=ed["gt"])
    sch.wait("t", ed["gt"])
    sch.emit("v", lambda v: v.tensor_scalar(
        amax_c[:], c.amax_all[:], 1e-5, None, OP.max),
        writes=("amc",), inc=ed["vt"])
    sch.wait("t", ed["vt"])
    sch.emit("t", lambda t: t.transpose(banks[3][0:16, 0:128], amax_c[:],
                                        c.ident_f[:]), inc=ed["ts"])
    sch.wait("s", ed["ts"])
    sch.emit("s", lambda s: s.copy(rows4[0:16, 0:128], banks[3][0:16, 0:128]),
             writes=("r4",), inc=ed["sv"])
    sch.wait("y", ed["sv"])
    sch.dma("y", lambda y: y.dma_start(
        out=arowr[0:1, 1:2049].rearrange("o (p c) -> o p c", p=NT),
        in_=rows4[0:16, 0:128]), ed["dy"])
    sch.wait("v", ed["dy"])
    sch.wait("v", ed["gt"])
    sch.emit("v", lambda v: v.tensor_scalar(
        arow[0:1, 1:2049], arowr[0:1, 1:2049], scl[0:1, 5:6], None, OP.mult),
        reads=("arowr",), writes=("arow",), inc=ed["vt"])
    arow_vt = ed["vt"].n

    for kk in range(KD):
        sch.dma("g", lambda g, kk=kk: g.dma_start(
            out=w1q[:, DH * kk:DH * (kk + 1)], in_=dt["w1qd"][kk]), ed["dg"])
    w1q_dg = ed["dg"].n
    sch.wait("t", ed["dg"], w1q_dg)
    sch.wait("t", ed["vt"], arow_vt)

    xqb_free_tv = [0, 0]
    bank_free_vt = [0, 0]
    xq2dma_dy = 0
    q2_vt = 0
    for s in range(NSTRIP):
        sp = s % 2
        t0 = SW * s
        sch.wait("y", ed["tv"], xqb_free_tv[sp])
        if s == 0:
            def z0(g):
                r = None
                for kk in range(KD):
                    r = g.memset(xqb[0][:, 258 * kk:258 * kk + 1], 0.0)
                return r
            sch.emit("g", z0, inc=ed["gt"])
            sch.wait("y", ed["gt"])
            sch.dma("y", lambda y: y.dma_start(
                out=xqb[0][:].rearrange("p (k c) -> p k c", k=KD)[:, :, 1:258],
                in_=dt["xq1d"][:, :, 0:257].transpose([1, 0, 2])), ed["dy"])
        elif s == NSTRIP - 1:
            sch.wait("g", ed["tv"], xqb_free_tv[sp])

            def z7(g, sp=sp):
                r = None
                for kk in range(KD):
                    r = g.memset(xqb[sp][:, 258 * kk + 257:258 * kk + 258], 0.0)
                return r
            sch.emit("g", z7, inc=ed["gt"])
            sch.wait("y", ed["gt"])
            sch.dma("y", lambda y, sp=sp, t0=t0: y.dma_start(
                out=xqb[sp][:].rearrange("p (k c) -> p k c", k=KD)[:, :, 0:257],
                in_=dt["xq1d"][:, :, t0 - 1:S].transpose([1, 0, 2])), ed["dy"])
        else:
            sch.dma("y", lambda y, sp=sp, t0=t0: y.dma_start(
                out=xqb[sp][:].rearrange("p (k c) -> p k c", k=KD),
                in_=dt["xq1d"][:, :, t0 - 1:t0 + 257].transpose([1, 0, 2])), ed["dy"])
        xqld = ed["dy"].n
        sch.wait("t", ed["dy"], xqld)
        if s > 0:
            sch.wait("s", ed["vt"], q2_vt)

        def arep_mm(t, t0=t0):
            return t.matmul(banks[2][:, 0:258], c.ones_f[0:1, :],
                            arow[0:1, t0:t0 + 258], start=True, stop=True)
        sch.emit("t", arep_mm, inc=ed["ts"])
        sch.wait("s", ed["ts"])
        sch.emit("s", lambda s: s.copy(a1rep[:], banks[2][:, 0:258]),
                 writes=("a1rep",), inc=ed["st"])
        sch.wait("v", ed["st"])
        sch.wait("t", ed["st"])

        for hc in range(KH):
            hp = hc % 2
            sch.wait("t", ed["vt"], bank_free_vt[hp])

            def w1mm(t, hc=hc, hp=hp, sp=sp):
                r = None
                for k in range(KD):
                    r = t.matmul(banks[hp][:, 0:258],
                                 w1q[:, DH * k + 128 * hc: DH * k + 128 * (hc + 1)],
                                 xqb[sp][:, 258 * k:258 * (k + 1)],
                                 start=(k == 0), stop=(k == KD - 1))
                return r
            sch.emit("t", w1mm, inc=ed["tv"])
            sch.wait("v", ed["tv"])
            sch.emit("v", lambda v, hp=hp: v.tensor_mul(
                hs[hp][:], banks[hp][:, 0:258], a1rep[:]),
                writes=(f"hs{hp}",), inc=ed["vt"])
            bank_free_vt[hp] = ed["vt"].n
            sch.emit("v", lambda v, hp=hp, hc=hc: v.tensor_scalar(
                cv[0][:], hs[hp][:, 0:256], convw_sb[:, 3 * hc:3 * hc + 1], None,
                OP.mult), reads=(f"hs{hp}",), writes=("cv0",))
            sch.emit("v", lambda v, hp=hp, hc=hc: v.scalar_tensor_tensor(
                cv[1][:], hs[hp][:, 1:257], convw_sb[:, 3 * hc + 1:3 * hc + 2],
                cv[0][:], OP.mult, OP.add), reads=("cv0",), writes=("cv1",))
            sch.emit("v", lambda v, hp=hp, hc=hc: v.scalar_tensor_tensor(
                cv[0][:], hs[hp][:, 2:258], convw_sb[:, 3 * hc + 2:3 * hc + 3],
                cv[1][:], OP.mult, OP.add), reads=("cv1",), writes=("cv0b",),
                inc=ed["vs"])
            sch.wait("s", ed["vs"])
            sch.emit("s", lambda s, hc=hc: s.activation(
                c_sb[:, SW * hc:SW * (hc + 1)], cv[0][:], AF.Silu,
                bias=convb_sb[:, hc:hc + 1]), writes=("csb",), inc=ed["sv"])
            sch.wait("v", ed["sv"])
            if hc == 0:
                sch.emit("v", lambda v, hc=hc: v.tensor_copy(
                    cmax[:], c_sb[:, SW * hc:SW * (hc + 1)]), writes=("cmax",))
                sch.emit("v", lambda v, hc=hc: v.tensor_copy(
                    cmin[:], c_sb[:, SW * hc:SW * (hc + 1)]), writes=("cmin",))
            else:
                sch.emit("v", lambda v, hc=hc: v.tensor_tensor(
                    cmax[:], cmax[:], c_sb[:, SW * hc:SW * (hc + 1)], OP.max),
                    reads=("cmax",), writes=("cmax",))
                sch.emit("v", lambda v, hc=hc: v.tensor_tensor(
                    cmin[:], cmin[:], c_sb[:, SW * hc:SW * (hc + 1)], OP.min),
                    reads=("cmin",), writes=("cmin",))
        w1mm_tv_last = ed["tv"].n
        xqb_free_tv[sp] = w1mm_tv_last
        sch.emit("v", lambda v: v.tensor_copy(scl[0:1, 15:16], cmax[0:1, 0:1]),
                 reads=("cmax",), writes=("cmaxd",), inc=ed["vt"])
        cmax_vt = ed["vt"].n

        sch.wait("t", ed["vt"], cmax_vt)

        def trc(t):
            t.transpose(banks[3][:, 0:128], cmax[:, 0:128], c.ident_f[:])
            return t.transpose(banks[3][:, 128:256], cmax[:, 128:256], c.ident_f[:])
        sch.emit("t", trc, inc=ed["ts"])
        sch.wait("s", ed["ts"])
        sch.emit("s", lambda s: s.copy(cmaxT[:], banks[3][:, 0:256]),
                 writes=("cmT",), inc=ed["sv"])
        sch.wait("v", ed["sv"])
        for j in range(2):
            sch.emit("v", lambda v, j=j: v.reduce_max(
                amax_c[:, j:j + 1], cmaxT[:, 128 * j:128 * (j + 1)], AX),
                reads=("cmT",), writes=("am2",))
        sch.emit("v", lambda v: v.tensor_scalar(
            amax_c[:, 2:4], amax_c[:, 0:2], 1e-5, None, OP.max),
            reads=("am2",), writes=("am2c",))
        sch.emit("v", lambda v: v.reciprocal(amax_c[:, 4:6], amax_c[:, 2:4]),
                 reads=("am2c",), writes=("am2r",))
        sch.emit("v", lambda v: v.tensor_scalar(
            amax_c[:, 6:8], amax_c[:, 4:6], 127.0, None, OP.mult),
            reads=("am2r",), writes=("as2",))
        sch.emit("v", lambda v: v.tensor_scalar(
            amax_c[:, 8:10], amax_c[:, 2:4], scl[:, 7:8], None, OP.mult),
            reads=("am2c",), writes=("al2",), inc=ed["vt"])
        sc2_vt = ed["vt"].n
        sch.wait("t", ed["vt"], sc2_vt)
        sch.emit("t", lambda t: t.transpose(banks[3][0:4, 0:128], amax_c[:, 6:10],
                                            c.ident_f[:]), inc=ed["ts"])
        sch.wait("s", ed["ts"])
        sch.emit("s", lambda s: s.copy(rows4[0:4, 0:128], banks[3][0:4, 0:128]),
                 writes=("r4b",), inc=ed["sv"])
        sch.wait("y", ed["sv"])
        sch.dma("y", lambda y: y.dma_start(
            out=rowasm[0:1, 0:512].rearrange("o (p c) -> o p c", p=4),
            in_=rows4[0:4, 0:128]), ed["dy"])
        rw_dy = ed["dy"].n
        sch.wait("t", ed["dy"], rw_dy)
        sch.emit("t", lambda t: t.matmul(banks[2][:, 0:512], c.ones_f[0:1, :],
                                         rowasm[0:1, 0:512], start=True, stop=True),
                 inc=ed["ts"])
        sch.wait("s", ed["ts"])
        sch.emit("s", lambda s: s.copy(reps[:], banks[2][:, 0:512]),
                 writes=("reps",), inc=ed["sv"])
        sch.wait("v", ed["sv"])

        sch.wait("v", ed["dy"], xq2dma_dy)
        for hc in range(KH):
            sch.emit("v", lambda v, hc=hc: v.tensor_mul(
                qz[:], c_sb[:, SW * hc:SW * (hc + 1)], reps[:, 0:256]),
                reads=("reps", "csb", "ro"), writes=("rz",))
            emit_round(c, qz[:], xq2i[:], rs1[:], rs2[:], "r")
            sch.emit("v", lambda v, hc=hc: v.tensor_mul(
                xq2b[:, SW * hc:SW * (hc + 1)], xq2i[:], reps[:, 256:512]),
                reads=("ro",), writes=("xq2b",))
        sch.emit("v", lambda v: v.tensor_copy(scl[0:1, 15:16], xq2b[0:1, 0:1]),
                 reads=("xq2b",), writes=("xq2b",), inc=ed["vt"])
        q2_vt = ed["vt"].n
        sch.wait("y", ed["vt"], q2_vt)
        sch.dma("y", lambda y, s=s: y.dma_start(out=dt["xq2d"][s], in_=xq2b[:]),
                ed["dy"])
        xq2dma_dy = ed["dy"].n
    arf.pop(); arb.pop()
    sch.barrier(ed)

    arf.push(); arb.push()
    w2q = arb.alloc(KH * D)
    xq2l = arb.alloc(KH * SW)
    x2a = [arf.alloc(1024), arf.alloc(1024)]
    x2b = [arf.alloc(1024), arf.alloc(1024)]
    scl2 = arf.alloc(4)
    xq2l_free_tv = 0
    p2b_store_dy = [0, 0]
    for kk in range(KH):
        sch.dma("g", lambda g, kk=kk: g.dma_start(
            out=w2q[:, D * kk:D * (kk + 1)], in_=dt["w2qd"][kk]), ed["dg"])
    w2q_dg = ed["dg"].n
    sch.wait("t", ed["dg"], w2q_dg)
    x2tap = c.dbg_tap("x2", (S, D))
    x2_vt = [0] * NT
    for s in range(NSTRIP):
        sch.wait("y", ed["tv"], xq2l_free_tv)
        sch.dma("y", lambda y, s=s: y.dma_start(out=xq2l[:], in_=dt["xq2d"][s]),
                ed["dy"])
        ld = ed["dy"].n
        sch.wait("t", ed["dy"], ld)
        for mh in range(2):
            ti = 2 * s + mh
            p = ti % 2
            if ti >= 2:
                sch.wait("y", ed["vt"], x2_vt[ti - 2])
            sch.dma("y", lambda y, ti=ti, p=p: y.dma_start(
                out=x2a[p][:], in_=dt["xs1"][128 * ti:128 * (ti + 1), :]), ed["dy"])
            xld = ed["dy"].n

            def w2mm(t, mh=mh):
                r = None
                for nn in range(2):
                    for k in range(KH):
                        r = t.matmul(banks[2 * mh + nn][:, 0:512],
                                     xq2l[:, SW * k + 128 * mh: SW * k + 128 * (mh + 1)],
                                     w2q[:, D * k + 512 * nn: D * k + 512 * (nn + 1)],
                                     start=(k == 0), stop=(k == KH - 1))
                return r
            sch.emit("t", w2mm, inc=ed["tv"])
            xq2l_free_tv = ed["tv"].n
            sch.wait("v", ed["tv"])
            sch.wait("v", ed["dy"], xld)
            if ti >= 2:
                sch.wait("v", ed["dy"], p2b_store_dy[p])
            for nn in range(2):
                sch.emit("v", lambda v, mh=mh, nn=nn, p=p: v.tensor_add(
                    x2b[p][:, 512 * nn:512 * (nn + 1)],
                    banks[2 * mh + nn][:, 0:512],
                    x2a[p][:, 512 * nn:512 * (nn + 1)]),
                    writes=(f"x2b{p}",))
            sch.emit("v", lambda v, p=p: v.tensor_copy(
                scl2[0:1, 0:1], x2b[p][0:1, 0:1]),
                reads=(f"x2b{p}",), writes=(f"x2b{p}",), inc=ed["vt"])
            x2_vt[ti] = ed["vt"].n
            sch.wait("y", ed["vt"], x2_vt[ti])
            sch.dma("y", lambda y, ti=ti, p=p: y.dma_start(
                out=dt["xs2"][128 * ti:128 * (ti + 1), :], in_=x2b[p][:]), ed["dy"])
            p2b_store_dy[p] = ed["dy"].n
            if x2tap is not None:
                sch.dma("y", lambda y, ti=ti, p=p: y.dma_start(
                    out=x2tap[128 * ti:128 * (ti + 1), :], in_=x2b[p][:]), ed["do"])
            sch.wait("t", ed["vt"], x2_vt[ti])
    arf.pop(); arb.pop()



def p3_l1_attn(c):
    sch, ed, arf, arb, banks, dt = c.sch, c.ed, c.arf, c.arb, c.banks, c.dt
    H, KVH, HD = 16, 4, 64
    SC_HI = float(np.float32(1.0 / np.sqrt(32.0)))
    SC_LO = float(np.float32(1.0 / np.sqrt(32.0) - SC_HI))
    CB = 11.3137085
    EPSF = float(np.finfo(np.float32).eps)

    arf.push(); arb.push()
    qT = arb.alloc(KD * S)
    kT = arb.alloc(2 * S)
    vaug = arb.alloc(NT * 264)
    arb.push()
    xnT2 = arb.alloc(2 * KD * 128)
    vtok = arb.alloc(NT * 256)

    xin = [arf.alloc(1024), arf.alloc(1024)]
    lnt = [arf.alloc(1024), arf.alloc(1024)]
    xn = [arf.alloc(1024), arf.alloc(1024)]
    sc = [arf.alloc(24), arf.alloc(24)]
    sqs = [arf.alloc(1024), arf.alloc(1024)]
    qtile = arf.alloc(1024)
    qsq = arf.alloc(1024)
    qrt = arf.alloc(1024)
    qrot = arb.alloc(1024)
    kvg = arf.alloc(768)
    krot = arb.alloc(256)
    vsig = arf.alloc(256)
    sin_tok = arf.alloc(512)
    cos_tok = arf.alloc(512)
    cosrep = arf.alloc(512)
    sinrep = arf.alloc(512)
    rlat = arf.alloc(64)
    grep = arf.alloc(16)
    lamrep = arf.alloc(16)
    negC = arf.alloc(1)
    epsr = arf.alloc(2)
    qsc = arf.alloc(64)
    rowbuf = arf.alloc(32)
    ind1 = arf.alloc(128)
    ind2 = arf.alloc(128)
    r1sb = arf.alloc(512)
    r2sb = arf.alloc(512)
    rrep_sb = arf.alloc(512)
    an_sb = arf.alloc(512)
    tsb = arf.alloc(512)
    diag01 = arf.alloc(128)
    x3a = xin
    x3b = lnt

    wqTs = arb.alloc(KD * 1024)
    wkvg = arb.alloc(KD * 768)
    for k in range(KD):
        sch.dma("g", lambda g, k=k: g.dma_start(
            out=wqTs[:, 1024 * k:1024 * (k + 1)],
            in_=dt["wqT"][128 * k:128 * (k + 1), :]), ed["dg"])
        sch.dma("g", lambda g, k=k: g.dma_start(
            out=wkvg[:, 768 * k:768 * (k + 1)],
            in_=dt["wkvgT"][128 * k:128 * (k + 1), :]), ed["dg"])
    sch.dma("y", lambda y: y.dma_start(out=rowbuf[0:1, 0:16], in_=dt["qgain"]),
            ed["dy"])
    sch.dma("y", lambda y: y.dma_start(out=rowbuf[0:1, 16:32], in_=dt["lam"]),
            ed["dy"])
    wl_dg = ed["dg"].n
    wl_dy = ed["dy"].n

    def p3setup_g(g):
        g.memset(ind1[:], 0.0)
        g.memset(ind2[:], 0.0)
        g.drain()
        g.iota(ind1[0:1, 0:33], pattern=[[0, 33]], base=1, channel_multiplier=0,
               allow_small_or_imprecise_dtypes=True)
        g.iota(ind2[0:1, 64:97], pattern=[[0, 33]], base=1, channel_multiplier=0,
               allow_small_or_imprecise_dtypes=True)
        return g.tensor_scalar(diag01[:], c.diagm1[:], 1.0, None, OP.add)
    sch.emit("g", p3setup_g, inc=ed["gt"])
    sch.wait("t", ed["gt"])
    sch.wait("v", ed["gt"])

    sch.wait("t", ed["dy"], wl_dy)
    sch.emit("t", lambda t: t.matmul(banks[4][:, 0:32], c.ones_f[0:1, :],
                                     rowbuf[0:1, 0:32], start=True, stop=True),
             inc=ed["ts"])
    sch.wait("s", ed["ts"])
    sch.emit("s", lambda s: s.copy(grep[:], banks[4][:, 0:16]), writes=("grep",))
    sch.emit("s", lambda s: s.copy(lamrep[:], banks[4][:, 16:32]),
             writes=("grep",), inc=ed["sv"])
    sch.wait("v", ed["sv"])
    sch.emit("v", lambda v: v.reduce_max(negC[:, 0:1], grep[:], AX,
                                         apply_absolute_value=True),
             writes=("negC",))
    sch.emit("v", lambda v: v.tensor_scalar(negC[:, 0:1], negC[:, 0:1], -CB, None,
                                            OP.mult),
             reads=("negC",), writes=("negC",))
    EPS64 = 2.220446049250313e-16
    sch.emit("v", lambda v: v.tensor_scalar(
        epsr[:, 0:1], c.g_ap, -EPSF, EPSF, OP.mult, OP.add), writes=("epsr",))
    sch.emit("v", lambda v: v.scalar_tensor_tensor(
        epsr[:, 1:2], c.g_ap, EPS64, epsr[:, 0:1], OP.mult, OP.add),
        reads=("epsr",), writes=("epsr",), inc=ed["vt"])
    setup_vt = ed["vt"].n

    sch.dma("y", lambda y: y.dma_start(out=sin_tok[:], in_=dt["sintab"]),
            ed["dy"])
    sch.dma("y", lambda y: y.dma_start(out=cos_tok[:], in_=dt["costab"]),
            ed["dy"])
    tab_dy = ed["dy"].n
    sch.wait("v", ed["dy"], tab_dy)
    sch.emit("v", lambda v: v.tensor_copy(qsc[:, 3:4], sin_tok[:, 0:1]),
             reads=("stok", "ctok"), writes=("rA",), inc=ed["vt"])
    rot_vt = ed["vt"].n
    rtap = c.dbg_tap("rtab", (128, 1024))
    if rtap is not None:
        sch.wait("y", ed["vt"], rot_vt)
        sch.dma("y", lambda y: y.dma_start(out=rtap[:, 0:512], in_=sin_tok[:]),
                ed["do"])
        sch.dma("y", lambda y: y.dma_start(out=rtap[:, 512:1024], in_=cos_tok[:]),
                ed["do"])

    sch.wait("t", ed["dg"], wl_dg)
    sch.wait("v", ed["vt"], rot_vt)
    sch.wait("v", ed["vt"], setup_vt)
    ln_vt = [0] * NT
    xnd_dy = [0] * NT
    prev_vs = [0, 0]
    for t in range(NT):
        p = t % 2
        if t >= 2:
            sch.wait("y", ed["vt"], ln_vt[t - 2])
        sch.dma("y", lambda y, t=t, p=p: y.dma_start(
            out=xin[p][:], in_=dt["xs2"][128 * t:128 * (t + 1), :]), ed["dy"])
        ld = ed["dy"].n
        sch.wait("s", ed["dy"], ld)
        if t >= 2:
            sch.wait("s", ed["vs"], prev_vs[p])
        sch.emit("s", lambda s, p=p: s.activation(
            sqs[p][:], xin[p][:], AF.Square, accum_out=sc[p][:, 1:2]),
            writes=(f"s3q{p}",), inc=ed["sv"])
        sq_sv = ed["sv"].n
        sch.wait("v", ed["dy"], ld)
        sch.emit("v", lambda v, p=p: v.reduce_sum(sc[p][:, 0:1], xin[p][:], AX),
                 writes=(f"sc{p}",))
        sch.wait("v", ed["sv"], sq_sv)
        sch.emit("v", lambda v, p=p: v.tensor_scalar(
            sc[p][:, 2:3], sc[p][:, 0:1], 1.0 / D, None, OP.mult),
            reads=(f"sc{p}",), writes=(f"sc{p}",))
        sch.emit("v", lambda v, p=p: v.tensor_scalar(
            sc[p][:, 3:4], sc[p][:, 1:2], 1.0 / D, None, OP.mult))
        sch.emit("v", lambda v, p=p: v.tensor_mul(
            sc[p][:, 4:5], sc[p][:, 2:3], sc[p][:, 2:3]),
            reads=(f"sc{p}",), writes=(f"sc{p}",))
        sch.emit("v", lambda v, p=p: v.tensor_sub(
            sc[p][:, 5:6], sc[p][:, 3:4], sc[p][:, 4:5]),
            reads=(f"sc{p}",), writes=(f"sc{p}",), inc=ed["vs"])
        vs_here = ed["vs"].n
        sch.wait("s", ed["vs"], vs_here)
        sch.emit("s", lambda s, p=p: s.activation(
            sc[p][:, 6:7], sc[p][:, 5:6], AF.Sqrt, bias=c.magic[:, 7:8]),
            writes=(f"s3c{p}",), inc=ed["sv"])
        sch.wait("v", ed["sv"])
        sch.emit("v", lambda v, p=p: v.reciprocal(sc[p][:, 7:8], sc[p][:, 6:7]),
                 reads=(f"sc{p}",), writes=(f"sc{p}",))
        sch.emit("v", lambda v, p=p: v.tensor_mul(
            sc[p][:, 8:9], sc[p][:, 2:3], sc[p][:, 7:8]),
            reads=(f"sc{p}",), writes=(f"sc{p}",))
        sch.emit("v", lambda v, p=p: v.tensor_scalar(
            lnt[p][:], xin[p][:], sc[p][:, 7:8], None, OP.mult),
            reads=(f"sc{p}",), writes=(f"lnt{p}",))
        sch.emit("v", lambda v, p=p: v.tensor_scalar(
            xn[p][:], lnt[p][:], sc[p][:, 8:9], None, OP.subtract),
            reads=(f"lnt{p}",), writes=(f"xn{p}",), inc=ed["vt"])
        ln_vt[t] = ed["vt"].n
        prev_vs[p] = vs_here
        sch.wait("t", ed["vt"], ln_vt[t])
        if t >= 1:
            pass
        if t >= 2:
            sch.wait("t", ed["dy"], xnd_dy[t - 2])
        for kk in range(KD):
            bsel = banks[4 + (kk % 2)]
            sch.emit("t", lambda tt_, p=p, kk=kk, bsel=bsel: tt_.transpose(
                bsel[:, 0:128], xn[p][:, 128 * kk:128 * (kk + 1)], c.ident_f[:]),
                inc=ed["ts"])
            sch.wait("s", ed["ts"])
            sch.emit("s", lambda s, p=p, kk=kk, bsel=bsel: s.copy(
                xnT2[:, 1024 * p + 128 * kk: 1024 * p + 128 * (kk + 1)],
                bsel[:, 0:128]), inc=ed["st"])
            sch.wait("t", ed["st"])
        tr_st = ed["st"].n
        sch.wait("y", ed["st"], tr_st)
        sch.dma("y", lambda y, t=t, p=p: y.dma_start(
            out=dt["xnd"][:, :, 128 * t:128 * (t + 1)].transpose([1, 0, 2]),
            in_=xnT2[:, 1024 * p:1024 * (p + 1)].rearrange("p (k c) -> p k c", k=KD)),
            ed["dy"])
        xnd_dy[t] = ed["dy"].n

        for hh in range(16):
            sch.emit("v", lambda v, t=t, hh=hh: v.tensor_copy(
                cosrep[:, 32 * hh:32 * (hh + 1)], cos_tok[:, 32 * t:32 * (t + 1)]),
                reads=("f2o",), writes=("crep",))
            sch.emit("v", lambda v, t=t, hh=hh: v.tensor_copy(
                sinrep[:, 32 * hh:32 * (hh + 1)], sin_tok[:, 32 * t:32 * (t + 1)]),
                reads=("f2o",), writes=("crep",))

        def qmm(tt_, p=p):
            r = None
            for nn in range(2):
                for k in range(KD):
                    r = tt_.matmul(banks[nn][:, 0:512],
                                   xnT2[:, 1024 * p + 128 * k: 1024 * p + 128 * (k + 1)],
                                   wqTs[:, 1024 * k + 512 * nn: 1024 * k + 512 * (nn + 1)],
                                   start=(k == 0), stop=(k == KD - 1))
            return r
        sch.emit("t", qmm, inc=ed["ts"])
        sch.wait("s", ed["ts"])

        def qev(s):
            s.copy(qtile[:, 0:512], banks[0][:, 0:512])
            return s.copy(qtile[:, 512:1024], banks[1][:, 0:512])
        sch.emit("s", qev, writes=("qtile",), inc=ed["sv"])
        sch.wait("v", ed["sv"])
        sch.wait("s", ed["sv"])
        sch.emit("s", lambda s: s.activation(qsq[:], qtile[:], AF.Square),
                 reads=("qtile",), writes=("qsq",), inc=ed["sv"])
        sch.wait("v", ed["sv"])
        for hh in range(16):
            sch.emit("v", lambda v, hh=hh: v.reduce_sum(
                qsc[:, hh:hh + 1], qsq[:, 64 * hh:64 * (hh + 1)], AX),
                reads=("qsq", "qrmz"), writes=("qrm",))
        sch.emit("v", lambda v: v.tensor_scalar(
            qsc[:, 16:32], qsc[:, 0:16], 1.0 / 64.0, None, OP.mult),
            reads=("qrm",), writes=("qrmz",), inc=ed["vs"])
        sch.wait("s", ed["vs"])
        sch.emit("s", lambda s: s.activation(qsc[:, 32:48], qsc[:, 16:32], AF.Sqrt,
                                             bias=epsr[:, 1:2]),
                 writes=("qrs",), inc=ed["sv"])
        sch.wait("v", ed["sv"])
        sch.emit("v", lambda v: v.reciprocal(qsc[:, 48:64], qsc[:, 32:48]),
                 reads=("qrs", "qrmz"), writes=("qrr",))
        sch.emit("v", lambda v: v.tensor_mul(qsc[:, 0:16], qsc[:, 48:64], grep[:]),
                 reads=("qrr",), writes=("qrg",))
        sch.emit("v", lambda v: v.tensor_scalar(
            qsc[:, 16:32], qsc[:, 0:16], SC_HI, None, OP.mult),
            reads=("qrg",), writes=("qrg2",))
        sch.emit("v", lambda v: v.scalar_tensor_tensor(
            qsc[:, 32:48], qsc[:, 0:16], SC_LO, qsc[:, 16:32], OP.mult, OP.add),
            reads=("qrg", "qrg2"), writes=("qrg3",))
        q3 = qtile[:].rearrange("p (h d) -> p h d", h=16)
        r3 = qrt[:].rearrange("p (h d) -> p h d", h=16)
        c3 = cosrep[:].rearrange("p (h d) -> p h d", h=16)
        s3 = sinrep[:].rearrange("p (h d) -> p h d", h=16)
        sch.emit("v", lambda v: v.tensor_mul(r3[:, :, 0:32], q3[:, :, 0:32],
                                             c3[:, :, 0:32]),
                 reads=("qtile", "crep"), writes=("qrt",))
        sch.emit("v", lambda v: v.scalar_tensor_tensor(
            r3[:, :, 0:32], q3[:, :, 32:64], 1.0, r3[:, :, 0:32], OP.bypass, OP.mult)
            if False else
            v.tensor_mul(qsq[:].rearrange("p (h d) -> p h d", h=16)[:, :, 0:32],
                         q3[:, :, 32:64], s3[:, :, 0:32]),
            reads=("qtile", "crep", "qsq"), writes=("qs2",))
        sch.emit("v", lambda v: v.tensor_add(
            r3[:, :, 0:32], r3[:, :, 0:32],
            qsq[:].rearrange("p (h d) -> p h d", h=16)[:, :, 0:32]),
            reads=("qrt", "qs2"), writes=("qrt",))
        sch.emit("v", lambda v: v.tensor_mul(
            qsq[:].rearrange("p (h d) -> p h d", h=16)[:, :, 32:64],
            q3[:, :, 32:64], c3[:, :, 0:32]),
            reads=("qtile", "crep"), writes=("qs3",))
        sch.emit("v", lambda v: v.tensor_mul(
            r3[:, :, 32:64], q3[:, :, 0:32], s3[:, :, 0:32]),
            reads=("qtile", "crep"), writes=("qrt2",))
        sch.emit("v", lambda v: v.tensor_sub(
            r3[:, :, 32:64],
            qsq[:].rearrange("p (h d) -> p h d", h=16)[:, :, 32:64],
            r3[:, :, 32:64]),
            reads=("qs3", "qrt2"), writes=("qrt",))
        OUTBLK = [0, 2, 4, 6, 1, 3, 5, 7, 8, 10, 12, 14, 9, 11, 13, 15]
        for hh in range(16):
            ob = OUTBLK[hh]
            sch.emit("v", lambda v, hh=hh, ob=ob: v.tensor_scalar(
                qrot[:, 64 * ob:64 * (ob + 1)], qrt[:, 64 * hh:64 * (hh + 1)],
                qsc[:, 32 + hh:33 + hh], None, OP.mult),
                reads=("qrt", "qrg3"), writes=("qrot",))
        sch.emit("v", lambda v: v.tensor_copy(qsc[:, 32:33], qrot[:, 0:1]),
                 reads=("qrot",), writes=("qrot",), inc=ed["vt"])
        q_vt = ed["vt"].n
        sch.wait("t", ed["vt"], q_vt)

        def qtr(tt_, t=t):
            r = None
            for kk in range(KD):
                r = tt_.transpose(c.pbb[0][:, 128 * kk:128 * (kk + 1)],
                                  qrot[:, 128 * kk:128 * (kk + 1)], c.ident_b[:])
            return r
        sch.emit("t", qtr, inc=ed["ts"])
        sch.wait("s", ed["ts"])

        def qtev(s, t=t):
            r = None
            for kk in range(KD):
                r = s.copy(qT[:, kk * S + 128 * t: kk * S + 128 * (t + 1)],
                           c.pbb[0][:, 128 * kk:128 * (kk + 1)])
            return r
        sch.emit("s", qtev, inc=ed["st"])
        sch.wait("t", ed["st"])

        def kmm(tt_, p=p):
            r = None
            for nn, (c0, cn) in enumerate(((0, 512), (512, 256))):
                for k in range(KD):
                    r = tt_.matmul(banks[nn][:, 0:cn],
                                   xnT2[:, 1024 * p + 128 * k: 1024 * p + 128 * (k + 1)],
                                   wkvg[:, 768 * k + c0: 768 * k + c0 + cn],
                                   start=(k == 0), stop=(k == KD - 1))
            return r
        sch.emit("t", kmm, inc=ed["ts"])
        sch.wait("s", ed["ts"])

        def kev(s):
            s.copy(kvg[:, 0:512], banks[0][:, 0:512])
            return s.copy(kvg[:, 512:768], banks[1][:, 0:256])
        sch.emit("s", kev, writes=("kvg",), inc=ed["sv"])
        sch.wait("v", ed["sv"])
        sch.wait("s", ed["sv"])
        sch.emit("s", lambda s: s.activation(qsq[:, 0:256], kvg[:, 0:256], AF.Square),
                 reads=("kvg",), writes=("ksq",), inc=ed["sv"])
        sch.wait("v", ed["sv"])
        for hh in range(4):
            sch.emit("v", lambda v, hh=hh: v.reduce_sum(
                qsc[:, hh:hh + 1], qsq[:, 64 * hh:64 * (hh + 1)], AX),
                reads=("ksq", "krmz"), writes=("krm",))
        sch.emit("v", lambda v: v.tensor_scalar(
            qsc[:, 4:8], qsc[:, 0:4], 1.0 / 64.0, None, OP.mult),
            reads=("krm",), writes=("krmz",), inc=ed["vs"])
        sch.wait("s", ed["vs"])
        sch.emit("s", lambda s: s.activation(qsc[:, 8:12], qsc[:, 4:8], AF.Sqrt,
                                             bias=epsr[:, 1:2]),
                 writes=("krs",), inc=ed["sv"])
        sch.wait("v", ed["sv"])
        sch.emit("v", lambda v: v.reciprocal(qsc[:, 12:16], qsc[:, 8:12]),
                 reads=("krs", "krmz"), writes=("krr",))
        k3 = kvg[:, 0:256].rearrange("p (h d) -> p h d", h=4)
        kr3 = qrt[:, 0:256].rearrange("p (h d) -> p h d", h=4)
        kq3 = qsq[:, 256:512].rearrange("p (h d) -> p h d", h=4)
        kc3 = cosrep[:, 0:128].rearrange("p (h d) -> p h d", h=4)
        ks3 = sinrep[:, 0:128].rearrange("p (h d) -> p h d", h=4)
        sch.emit("v", lambda v: v.tensor_mul(kr3[:, :, 0:32], k3[:, :, 0:32],
                                             kc3[:, :, 0:32]),
                 reads=("kvg", "crep"), writes=("krt",))
        sch.emit("v", lambda v: v.tensor_mul(kq3[:, :, 0:32], k3[:, :, 32:64],
                                             ks3[:, :, 0:32]),
                 reads=("kvg", "crep"), writes=("kq2",))
        sch.emit("v", lambda v: v.tensor_add(kr3[:, :, 0:32], kr3[:, :, 0:32],
                                             kq3[:, :, 0:32]),
                 reads=("krt", "kq2"), writes=("krt",))
        sch.emit("v", lambda v: v.tensor_mul(kq3[:, :, 32:64], k3[:, :, 32:64],
                                             kc3[:, :, 0:32]),
                 reads=("kvg", "crep"), writes=("kq3",))
        sch.emit("v", lambda v: v.tensor_mul(kr3[:, :, 32:64], k3[:, :, 0:32],
                                             ks3[:, :, 0:32]),
                 reads=("kvg", "crep"), writes=("krt2",))
        sch.emit("v", lambda v: v.tensor_sub(kr3[:, :, 32:64], kq3[:, :, 32:64],
                                             kr3[:, :, 32:64]),
                 reads=("kq3", "krt2"), writes=("krt",))
        for hh in range(4):
            sch.emit("v", lambda v, hh=hh: v.tensor_scalar(
                krot[:, 64 * hh:64 * (hh + 1)], qrt[:, 64 * hh:64 * (hh + 1)],
                qsc[:, 12 + hh:13 + hh], None, OP.mult),
                reads=("krt", "krr"), writes=("krot",))
        sch.emit("v", lambda v: v.tensor_copy(qsc[:, 32:33], krot[:, 0:1]),
                 reads=("krot",), writes=("krot",), inc=ed["vs"])
        k_vs = ed["vs"].n
        sch.wait("s", ed["vs"], k_vs)
        sch.emit("s", lambda s: s.activation(vsig[:], kvg[:, 512:768], AF.Sigmoid),
                 reads=("kvg",), writes=("vsig",), inc=ed["sv"])
        sch.wait("v", ed["sv"])
        sch.emit("v", lambda v, t=t: v.tensor_mul(
            vtok[:, 256 * t:256 * (t + 1)], kvg[:, 256:512], vsig[:]),
            reads=("vsig", "kvg"), writes=("vtok",), inc=ed["vt"])
        kv_vt = ed["vt"].n
        sch.wait("t", ed["vt"], kv_vt)

        def ktr(tt_, t=t):
            tt_.transpose(c.pbb[0][:, 0:128], krot[:, 0:128], c.ident_b[:])
            return tt_.transpose(c.pbb[0][:, 128:256], krot[:, 128:256], c.ident_b[:])
        sch.emit("t", ktr, inc=ed["ts"])
        sch.wait("s", ed["ts"])

        def ktev(s, t=t):
            s.copy(kT[:, 128 * t:128 * (t + 1)], c.pbb[0][:, 0:128])
            return s.copy(kT[:, S + 128 * t:S + 128 * (t + 1)],
                          c.pbb[0][:, 128:256])
        sch.emit("s", ktev, inc=ed["st"])
        sch.wait("t", ed["st"])

    def vaug_build(g):
        r = None
        for jt in range(NT):
            for kvh in range(4):
                for hf in range(2):
                    off = 264 * jt + 66 * kvh + 33 * hf
                    r = g.memset(vaug[:, off + 32:off + 33], 1.0)
        return r
    sch.emit("g", vaug_build, inc=ed["gv"])
    sch.wait("v", ed["gv"])
    vg_vt = []
    for jt in range(NT):
        for kvh in range(4):
            for hf in range(2):
                off = 264 * jt + 66 * kvh + 33 * hf
                sch.emit("v", lambda v, jt=jt, kvh=kvh, hf=hf, off=off: v.tensor_copy(
                    vaug[:, off:off + 32],
                    vtok[:, 256 * jt + 64 * kvh + 32 * hf: 256 * jt + 64 * kvh + 32 * hf + 32]),
                    reads=("vtok",), writes=("vaug",))
    sch.emit("v", lambda v: v.tensor_copy(qsc[:, 32:33], vaug[:, 0:1]),
             reads=("vaug",), writes=("vaug",), inc=ed["vt"])
    vaug_vt = ed["vt"].n
    sch.wait("t", ed["vt"], vaug_vt)
    arb.pop()
    sch.barrier(ed)
    arb.push()
    yT = [arb.alloc(KD * 512), arb.alloc(KD * 512)]
    Pb = [arb.alloc(512), arb.alloc(512)]
    woTs = arb.alloc(KD * 1024)
    for k in range(KD):
        sch.dma("g", lambda g, k=k: g.dma_start(
            out=woTs[:, 1024 * k:1024 * (k + 1)],
            in_=dt["woT"][128 * k:128 * (k + 1), :]), ed["dg"])
    wo_dg = ed["dg"].n
    sch.wait("t", ed["dg"], wo_dg)

    x3tap = c.dbg_tap("x3", (S, D))
    sco_free_sv = {0: 0, 1: 0, 4: 0, 5: 0}
    pv_free_vt = 0
    x3_vt = [0] * NT
    st3_dy = [0] * NT
    yring_free_tv = [0, 0]
    for ic in range(4):
        icp = ic % 2
        njc = 4 * ic + 4
        sch.wait("v", ed["tv"], yring_free_tv[icp])
        for h in range(H):
            kvh = h // 4
            kchunk = kvh // 2
            kbase = (64 * kvh) % 128
            for jc in range(njc):
                m = jc - 4 * ic
                icol0 = 128 * m if m > 0 else 0
                ncols = 512 - icol0
                for hf in range(2):
                    sb = (jc % 2) + (0 if hf == 0 else 4)
                    qchunk = (h % 4) + 4 * (h // 8)
                    qbase = 64 * ((h // 4) % 2) + 32 * hf
                    kb2 = kbase + 32 * hf
                    sch.wait("t", ed["sv"], sco_free_sv[sb])

                    def smm(tt_, ic=ic, jc=jc, sb=sb, qbase=qbase,
                            qchunk=qchunk, kb2=kb2, kchunk=kchunk, icol0=icol0,
                            ncols=ncols):
                        return tt_.matmul(
                            banks[sb][:, icol0:icol0 + ncols],
                            kT[kb2:kb2 + 32,
                               kchunk * S + 128 * jc: kchunk * S + 128 * (jc + 1)],
                            qT[qbase:qbase + 32,
                               qchunk * S + 512 * ic + icol0:
                               qchunk * S + 512 * ic + icol0 + ncols],
                            start=True, stop=True,
                            tile_position=(kb2, 0))
                    sch.emit("t", smm, inc=ed["ts"])
                    sch.wait("s", ed["ts"])
                    sch.wait("s", ed["vt"], pv_free_vt)

                    def sev(s, hf=hf, sb=sb, icol0=icol0, ncols=ncols):
                        return s.activation(
                            Pb[hf][:, icol0:icol0 + ncols],
                            banks[sb][:, icol0:icol0 + ncols], AF.Exp,
                            bias=negC[:, 0:1])
                    sch.emit("s", sev, writes=(f"pb{hf}",), inc=ed["sv"])
                    sco_free_sv[sb] = ed["sv"].n
                sch.wait("v", ed["sv"], sco_free_sv[(jc % 2) + 4])
                if m >= 0:
                    for hf in range(2):
                        if icol0 > 0:
                            sch.emit("v", lambda v, hf=hf, icol0=icol0: v.tensor_scalar(
                                Pb[hf][:, 0:icol0], Pb[hf][:, 0:icol0], 0.0, None,
                                OP.mult), reads=(f"pb{hf}",), writes=(f"pb{hf}",))
                        sch.emit("v", lambda v, hf=hf, icol0=icol0: v.tensor_mul(
                            Pb[hf][:, icol0:icol0 + 128],
                            Pb[hf][:, icol0:icol0 + 128], diag01[:]),
                            reads=(f"pb{hf}",), writes=(f"pb{hf}",))
                sch.emit("v", lambda v: v.tensor_copy(qsc[:, 33:34], Pb[0][:, 0:1].to_broadcast([128, 1]) if False else Pb[0][:, 0:1]),
                         reads=("pb0", "pb1"), writes=("pbm",), inc=ed["vt"])
                pv_free_vt = ed["vt"].n
                sch.wait("t", ed["vt"], pv_free_vt)

                def pvm2(tt_, jc=jc, kvh=kvh, njc=njc):
                    r = None
                    for hf in range(2):
                        r = tt_.matmul(banks[2 + hf][0:33, 0:512],
                                       vaug[:, 264 * jc + 66 * kvh + 33 * hf:
                                            264 * jc + 66 * kvh + 33 * hf + 33],
                                       Pb[hf][:, 0:512],
                                       start=(jc == 0), stop=(jc == njc - 1))
                    return r
                sch.emit("t", pvm2, inc=ed["ts"])
            pv_done_ts = ed["ts"].n
            sch.wait("v", ed["ts"], pv_done_ts)
            sch.emit("v", lambda v: v.reciprocal(r1sb[0:1, :], banks[2][32:33, 0:512]),
                     writes=("r12",))
            sch.emit("v", lambda v: v.reciprocal(r2sb[0:1, :], banks[3][32:33, 0:512]),
                     writes=("r12",), inc=ed["vt"])
            rc_vt = ed["vt"].n
            sch.wait("t", ed["vt"], rc_vt)

            def rrepmm(tt_):
                tt_.matmul(banks[4][:, 0:512], ind1[0:1, :], r1sb[0:1, :],
                           start=True, stop=False)
                return tt_.matmul(banks[4][:, 0:512], ind2[0:1, :], r2sb[0:1, :],
                                  start=False, stop=True)
            sch.emit("t", rrepmm, inc=ed["ts"])
            sch.wait("s", ed["ts"])
            sch.emit("s", lambda s: s.copy(rrep_sb[:], banks[4][:, 0:512]),
                     writes=("rrep",), inc=ed["sv"])
            sch.wait("v", ed["sv"])
            ybase = (64 * h) % 128
            ychunk = (64 * h) // 128
            sch.emit("v", lambda v: v.tensor_mul(
                an_sb[0:32, :], banks[2][0:32, 0:512], rrep_sb[0:32, :]),
                reads=("rrep",), writes=("an",))
            sch.emit("v", lambda v: v.tensor_mul(
                an_sb[64:96, :], banks[3][0:32, 0:512], rrep_sb[64:96, :]),
                reads=("rrep",), writes=("an",))
            sch.emit("v", lambda v, h=h: v.tensor_scalar(
                tsb[0:32, :], an_sb[64:96, :], lamrep[64:96, h:h + 1], None, OP.mult),
                reads=("an",), writes=("tsb",))
            sch.emit("v", lambda v, icp=icp, ybase=ybase, ychunk=ychunk: v.tensor_sub(
                yT[icp][ybase:ybase + 32, 512 * ychunk:512 * (ychunk + 1)],
                an_sb[0:32, :], tsb[0:32, :]),
                reads=("an", "tsb"), writes=("yT",))
            sch.emit("v", lambda v, icp=icp, ybase=ybase, ychunk=ychunk: v.tensor_add(
                yT[icp][ybase + 32:ybase + 64, 512 * ychunk:512 * (ychunk + 1)],
                an_sb[0:32, :], tsb[0:32, :]),
                reads=("an", "tsb"), writes=("yT",), inc=ed["vt"])
            yv = ed["vt"].n
            sch.wait("t", ed["vt"], yv)
        y_ic_vt = ed["vt"].n

        sch.wait("t", ed["vt"], y_ic_vt)
        for tl in range(4):
            t = 4 * ic + tl
            p = t % 2
            if t >= 2:
                sch.wait("y", ed["vt"], x3_vt[t - 2])
            else:
                sch.wait("y", ed["vt"], y_ic_vt)
            sch.dma("y", lambda y, t=t, p=p: y.dma_start(
                out=x3a[p][:], in_=dt["xs2"][128 * t:128 * (t + 1), :]), ed["dy"])
            xld = ed["dy"].n

            def omm(tt_, icp=icp, tl=tl):
                r = None
                for nn in range(2):
                    for k in range(KD):
                        r = tt_.matmul(banks[4 + nn][:, 0:512],
                                       yT[icp][:, 512 * k + 128 * tl:
                                               512 * k + 128 * (tl + 1)],
                                       woTs[:, 1024 * k + 512 * nn:
                                            1024 * k + 512 * (nn + 1)],
                                       start=(k == 0), stop=(k == KD - 1))
                return r
            sch.emit("t", omm, inc=ed["tv"])
            sch.wait("v", ed["tv"])
            sch.wait("v", ed["dy"], xld)
            if t >= 2:
                sch.wait("v", ed["dy"], st3_dy[t - 2])
            for nn in range(2):
                sch.emit("v", lambda v, nn=nn, p=p: v.tensor_add(
                    x3b[p][:, 512 * nn:512 * (nn + 1)], banks[4 + nn][:, 0:512],
                    x3a[p][:, 512 * nn:512 * (nn + 1)]),
                    writes=(f"x3b{p}",))
            sch.emit("v", lambda v, p=p: v.tensor_copy(
                qsc[0:1, 34:35], x3b[p][0:1, 0:1]),
                reads=(f"x3b{p}",), writes=(f"x3b{p}",), inc=ed["vt"])
            x3_vt[t] = ed["vt"].n
            sch.wait("y", ed["vt"], x3_vt[t])
            sch.dma("y", lambda y, t=t, p=p: y.dma_start(
                out=dt["xs3"][128 * t:128 * (t + 1), :], in_=x3b[p][:]), ed["dy"])
            st3_dy[t] = ed["dy"].n
            if x3tap is not None:
                sch.dma("y", lambda y, t=t, p=p: y.dma_start(
                    out=x3tap[128 * t:128 * (t + 1), :], in_=x3b[p][:]), ed["do"])
            sch.wait("t", ed["vt"], x3_vt[t])
        yring_free_tv[icp] = ed["tv"].n
    arb.pop()
    arf.pop(); arb.pop()



def p4_l1_moe(c):
    sch, ed, arf, arb, banks, dt = c.sch, c.ed, c.arf, c.arb, c.banks, c.dt
    HM = 8
    NE = 16

    arf.push(); arb.push()
    hT = arb.alloc(KD * S)
    bm_sb = arf.alloc(KD)
    scl = arf.alloc(16)
    wsel = arf.alloc(64)
    comb = [arf.alloc(512), arf.alloc(512)]
    moet = arf.alloc(4096)
    x4a = [arf.alloc(1024), arf.alloc(1024)]
    x4b = [arf.alloc(1024), arf.alloc(1024)]

    arb.push()
    xnT2m = arb.alloc(KD * S)
    wm_sb = arb.alloc(KD * 1024)
    for k in range(KD):
        sch.dma("g", lambda g, k=k: g.dma_start(
            out=xnT2m[:, S * k:S * (k + 1)], in_=dt["xnd"][k]), ed["dg"])
        sch.dma("g", lambda g, k=k: g.dma_start(
            out=wm_sb[:, 1024 * k:1024 * (k + 1)],
            in_=dt["wm"][128 * k:128 * (k + 1), :]), ed["dg"])
    sch.dma("y", lambda y: y.dma_start(out=bm_sb[:], in_=dt["bm"]), ed["dy"])
    wm_dg = ed["dg"].n
    wm_dy = ed["dy"].n
    sch.wait("t", ed["dg"], wm_dg)
    for m in range(KD):
        for n in range(4):
            b = (m * 4 + n) % 2
            sch.wait("t", ed["st"])
            sch.wait("v", ed["dy"], wm_dy)

            def wmmm(tt_, m=m, n=n, b=b):
                r = None
                for k in range(KD):
                    r = tt_.matmul(banks[b][:, 0:512],
                                   wm_sb[:, 1024 * k + 128 * m: 1024 * k + 128 * (m + 1)],
                                   xnT2m[:, S * k + 512 * n: S * k + 512 * (n + 1)],
                                   start=(k == 0), stop=(k == KD - 1))
                return r
            sch.emit("t", wmmm, inc=ed["tv"])
            sch.wait("v", ed["tv"])
            sch.emit("v", lambda v, m=m, n=n, b=b: v.tensor_scalar(
                hT[:, S * m + 512 * n: S * m + 512 * (n + 1)], banks[b][:, 0:512],
                bm_sb[:, m:m + 1], None, OP.add), inc=ed["st"])
    hT_st = ed["st"].n
    arb.pop()
    sch.barrier(ed)

    arb.push()
    w1e = arb.alloc(NE * 512)
    w2e = arb.alloc(NE * 512)
    wo2s = arb.alloc(KD * 1024)
    emb_sb = arb.alloc(16)
    wselT = arb.alloc(512)
    b2_sb = arb.alloc(128)
    hm_sb = arb.alloc(2048)
    moeT = arb.alloc(1024)
    b1_sb = arf.alloc(64)
    for e in range(NE):
        sch.dma("g", lambda g, e=e: g.dma_start(
            out=w1e[:, 512 * e:512 * (e + 1)], in_=dt["w1e"][e]), ed["dg"])
        sch.dma("g", lambda g, e=e: g.dma_start(
            out=w2e[:, 512 * e:512 * (e + 1)].rearrange("p (i o) -> p i o", i=4),
            in_=dt["w2e"][e].transpose([1, 0, 2])), ed["dg"])
    for k in range(KD):
        sch.dma("g", lambda g, k=k: g.dma_start(
            out=wo2s[:, 1024 * k:1024 * (k + 1)],
            in_=dt["wo2"][128 * k:128 * (k + 1), :]), ed["dg"])
    sch.dma("g", lambda g: g.dma_start(out=emb_sb[:], in_=dt["emb"]), ed["dg"])
    sch.dma("g", lambda g: g.dma_start(
        out=b2_sb[0:16, 0:128], in_=dt["b2e"]), ed["dg"])
    sch.dma("y", lambda y: y.dma_start(out=b1_sb[:], in_=dt["b1e"]), ed["dy"])
    b1_dy = ed["dy"].n
    we_dg = ed["dg"].n
    sch.wait("t", ed["dg"], we_dg)
    sch.wait("s", ed["dy"], b1_dy)
    sch.wait("t", ed["st"], hT_st)
    sch.wait("v", ed["vt"], c.magic_vt)

    out4tap = c.dbg_tap("x4", (S, D))
    hb_free_sv = [0, 0]
    eo_free_vt = [0, 0]
    wsel_done_vs = 0
    x4_vt = [0] * NT
    st4_dy = [0] * NT
    moet_free_ts = [0] * 2
    for tc in range(4):
        tcp = tc % 2
        sch.wait("v", ed["ts"], moet_free_ts[tcp])
        for h in range(HM):
            for tl in range(4):
                sch.wait("t", ed["vs"], wsel_done_vs)

                def lgm(tt_, h=h, tc=tc, tl=tl):
                    return tt_.matmul(banks[4][:, 0:16],
                                      hT[:, S * h + 512 * tc + 128 * tl:
                                         S * h + 512 * tc + 128 * (tl + 1)],
                                      emb_sb[:], start=True, stop=True)
                sch.emit("t", lgm, inc=ed["tv"])
                sch.wait("v", ed["tv"])
                W = wsel[:, 16 * tl:16 * (tl + 1)]
                sch.emit("v", lambda v, W=W: v.reduce_max(
                    scl[:, 0:1], banks[4][:, 0:16], AX, negate=True),
                    writes=("mx",))
                sch.emit("v", lambda v: v.tensor_copy(scl[:, 8:9], scl[:, 0:1]),
                         reads=("mx",), writes=("mx",), inc=ed["vs"])
                sch.wait("s", ed["vs"])
                sch.emit("s", lambda s, W=W: s.activation(
                    W, banks[4][:, 0:16], AF.Exp, bias=scl[:, 8:9],
                    accum_out=scl[:, 1:2]), writes=("wex",), inc=ed["sv"])
                sch.wait("v", ed["sv"])
                sch.emit("v", lambda v: v.reciprocal(scl[:, 2:3], scl[:, 1:2]),
                         writes=("mx",))
                sch.emit("v", lambda v, W=W: v.tensor_scalar(
                    W, W, scl[:, 2:3], None, OP.mult),
                    reads=("wex", "mx"), writes=("wex",))
                sch.emit("v", lambda v, W=W: v.reduce_max(scl[:, 3:4], W, AX),
                         reads=("wex",), writes=("mx",))
                sch.emit("v", lambda v, W=W, tl=tl: v.tensor_scalar(
                    comb[0][:, 128 * tl:128 * tl + 16], W, scl[:, 3:4], None,
                    OP.is_ge), reads=("wex", "mx"), writes=("ismax",))
                sch.emit("v", lambda v, W=W, tl=tl: v.scalar_tensor_tensor(
                    comb[1][:, 128 * tl:128 * tl + 16],
                    comb[0][:, 128 * tl:128 * tl + 16], -1e9, W, OP.mult, OP.add),
                    reads=("ismax", "wex"), writes=("wmask",))
                sch.emit("v", lambda v, tl=tl: v.reduce_max(
                    scl[:, 5:6], comb[1][:, 128 * tl:128 * tl + 16], AX),
                    reads=("wmask",), writes=("m2",))
                sch.emit("v", lambda v, W=W, tl=tl: v.tensor_scalar(
                    comb[0][:, 128 * tl:128 * tl + 16], W, scl[:, 5:6], None,
                    OP.is_ge), reads=("m2", "wex"), writes=("sel01",))
                sch.emit("v", lambda v, W=W, tl=tl: v.tensor_mul(
                    W, W, comb[0][:, 128 * tl:128 * tl + 16]),
                    reads=("sel01", "wex"), writes=("wsel",), inc=ed["vs"])
                wsel_done_vs = ed["vs"].n
            sch.wait("t", ed["vs"], wsel_done_vs)

            def wtr(tt_):
                r = None
                for tl in range(4):
                    r = tt_.transpose(banks[5][0:16, 128 * tl:128 * (tl + 1)],
                                      wsel[:, 16 * tl:16 * (tl + 1)], c.ident_f[:])
                return r
            sch.emit("t", wtr, inc=ed["ts"])
            sch.wait("s", ed["ts"])
            sch.emit("s", lambda s: s.copy(wselT[0:16, :], banks[5][0:16, 0:512]),
                     writes=("wselT",), inc=ed["st"])
            sch.wait("t", ed["st"])

            for tl in range(4):
                sch.wait("t", ed["vt"], eo_free_vt[0])

                def b2m(tt_, tl=tl):
                    return tt_.matmul(banks[2][:, 128 * tl:128 * (tl + 1)],
                                      wselT[0:16, 128 * tl:128 * (tl + 1)],
                                      b2_sb[0:16, 0:128], start=True, stop=True)
                sch.emit("t", b2m, inc=ed["ts"])
            sch.wait("s", ed["ts"])

            def b2ev(s):
                return s.copy(comb[1][:, 0:512], banks[2][:, 0:512])
            sch.emit("s", b2ev, writes=("comb1",), inc=ed["sv"])
            sch.wait("v", ed["sv"])
            sch.wait("t", ed["sv"])

            for e in range(NE):
                ep = e % 2
                src_comb = comb[1 - (e % 2)] if e > 0 else comb[1]
                dst_comb = comb[e % 2]
                for icx in range(4):
                    hb = icx % 2
                    sch.wait("t", ed["sv"], hb_free_sv[hb])

                    def hmm(tt_, h=h, tc=tc, e=e, icx=icx, hb=hb):
                        return tt_.matmul(
                            banks[hb][:, 0:512],
                            w1e[:, 512 * e + 128 * icx: 512 * e + 128 * (icx + 1)],
                            hT[:, S * h + 512 * tc: S * h + 512 * (tc + 1)],
                            start=True, stop=True)
                    sch.emit("t", hmm, inc=ed["ts"])
                    sch.wait("s", ed["ts"])
                    sch.emit("s", lambda s, e=e, icx=icx, hb=hb: s.activation(
                        hm_sb[:, 512 * icx:512 * (icx + 1)], banks[hb][:, 0:512],
                        AF.Gelu, bias=b1_sb[:, 4 * e + icx:4 * e + icx + 1]),
                        writes=("hm",), inc=ed["sv"])
                    hb_free_sv[hb] = ed["sv"].n
                hm_sv = ed["sv"].n
                sch.wait("t", ed["sv"], hm_sv)
                for tl in range(4):
                    eb = 2 + (tl % 2)
                    sch.wait("t", ed["vt"], eo_free_vt[tl % 2])

                    def eom(tt_, e=e, tl=tl, eb=eb):
                        r = None
                        for icx in range(4):
                            r = tt_.matmul(
                                banks[eb][:, 0:128],
                                hm_sb[:, 512 * icx + 128 * tl: 512 * icx + 128 * (tl + 1)],
                                w2e[:, 512 * e + 128 * icx: 512 * e + 128 * (icx + 1)],
                                start=(icx == 0), stop=(icx == 3))
                        return r
                    sch.emit("t", eom, inc=ed["tv"])
                    sch.wait("v", ed["tv"])
                    if e == NE - 1:
                        dst = moet[:, 1024 * tl + 128 * h: 1024 * tl + 128 * (h + 1)]
                        dtag = "moet"
                    else:
                        dst = dst_comb[:, 128 * tl:128 * (tl + 1)]
                        dtag = f"comb{e % 2}"
                    sch.emit("v", lambda v, e=e, tl=tl, eb=eb, dst=dst,
                             src_comb=src_comb: v.scalar_tensor_tensor(
                        dst, banks[eb][:, 0:128], wsel[:, 16 * tl + e:16 * tl + e + 1],
                        src_comb[:, 128 * tl:128 * (tl + 1)], OP.mult, OP.add),
                        reads=("wsel", f"comb{1 - (e % 2)}" if e > 0 else "comb1"),
                        writes=(dtag,), inc=ed["vt"])
                    eo_free_vt[tl % 2] = ed["vt"].n
        moet_done_vt = ed["vt"].n

        sch.wait("t", ed["vt"], moet_done_vt)
        for tl in range(4):
            t = 4 * tc + tl
            p = t % 2

            def mtr(tt_, tl=tl):
                r = None
                for kk in range(KD):
                    r = tt_.transpose(banks[4 + kk // 4][:, 128 * (kk % 4):128 * (kk % 4 + 1)],
                                      moet[:, 1024 * tl + 128 * kk:
                                           1024 * tl + 128 * (kk + 1)], c.ident_f[:])
                return r
            sch.emit("t", mtr, inc=ed["ts"])
            sch.wait("s", ed["ts"])

            def mtev(s):
                s.copy(moeT[:, 0:512], banks[4][:, 0:512])
                return s.copy(moeT[:, 512:1024], banks[5][:, 0:512])
            sch.emit("s", mtev, writes=("moeT",), inc=ed["st"])
            sch.wait("t", ed["st"])
            if t >= 2:
                sch.wait("y", ed["vt"], x4_vt[t - 2])
            sch.dma("y", lambda y, t=t, p=p: y.dma_start(
                out=x4a[p][:], in_=dt["xs3"][128 * t:128 * (t + 1), :]), ed["dy"])
            xld = ed["dy"].n

            def o2mm(tt_):
                r = None
                for nn in range(2):
                    for k in range(KD):
                        r = tt_.matmul(banks[4 + nn][:, 0:512],
                                       moeT[:, 128 * k:128 * (k + 1)],
                                       wo2s[:, 1024 * k + 512 * nn:
                                            1024 * k + 512 * (nn + 1)],
                                       start=(k == 0), stop=(k == KD - 1))
                return r
            sch.emit("t", o2mm, inc=ed["tv"])
            sch.wait("v", ed["tv"])
            sch.wait("v", ed["dy"], xld)
            sch.wait("v", ed["st"], c.bias_st)
            if t >= 2:
                sch.wait("v", ed["dy"], st4_dy[t - 2])
            for nn in range(2):
                sch.emit("v", lambda v, nn=nn, p=p: v.tensor_add(
                    x4b[p][:, 512 * nn:512 * (nn + 1)], banks[4 + nn][:, 0:512],
                    c.bo2_rep[:, 512 * nn:512 * (nn + 1)]),
                    writes=(f"x4b{p}",))
            sch.emit("v", lambda v, p=p: v.tensor_add(
                x4b[p][:], x4b[p][:], x4a[p][:]),
                reads=(f"x4b{p}",), writes=(f"x4b{p}",), inc=ed["vt"])
            x4_vt[t] = ed["vt"].n
            sch.wait("y", ed["vt"], x4_vt[t])
            sch.dma("y", lambda y, t=t, p=p: y.dma_start(
                out=c.out[128 * t:128 * (t + 1), :], in_=x4b[p][:]), ed["do"])
            st4_dy[t] = ed["do"].n
            sch.wait("t", ed["vt"], x4_vt[t])
        moet_free_ts[tcp] = ed["ts"].n
    arf.pop(); arb.pop()




def host_inputs(ins):
    w = {}
    w["wqkv_qk"] = np.ascontiguousarray(ins["L0_attn_w_qkv"][:, :2048])
    w["wout_r"] = np.ascontiguousarray(ins["L0_attn_w_out"])
    w["bout"] = np.ascontiguousarray(ins["L0_attn_b_out"].reshape(1, D))
    w["w1T"] = np.ascontiguousarray(ins["L0_ffn_w1"].T)
    w["w2T"] = np.ascontiguousarray(ins["L0_ffn_w2"].T)
    w["convw"] = np.ascontiguousarray(
        ins["L0_ffn_conv_w"].reshape(KH, 128, 3).transpose(1, 0, 2).reshape(128, KH * 3))
    w["convb"] = np.ascontiguousarray(ins["L0_ffn_conv_b"].reshape(KH, 128).T)
    w["wqT"] = np.ascontiguousarray(ins["L1_attn_Wq"].T)
    w["wkvgT"] = np.ascontiguousarray(np.concatenate(
        [ins["L1_attn_Wk"].T, ins["L1_attn_Wv"].T, ins["L1_attn_Wg"].T], axis=1))
    w["woT"] = np.ascontiguousarray(ins["L1_attn_Wo"].T)
    w["qgain"] = np.ascontiguousarray(ins["L1_attn_q_gain"].reshape(1, 16))
    w["lam"] = np.ascontiguousarray(ins["L1_attn_lambda_param"].reshape(1, 16))
    w["wm"] = np.ascontiguousarray(ins["L1_ffn_Wm"])
    w["bm"] = np.ascontiguousarray(ins["L1_ffn_bm"].reshape(KD, 128).T)
    w["emb"] = np.ascontiguousarray(ins["L1_ffn_expert_emb"])
    w["w1e"] = np.ascontiguousarray(ins["L1_ffn_W1"])
    w["w2e"] = np.ascontiguousarray(ins["L1_ffn_W2"].reshape(16, 4, 128, 128))
    w["b1e"] = np.ascontiguousarray(
        ins["L1_ffn_b1"].reshape(16, 4, 128).transpose(2, 0, 1).reshape(128, 64))
    w["b2e"] = np.ascontiguousarray(ins["L1_ffn_b2"])
    w["wo2"] = np.ascontiguousarray(ins["L1_ffn_Wo"])
    w["bo2"] = np.ascontiguousarray(ins["L1_ffn_bo"].reshape(1, D))
    import jax
    import jax.numpy as jnp
    with jax.default_device(jax.devices("cpu")[0]):
        angles = jnp.arange(0, 64, 2, dtype=jnp.float32) / 64 * math.pi
        pos = jnp.arange(S, dtype=jnp.float32)
        radius = 1.0 / (1.0 + pos[:, None] * 0.01)
        ang = pos[:, None] * angles[None, :]
        costab = np.asarray(radius * jnp.cos(ang))
        sintab = np.asarray(radius * jnp.sin(ang))
    w["sintab"] = np.ascontiguousarray(
        sintab.reshape(16, 128, 32).transpose(1, 0, 2).reshape(128, 512))
    w["costab"] = np.ascontiguousarray(
        costab.reshape(16, 128, 32).transpose(1, 0, 2).reshape(128, 512))
    return w


def kernel(**ins):
    x = np.ascontiguousarray(ins["L0_attn_x"])
    shared = host_inputs(ins)
    c = build_program()
    in_maps = [dict(shared, x=np.ascontiguousarray(x[cc])) for cc in range(8)]
    res = run_bass_kernel_spmd(c.nc, in_maps, core_ids=list(range(8)))
    return np.stack([r["out"] for r in res.results], axis=0)
